# version 03
# baseline (speedup 1.0000x reference)
import math
import numpy as np
import ml_dtypes
from contextlib import ExitStack

import concourse.bass as bass
import concourse.mybir as mybir
from concourse.bass_utils import run_bass_kernel_spmd

F32 = mybir.dt.float32
BF16 = mybir.dt.bfloat16
I32 = mybir.dt.int32
AF = mybir.ActivationFunctionType
ALU = mybir.AluOpType
AX = mybir.AxisListType

T = 2048
D = 2048
NH = 16
HS = 128
HF = 8192
RK = 16
NT = T // 128
ND = D // 128
NHC = HF // 128
SM_SCALE = 1.0 / math.sqrt(HS)
LN_EPS = 1e-5
INV_D = 1.0 / D

BF = ml_dtypes.bfloat16


class Sem:
    __slots__ = ("h", "v")

    def __init__(self, h):
        self.h = h
        self.v = 0


class DSem:

    def __init__(self, p, name, n):
        self.n = n
        self.s = [p.sem(f"{name}_sl{i}", global_=True) for i in range(n)]
        self.base = [x.v for x in self.s]

    def rebase(self):
        self.base = [x.v for x in self.s]

    def inc(self, k, amt=16):
        ss = self.s[k % self.n]
        ss.v += amt
        return ss.h

    def h(self, k):
        return self.s[k % self.n].h

    def tgt(self, k, amt=16):
        return self.base[k % self.n] + amt * (k // self.n + 1)


class Prog:
    def __init__(self, name="prog"):
        self.nc = bass.Bass("TRN2", target_bir_lowering=False,
                            detect_race_conditions=False)
        self.sem_st = ExitStack()
        self.stacks = [ExitStack()]
        self._n = 0

    def uid(self):
        self._n += 1
        return self._n

    def sem(self, name, global_=False):
        if global_:
            if not hasattr(self, "_gsems"):
                self._gsems = {}
            if name not in self._gsems:
                h = self.stacks[0].enter_context(self.nc.semaphore(name))
                self._gsems[name] = Sem(h)
            return self._gsems[name]
        h = self.stacks[-1].enter_context(self.nc.semaphore(f"{name}{self.uid()}"))
        if not hasattr(self, "phase_sems"):
            self.phase_sems = []
        self.phase_sems.append(h)
        return Sem(h)

    def gdsem(self, name, n):
        if not hasattr(self, "_gds"):
            self._gds = {}
        if name not in self._gds:
            self._gds[name] = DSem(self, name, n)
        d = self._gds[name]
        assert d.n == n
        d.rebase()
        return d

    def sb(self, name, shape, dt):
        return self.stacks[-1].enter_context(
            self.nc.sbuf_tensor(f"{name}{self.uid()}", shape, dt))

    def ps(self, name, shape, dt):
        return self.stacks[-1].enter_context(
            self.nc.psum_tensor(f"{name}{self.uid()}", shape, dt))

    def push(self):
        self.stacks.append(ExitStack())

    def pop(self):
        self.stacks.pop().close()

    def inp(self, name, shape, dt):
        return self._dram(name, shape, dt, "ExternalInput")

    def outp(self, name, shape, dt):
        return self._dram(name, shape, dt, "ExternalOutput")

    def _dram(self, name, shape, dt, kind):
        if not hasattr(self, "_dram_cache"):
            self._dram_cache = {}
        if name in self._dram_cache:
            return self._dram_cache[name]
        if getattr(self, "internal", None) and name in self.internal:
            kind = "Internal"
        ap = self.nc.dram_tensor(name, shape, dt, kind=kind).ap()
        self._dram_cache[name] = ap
        return ap

    def close(self):
        while self.stacks:
            self.stacks.pop().close()
        self.sem_st.close()



def host_consts():
    c = {}
    c["c_ident"] = np.eye(128, dtype=BF)
    c["c_tril"] = np.tril(np.ones((128, 128), dtype=np.float32)).astype(BF)
    triu_neg = np.triu(np.full((128, 128), -1e30, dtype=np.float32), k=1)
    c["c_triuneg"] = triu_neg
    c["c_ones1"] = np.ones((1, 128), dtype=BF)
    c["c_onesrow"] = np.ones((1, 512), dtype=BF)
    t = np.arange(T, dtype=np.float32)
    inv_freq = (1.0 / (10000.0 ** (np.arange(0, HS, 2, dtype=np.float32) / HS))).astype(np.float32)
    ang = t[None, :] * inv_freq[:, None]
    cos = np.cos(ang).astype(np.float32)
    sin = np.sin(ang).astype(np.float32)
    C = np.empty((128, T), dtype=np.float32)
    S = np.empty((128, T), dtype=np.float32)
    C[0::2] = cos
    C[1::2] = cos
    S[0::2] = -sin
    S[1::2] = sin
    c["c_ropeC"] = C
    c["c_ropeS"] = S
    return c


SWAP_MASK = []
for _i in range(16):
    SWAP_MASK += [2 * _i + 1, 2 * _i]



def emit_load_xfm(p, x_dram, x_fm, ident_sb, bufs=None):
    nc = p.nc
    p.push()
    xbf = bufs if bufs is not None else [p.sb("xbf", [128, D], BF16) for _ in range(2)]
    pt = [p.ps("xtp", [128, 128], BF16) for _ in range(2)]
    dld = p.gdsem("xld", 2)
    s_tp, s_ev = p.sem("xtp"), p.sem("xev")
    tp0, ev0 = s_tp.v, s_ev.v
    with nc.Block() as blk:
        @blk.gpsimd
        def _(g):
            for tc in range(NT):
                if tc >= 2:
                    g.wait_ge(s_ev.h, ev0 + (tc - 1) * ND)
                g.dma_start(xbf[tc % 2][:], x_dram[tc * 128:(tc + 1) * 128, :]).then_inc(dld.inc(tc), 16)

        @blk.tensor
        def _(t):
            for tc in range(NT):
                t.wait_ge(dld.h(tc), dld.tgt(tc))
                for dc in range(ND):
                    i = tc * ND + dc
                    if i >= 2:
                        t.wait_ge(s_ev.h, ev0 + i - 1)
                    t.transpose(pt[i % 2][:], xbf[tc % 2][:, dc * 128:(dc + 1) * 128],
                                ident_sb[:]).then_inc(s_tp.h, 1)
        s_tp.v += NT * ND

        @blk.vector
        def _(v):
            for tc in range(NT):
                for dc in range(ND):
                    i = tc * ND + dc
                    v.wait_ge(s_tp.h, tp0 + i + 1)
                    v.tensor_copy(x_fm[:, dc, tc * 128:(tc + 1) * 128], pt[i % 2][:]).then_inc(s_ev.h, 1)
        s_ev.v += NT * ND
    p.pop()


def emit_res_ln(p, blk_engines, delta_ps, xres_sb, xout_sb, tmp):
    raise NotImplementedError



def phase_qkv0(p):
    nc = p.nc
    x = p.inp("x_in", [T, D], F32)
    Wq = p.inp("L0_attn_Wq", [D, D], F32)
    Wk = p.inp("L0_attn_Wk", [D, D], F32)
    Wv = p.inp("L0_attn_Wv", [D, D], F32)
    bq = p.inp("bq_cols", [128, ND], F32)
    bk = p.inp("bk_cols", [128, ND], F32)
    bv = p.inp("bv_row", [1, D], F32)
    ident = p.inp("c_ident", [128, 128], BF16)
    ropeC = p.inp("c_ropeC", [128, T], F32)
    ropeS = p.inp("c_ropeS", [128, T], F32)
    ones1 = p.inp("c_ones1", [1, 128], BF16)
    Qo = p.outp("Q_fm", [D, T], BF16)
    Ko = p.outp("K_fm", [D, T], BF16)
    Vo = p.outp("V_tm", [T, D], BF16)

    ident_sb = p.sb("ident", [128, 128], BF16)
    ropeC_sb = p.sb("ropeC", [128, T], F32)
    ropeS_sb = p.sb("ropeS", [128, T], F32)
    ones1_sb = p.sb("ones1", [1, 128], BF16)
    bq_sb = p.sb("bq", [128, ND], F32)
    bk_sb = p.sb("bk", [128, ND], F32)
    bv_sb = p.sb("bv", [1, D], BF16)
    x_fm = p.sb("xfm", [128, ND, T], BF16)
    w_sb = p.sb("wsb", [128, ND, D], BF16)

    s_c = p.sem("qk0c", global_=True)
    s_cg = p.sem("qk0cg", global_=True)
    c0 = s_c.v
    with nc.Block() as blk:
        @blk.sync
        def _(s):
            s.dma_start(ident_sb[:], ident).then_inc(s_c.h, 16)
            s.dma_start(ropeC_sb[:], ropeC).then_inc(s_c.h, 16)
            s.dma_start(ropeS_sb[:], ropeS).then_inc(s_c.h, 16)
            s.dma_start(ones1_sb[:], ones1).then_inc(s_c.h, 16)
            s.dma_start(bq_sb[:], bq).then_inc(s_c.h, 16)
            s.dma_start(bk_sb[:], bk).then_inc(s_c.h, 16)
            s.wait_ge(s_c.h, c0 + 96)
        s_c.v += 96

        @blk.gpsimd
        def _(g):
            g.dma_start(bv_sb[:], bv).then_inc(s_cg.h, 16)
            g.wait_ge(s_cg.h, s_cg.v + 16)
        s_cg.v += 16

    emit_load_xfm(p, x, x_fm, ident_sb)

    proj_ps = p.ps("pj", [128, T], F32)
    for W, bias_cols, OUT, roped in ((Wq, bq_sb, Qo, True), (Wk, bk_sb, Ko, True)):
        _emit_proj_fm(p, proj_ps, W, w_sb, x_fm, bias_cols, OUT,
                      rope=(ropeC_sb, ropeS_sb) if roped else None)

    _emit_proj_tm_v(p, proj_ps, Wv, w_sb, x_fm, bv_sb, ones1_sb, Vo)
    return ["x_in", "L0_attn_Wq", "L0_attn_Wk", "L0_attn_Wv", "bq_cols",
            "bk_cols", "bv_row", "c_ident", "c_ropeC", "c_ropeS", "c_ones1"]


def _emit_proj_fm(p, ps, W, w_sb, x_fm, bias_cols, OUT, rope=None):
    nc = p.nc
    p.push()
    raw = [p.sb("qraw", [128, T], BF16) for _ in range(2)]
    sw = p.sb("qsw", [128, T], BF16)
    t1 = p.sb("qt1", [128, T], BF16)
    out_sb = [p.sb("qout", [128, T], BF16) for _ in range(2)]
    s_w = p.sem("pfw", global_=True)
    s_mm, s_ev, s_rp = (p.sem(x) for x in ("mm", "ev", "rp"))
    dst = p.gdsem("pfst", 2)
    w0, mm0, ev0, rp0 = s_w.v, s_mm.v, s_ev.v, s_rp.v
    with nc.Block() as blk:
        @blk.gpsimd
        def _(g):
            for dc in range(ND):
                g.dma_start(w_sb[:, dc, :], W[dc * 128:(dc + 1) * 128, :]).then_inc(s_w.h, 16)
            for ec in range(ND):
                g.wait_ge(s_rp.h, rp0 + ec + 1)
                g.dma_start(OUT[ec * 128:(ec + 1) * 128, :], out_sb[ec % 2][:]).then_inc(dst.inc(ec), 16)
            g.wait_ge(dst.h(ND - 2), dst.tgt(ND - 2))
            g.wait_ge(dst.h(ND - 1), dst.tgt(ND - 1))
        s_w.v += 16 * ND

        @blk.tensor
        def _(t):
            t.wait_ge(s_w.h, w0 + 16 * ND)
            for ec in range(ND):
                if ec > 0:
                    t.wait_ge(s_ev.h, ev0 + ec)
                for dc in range(ND):
                    for n in range(4):
                        ins = t.matmul(ps[:, n * 512:(n + 1) * 512],
                                       w_sb[:, dc, ec * 128:(ec + 1) * 128],
                                       x_fm[:, dc, n * 512:(n + 1) * 512],
                                       start=(dc == 0), stop=(dc == ND - 1))
                        if dc == ND - 1 and n == 3:
                            ins.then_inc(s_mm.h, 1)
        s_mm.v += ND

        @blk.scalar
        def _(sc):
            for ec in range(ND):
                sc.wait_ge(s_mm.h, mm0 + ec + 1)
                if ec >= 2:
                    sc.wait_ge(s_rp.h, rp0 + ec - 1)
                sc.activation(raw[ec % 2][:], ps[:], AF.Identity,
                              bias=bias_cols[:, ec:ec + 1], scale=1.0).then_inc(s_ev.h, 1)
        s_ev.v += ND

        @blk.vector
        def _(v):
            C, S = rope
            for ec in range(ND):
                v.wait_ge(s_ev.h, ev0 + ec + 1)
                if ec >= 2:
                    v.wait_ge(dst.h(ec), dst.tgt(ec - 2))
                r = raw[ec % 2]
                v.stream_shuffle(sw[:], r[:], SWAP_MASK)
                v.tensor_mul(t1[:], r[:], C[:])
                v.tensor_mul(sw[:], sw[:], S[:])
                v.tensor_add(out_sb[ec % 2][:], t1[:], sw[:]).then_inc(s_rp.h, 1)
        s_rp.v += ND
    p.pop()


def _emit_proj_tm_v(p, ps, W, w_sb, x_fm, bv_sb, ones1_sb, OUT):
    nc = p.nc
    p.push()
    out_sb = [p.sb("vout", [128, D], BF16) for _ in range(2)]
    s_w = p.sem("tvw", global_=True)
    s_mm, s_ev = (p.sem(x) for x in ("mm", "ev"))
    dst = p.gdsem("tvst", 2)
    w0, mm0, ev0 = s_w.v, s_mm.v, s_ev.v
    with nc.Block() as blk:
        @blk.gpsimd
        def _(g):
            for dc in range(ND):
                g.dma_start(w_sb[:, dc, :], W[dc * 128:(dc + 1) * 128, :]).then_inc(s_w.h, 16)
            for tc in range(NT):
                g.wait_ge(s_ev.h, ev0 + tc + 1)
                g.dma_start(OUT[tc * 128:(tc + 1) * 128, :], out_sb[tc % 2][:]).then_inc(dst.inc(tc), 16)
            g.wait_ge(dst.h(NT - 2), dst.tgt(NT - 2))
            g.wait_ge(dst.h(NT - 1), dst.tgt(NT - 1))
        s_w.v += 16 * ND

        @blk.tensor
        def _(t):
            t.wait_ge(s_w.h, w0 + 16 * ND)
            for tc in range(NT):
                if tc > 0:
                    t.wait_ge(s_ev.h, ev0 + tc)
                for dc in range(ND):
                    for n in range(4):
                        t.matmul(ps[:, n * 512:(n + 1) * 512],
                                 x_fm[:, dc, tc * 128:(tc + 1) * 128],
                                 w_sb[:, dc, n * 512:(n + 1) * 512],
                                 start=(dc == 0), stop=False)
                for n in range(4):
                    ins = t.matmul(ps[:, n * 512:(n + 1) * 512],
                                   ones1_sb[:],
                                   bv_sb[:, n * 512:(n + 1) * 512],
                                   start=False, stop=True)
                    if n == 3:
                        ins.then_inc(s_mm.h, 1)
        s_mm.v += NT

        @blk.vector
        def _(v):
            for tc in range(NT):
                v.wait_ge(s_mm.h, mm0 + tc + 1)
                if tc >= 2:
                    v.wait_ge(dst.h(tc), dst.tgt(tc - 2))
                v.tensor_copy(out_sb[tc % 2][:], ps[:]).then_inc(s_ev.h, 1)
        s_ev.v += NT
    p.pop()



def phase_attn(p, causal, qn="Q_fm", kn="K_fm", vn="V_tm", on="O_fm"):
    nc = p.nc
    Q = p.inp(qn, [D, T], BF16)
    K = p.inp(kn, [D, T], BF16)
    V = p.inp(vn, [T, D], BF16)
    O = p.outp(on, [D, T], BF16)
    ident = p.inp("c_ident", [128, 128], BF16)
    if causal:
        mask_in = p.inp("c_triuneg", [128, 128], F32)
    else:
        mask_in = p.inp("c_tril", [128, 128], BF16)

    p.push()
    V_sb = p.sb("v", [128, NT, D], BF16)
    K_sb = [p.sb("k", [128, T], BF16) for _ in range(2)]
    Q_sb = [p.sb("q", [128, T], BF16) for _ in range(2)]
    A_sb = p.sb("a", [128, T], BF16)
    AT_sb = p.sb("at", [128, T], BF16)
    O_sb = [p.sb("o", [128, T], BF16) for _ in range(2)]
    ident_sb = p.sb("id", [128, 128], BF16)
    mask_sb = p.sb("mk", [128, 128], F32 if causal else BF16)
    n_col = p.sb("ncol", [128, 1], F32)
    r_col = p.sb("rcol", [128, 1], F32)
    s_ps = p.ps("sps", [128, T], F32)
    at_ps = [p.ps("atps", [128, 128], BF16) for _ in range(2)]
    o_ps = p.ps("ops", [128, 128], F32)

    s_v = p.sem("attv", global_=True)
    s_S, s_msk, s_exp, s_anorm = (p.sem(x) for x in ("S", "msk", "exp", "an"))
    s_tp, s_tpev, s_av, s_oev = (p.sem(x) for x in ("tp", "tpev", "av", "oev"))
    s_rr = p.sem("rr")
    dkq = p.gdsem("attkq", 2)
    dost = p.gdsem("attost", 2)
    v0, S0, msk0, exp0, an0 = s_v.v, s_S.v, s_msk.v, s_exp.v, s_anorm.v
    rr0 = s_rr.v
    tp0, tpev0, av0, oev0 = s_tp.v, s_tpev.v, s_av.v, s_oev.v

    def cum_tp(g):
        h, qt = divmod(g, NT)
        return h * (NT * (NT + 1) // 2) + qt * (qt + 1) // 2

    with nc.Block() as blk:
        @blk.sync
        def _(s):
            s.dma_start(ident_sb[:], ident).then_inc(s_v.h, 16)
            s.dma_start(mask_sb[:], mask_in).then_inc(s_v.h, 16)
            for tc in range(NT):
                s.dma_start(V_sb[:, tc, :], V[tc * 128:(tc + 1) * 128, :]).then_inc(s_v.h, 16)
            for h in range(NH):
                if h >= 2:
                    s.wait_ge(s_S.h, S0 + NT * (h - 1))
                s.dma_start(K_sb[h % 2][:], K[h * 128:(h + 1) * 128, :]).then_inc(dkq.inc(h), 16)
                s.dma_start(Q_sb[h % 2][:], Q[h * 128:(h + 1) * 128, :]).then_inc(dkq.inc(h), 16)
        s_v.v += 16 * (NT + 2)

        @blk.tensor
        def _(t):
            t.wait_ge(s_v.h, v0 + 16 * (NT + 2))
            for h in range(NH):
                t.wait_ge(dkq.h(h), dkq.tgt(h, 32))
                for qt in range(NT):
                    g = h * NT + qt
                    W = (qt + 1) * 128 if causal else T
                    if g > 0:
                        t.wait_ge(s_exp.h, exp0 + g)
                    nfull = (W + 511) // 512
                    for n in range(nfull):
                        lo, hi = n * 512, min(W, (n + 1) * 512)
                        ins = t.matmul(s_ps[:, lo:hi],
                                       Q_sb[h % 2][:, qt * 128:(qt + 1) * 128],
                                       K_sb[h % 2][:, lo:hi],
                                       start=True, stop=True)
                        if hi == W:
                            ins.then_inc(s_S.h, 1)
                    t.wait_ge(s_anorm.h, an0 + g + 1)
                    base = tp0 + cum_tp(g)
                    for kc in range(qt + 1):
                        if kc >= 2:
                            t.wait_ge(s_tpev.h, tpev0 + cum_tp(g) + kc - 1)
                        t.transpose(at_ps[kc % 2][:], A_sb[:, kc * 128:(kc + 1) * 128],
                                    ident_sb[:]).then_inc(s_tp.h, 1)
                    t.wait_ge(s_tpev.h, tpev0 + cum_tp(g) + qt + 1)
                    t.wait_ge(s_oev.h, oev0 + g)
                    for kc in range(qt + 1):
                        ins = t.matmul(o_ps[:], V_sb[:, kc, h * 128:(h + 1) * 128],
                                       AT_sb[:, kc * 128:(kc + 1) * 128],
                                       start=(kc == 0), stop=(kc == qt))
                        if kc == qt:
                            ins.then_inc(s_av.h, 1)
        s_S.v += NH * NT
        s_tp.v += NH * (NT * (NT + 1) // 2)
        s_av.v += NH * NT

        @blk.scalar
        def _(sc):
            for h in range(NH):
                for qt in range(NT):
                    g = h * NT + qt
                    W = (qt + 1) * 128 if causal else T
                    if causal:
                        sc.wait_ge(s_msk.h, msk0 + g + 1)
                    else:
                        sc.wait_ge(s_S.h, S0 + g + 1)
                    if g > 0:
                        sc.wait_ge(s_anorm.h, an0 + g)
                    sc.activation(A_sb[:, 0:W], s_ps[:, 0:W], AF.Exp,
                                  scale=SM_SCALE, accum_out=n_col[:]).then_inc(s_exp.h, 1)
        s_exp.v += NH * NT

        @blk.vector
        def _(v):
            for h in range(NH):
                for qt in range(NT):
                    g = h * NT + qt
                    W = (qt + 1) * 128 if causal else T
                    if causal:
                        v.wait_ge(s_S.h, S0 + g + 1)
                        v.tensor_add(s_ps[:, qt * 128:(qt + 1) * 128],
                                     s_ps[:, qt * 128:(qt + 1) * 128],
                                     mask_sb[:]).then_inc(s_msk.h, 1)
                    v.wait_ge(s_exp.h, exp0 + g + 1)
                    v.reciprocal(r_col[:], n_col[:]).then_inc(s_rr.h, 1)
                    v.wait_ge(s_rr.h, rr0 + g + 1)
                    ins = v.tensor_scalar_mul(A_sb[:, 0:W], A_sb[:, 0:W], r_col[:])
                    if not causal:
                        ins = v.tensor_mul(A_sb[:, qt * 128:(qt + 1) * 128],
                                           A_sb[:, qt * 128:(qt + 1) * 128],
                                           mask_sb[:])
                    ins.then_inc(s_anorm.h, 1)
                    for kc in range(qt + 1):
                        v.wait_ge(s_tp.h, tp0 + cum_tp(g) + kc + 1)
                        v.tensor_copy(AT_sb[:, kc * 128:(kc + 1) * 128],
                                      at_ps[kc % 2][:]).then_inc(s_tpev.h, 1)
                    v.wait_ge(s_av.h, av0 + g + 1)
                    if qt == 0 and h >= 2:
                        v.wait_ge(dost.h(h), dost.tgt(h - 2))
                    v.tensor_copy(O_sb[h % 2][:, qt * 128:(qt + 1) * 128],
                                  o_ps[:]).then_inc(s_oev.h, 1)
        if causal:
            s_msk.v += NH * NT
        s_rr.v += NH * NT
        s_anorm.v += NH * NT
        s_tpev.v += NH * (NT * (NT + 1) // 2)
        s_oev.v += NH * NT

        @blk.gpsimd
        def _(g):
            for h in range(NH):
                g.wait_ge(s_oev.h, oev0 + NT * (h + 1))
                g.dma_start(O[h * 128:(h + 1) * 128, :], O_sb[h % 2][:]).then_inc(dost.inc(h), 16)
            g.wait_ge(dost.h(NH - 2), dost.tgt(NH - 2))
            g.wait_ge(dost.h(NH - 1), dost.tgt(NH - 1))
    p.pop()



def phase_proj_ln(p, layer, on="O_fm", wname="L0_attn_Wo", bname=None,
                  xres_name="x_in", xout_name="x1", w_transposed=False):
    nc = p.nc
    O = p.inp(on, [D, T], BF16)
    Wo = p.inp(wname, [D, D], F32)
    bo = p.inp(bname, [1, D], F32) if bname else None
    xres = p.inp(xres_name, [T, D], F32)
    xout = p.outp(xout_name, [T, D], F32)
    ident = p.inp("c_ident", [128, 128], BF16)
    ones1 = p.inp("c_ones1", [1, 128], BF16)

    p.push()
    O_sb = p.sb("osb", [128, ND, T], BF16)
    W_sb = p.sb("wsb", [128, ND, D], BF16)
    ident_sb = p.sb("id", [128, 128], BF16)
    ones1_sb = p.sb("on", [1, 128], BF16)
    bo_sb = p.sb("bo", [1, D], BF16) if bname else None
    xr = [p.sb("xr", [128, D], F32) for _ in range(2)]
    xo = [p.sb("xo", [128, D], F32) for _ in range(2)]
    ssum = p.sb("ssum", [128, 1], F32)
    ssq = p.sb("ssq", [128, 1], F32)
    mean = p.sb("mean", [128, 1], F32)
    var = p.sb("var", [128, 1], F32)
    sd = p.sb("sd", [128, 1], F32)
    rstd = p.sb("rstd", [128, 1], F32)
    ps = p.ps("pj", [128, D], F32)

    s_c = p.sem("plc", global_=True)
    s_w = p.sem("plw", global_=True)
    c0, w0 = s_c.v, s_w.v

    with nc.Block() as blk:
        @blk.sync
        def _(s):
            s.dma_start(ident_sb[:], ident).then_inc(s_c.h, 16)
            s.dma_start(ones1_sb[:], ones1).then_inc(s_c.h, 16)
            for dc in range(ND):
                s.dma_start(O_sb[:, dc, :], O[dc * 128:(dc + 1) * 128, :]).then_inc(s_c.h, 16)
            s.wait_ge(s_c.h, c0 + 16 * (ND + 2))
        s_c.v += 16 * (ND + 2)

        @blk.gpsimd
        def _(g):
            if bname:
                g.dma_start(bo_sb[:], bo).then_inc(s_w.h, 16)
                g.wait_ge(s_w.h, w0 + 16)
        if bname:
            s_w.v += 16

    if not w_transposed:
        w0b = s_w.v
        with nc.Block() as blk:
            @blk.gpsimd
            def _(g):
                for dc in range(ND):
                    g.dma_start(W_sb[:, dc, :], Wo[dc * 128:(dc + 1) * 128, :]).then_inc(s_w.h, 16)
                g.wait_ge(s_w.h, w0b + 16 * ND)
            s_w.v += 16 * ND
    else:
        wtmp = [p.sb("wt", [128, D], BF16) for _ in range(2)]
        wt_ps = [p.ps("wtp", [128, 128], BF16) for _ in range(2)]
        s_tt, s_te = p.sem("tt"), p.sem("te")
        dwl = p.gdsem("plwl", 2)
        tt0, te0 = s_tt.v, s_te.v
        with nc.Block() as blk:
            @blk.gpsimd
            def _(g):
                for ec in range(ND):
                    if ec >= 2:
                        g.wait_ge(s_te.h, te0 + (ec - 1) * ND)
                    g.dma_start(wtmp[ec % 2][:], Wo[ec * 128:(ec + 1) * 128, :]).then_inc(dwl.inc(ec), 16)

            @blk.tensor
            def _(t):
                for ec in range(ND):
                    t.wait_ge(dwl.h(ec), dwl.tgt(ec))
                    for dc in range(ND):
                        i = ec * ND + dc
                        if i >= 2:
                            t.wait_ge(s_te.h, te0 + i - 1)
                        t.transpose(wt_ps[i % 2][:], wtmp[ec % 2][:, dc * 128:(dc + 1) * 128],
                                    ident_sb[:]).then_inc(s_tt.h, 1)
            s_tt.v += ND * ND

            @blk.vector
            def _(v):
                for ec in range(ND):
                    for dc in range(ND):
                        i = ec * ND + dc
                        v.wait_ge(s_tt.h, tt0 + i + 1)
                        v.tensor_copy(W_sb[:, dc, ec * 128:(ec + 1) * 128],
                                      wt_ps[i % 2][:]).then_inc(s_te.h, 1)
            s_te.v += ND * ND

    _emit_proj_res_ln(p, O_sb, W_sb,
                      (ones1_sb, bo_sb) if bname else None, xres, xout, ps,
                      xr, xo, ssum, ssq, mean, var, sd, rstd)
    p.pop()
    return


def _emit_proj_res_ln(p, lhsT_sb, W_sb, bias_pair, xres, xout, ps,
                      xr, xo, ssum, ssq, mean, var, sd, rstd):
    nc = p.nc
    s_mm, s_r1, s_sq, s_nm = (p.sem(x) for x in ("mm", "r1", "sq", "nm"))
    s_v2, s_sd, s_rc = p.sem("v2"), p.sem("sd"), p.sem("rc")
    dx = p.gdsem("lnx", 2)
    dst = p.gdsem("lnst", 2)
    mm0, r10, sq0, nm0 = (s.v for s in (s_mm, s_r1, s_sq, s_nm))
    v20, sd0, rc0 = s_v2.v, s_sd.v, s_rc.v
    with nc.Block() as blk:
        @blk.sync
        def _(s):
            for qc in range(NT):
                if qc >= 2:
                    s.wait_ge(s_nm.h, nm0 + qc - 1)
                s.dma_start(xr[qc % 2][:], xres[qc * 128:(qc + 1) * 128, :]).then_inc(dx.inc(qc), 16)

        @blk.tensor
        def _(t):
            for qc in range(NT):
                if qc > 0:
                    t.wait_ge(s_sq.h, sq0 + qc)
                last = None
                for dc in range(ND):
                    for n in range(4):
                        last = t.matmul(ps[:, n * 512:(n + 1) * 512],
                                        lhsT_sb[:, dc, qc * 128:(qc + 1) * 128],
                                        W_sb[:, dc, n * 512:(n + 1) * 512],
                                        start=(dc == 0),
                                        stop=(False if bias_pair else dc == ND - 1))
                if bias_pair:
                    ones1_sb, bo_sb = bias_pair
                    for n in range(4):
                        last = t.matmul(ps[:, n * 512:(n + 1) * 512], ones1_sb[:],
                                        bo_sb[:, n * 512:(n + 1) * 512],
                                        start=False, stop=True)
                last.then_inc(s_mm.h, 1)
        s_mm.v += NT

        @blk.vector
        def _(v):
            for qc in range(NT):
                v.wait_ge(s_mm.h, mm0 + qc + 1)
                v.wait_ge(dx.h(qc), dx.tgt(qc))
                v.scalar_tensor_tensor(xr[qc % 2][:], ps[:], 1.0, xr[qc % 2][:],
                                       ALU.mult, ALU.add,
                                       accum_out=ssum[:]).then_inc(s_r1.h, 1)
                v.wait_ge(s_sq.h, sq0 + qc + 1)
                v.tensor_scalar_mul(mean[:], ssum[:], INV_D)
                v.tensor_scalar_mul(var[:], ssq[:], INV_D)
                v.tensor_mul(rstd[:], mean[:], mean[:])
                v.tensor_sub(var[:], var[:], rstd[:])
                v.tensor_scalar_add(var[:], var[:], LN_EPS).then_inc(s_v2.h, 1)
                v.wait_ge(s_sd.h, sd0 + qc + 1)
                if qc >= 2:
                    v.wait_ge(dst.h(qc), dst.tgt(qc - 2))
                v.reciprocal(rstd[:], sd[:]).then_inc(s_rc.h, 1)
                v.wait_ge(s_rc.h, rc0 + qc + 1)
                v.tensor_scalar(xo[qc % 2][:], xr[qc % 2][:], mean[:], rstd[:],
                                ALU.subtract, ALU.mult).then_inc(s_nm.h, 1)
        s_r1.v += NT
        s_v2.v += NT
        s_rc.v += NT
        s_nm.v += NT

        @blk.scalar
        def _(sc):
            for qc in range(NT):
                sc.wait_ge(s_r1.h, r10 + qc + 1)
                sc.activation(ps[:], xr[qc % 2][:], AF.Square,
                              accum_out=ssq[:]).then_inc(s_sq.h, 1)
                sc.wait_ge(s_v2.h, v20 + qc + 1)
                sc.activation(sd[:], var[:], AF.Sqrt).then_inc(s_sd.h, 1)
        s_sq.v += NT
        s_sd.v += NT

        @blk.gpsimd
        def _(g):
            for qc in range(NT):
                g.wait_ge(s_nm.h, nm0 + qc + 1)
                g.dma_start(xout[qc * 128:(qc + 1) * 128, :], xo[qc % 2][:]).then_inc(dst.inc(qc), 16)
            g.wait_ge(dst.h(NT - 2), dst.tgt(NT - 2))
            g.wait_ge(dst.h(NT - 1), dst.tgt(NT - 1))



def phase_ffn_prep(p):
    nc = p.nc
    wup_q = p.inp("L0_ffn_w_up_q", [HF, D], I32)
    wup_s = p.inp("L0_ffn_w_up_scale", [HF, 32], F32)
    wdn_q = p.inp("L0_ffn_w_down_q", [D, HF], I32)
    wdn_s = p.inp("L0_ffn_w_down_scale", [D, HF // 64], F32)
    ident = p.inp("c_ident", [128, 128], BF16)
    WupT = p.outp("WupT", [NHC, 128, D], BF16)
    WdnT = p.outp("WdnT", [NHC, 128, D], BF16)

    p.push()
    ident_sb = p.sb("id", [128, 128], BF16)
    qt_sb = [p.sb("qt", [128, 128], I32) for _ in range(2)]
    dq_sb = [p.sb("dq", [128, 128], BF16) for _ in range(2)]
    wus_sb = [p.sb("wus", [128, 32], F32) for _ in range(2)]
    wds_sb = p.sb("wds", [128, ND, HF // 64], F32)
    upstage = [p.sb("ups", [128, D], BF16) for _ in range(2)]
    dnstage = [p.sb("dns", [128, D], BF16) for _ in range(2)]
    tp_ps = [p.ps("tp", [128, 128], BF16) for _ in range(2)]

    s_c = p.sem("d1c", global_=True)
    s_dq, s_tp, s_ev = (p.sem(x) for x in ("dq", "tp", "ev"))
    dld = p.gdsem("d1ld", 2)
    dsc = p.gdsem("d1sc", 2)
    dstg = p.gdsem("d1stg", 2)
    c0, dq0, tp0, ev0 = (s.v for s in (s_c, s_dq, s_tp, s_ev))

    with nc.Block() as blk:
        @blk.sync
        def _(s):
            s.dma_start(ident_sb[:], ident).then_inc(s_c.h, 16)
            for dc in range(ND):
                s.dma_start(wds_sb[:, dc, :],
                            wdn_s[dc * 128:(dc + 1) * 128, :]).then_inc(s_c.h, 16)
            s.wait_ge(s_c.h, c0 + 16 * (ND + 1))
        s_c.v += 16 * (ND + 1)

    with nc.Block() as blk:
        @blk.sync
        def _(s):
            for hc in range(NHC):
                for j in range(32):
                    gt = hc * 32 + j
                    if gt >= 2:
                        s.wait_ge(s_dq.h, dq0 + gt - 1)
                    if j < ND:
                        src = wup_q[hc * 128:(hc + 1) * 128, j * 128:(j + 1) * 128]
                    else:
                        dc = j - ND
                        src = wdn_q[dc * 128:(dc + 1) * 128, hc * 128:(hc + 1) * 128]
                    s.dma_start(qt_sb[gt % 2][:], src).then_inc(dld.inc(gt), 16)

        @blk.gpsimd
        def _(g):
            def stores(hcs):
                g.wait_ge(s_ev.h, ev0 + hcs * 32 + ND)
                g.dma_start(WupT[hcs], upstage[hcs % 2][:]).then_inc(dstg.inc(hcs), 16)
                g.wait_ge(s_ev.h, ev0 + hcs * 32 + 32)
                g.dma_start(WdnT[hcs], dnstage[hcs % 2][:]).then_inc(dstg.inc(hcs), 16)
            for hc in range(NHC):
                if hc >= 2:
                    g.wait_ge(s_dq.h, dq0 + (hc - 1) * 32)
                g.dma_start(wus_sb[hc % 2][:],
                            wup_s[hc * 128:(hc + 1) * 128, :]).then_inc(dsc.inc(hc), 16)
                if hc >= 1:
                    stores(hc - 1)
            stores(NHC - 1)
            g.wait_ge(dstg.h(NHC - 2), dstg.tgt(NHC - 2, 32))
            g.wait_ge(dstg.h(NHC - 1), dstg.tgt(NHC - 1, 32))

        @blk.vector
        def _(v):
            def emit_evac(gte):
                hce, je = divmod(gte, 32)
                v.wait_ge(s_tp.h, tp0 + gte + 1)
                if je < ND:
                    dst = upstage[hce % 2][:, je * 128:(je + 1) * 128]
                else:
                    dst = dnstage[hce % 2][:, (je - ND) * 128:(je - ND + 1) * 128]
                v.tensor_copy(dst, tp_ps[gte % 2][:]).then_inc(s_ev.h, 1)
            for hc in range(NHC):
                for j in range(32):
                    gt = hc * 32 + j
                    v.wait_ge(dld.h(gt), dld.tgt(gt))
                    if gt >= 2:
                        v.wait_ge(s_tp.h, tp0 + gt - 1)
                    if j < ND:
                        v.wait_ge(dsc.h(hc), dsc.tgt(hc))
                        sca = wus_sb[hc % 2][:, 2 * j:2 * j + 1]
                        scb = wus_sb[hc % 2][:, 2 * j + 1:2 * j + 2]
                    else:
                        dc = j - ND
                        sca = wds_sb[:, dc, 2 * hc:2 * hc + 1]
                        scb = wds_sb[:, dc, 2 * hc + 1:2 * hc + 2]
                    q = qt_sb[gt % 2]
                    d = dq_sb[gt % 2]
                    v.tensor_scalar(d[:, 0:64], q[:, 0:64], -7.5, sca,
                                    ALU.add, ALU.mult)
                    v.tensor_scalar(d[:, 64:128], q[:, 64:128], -7.5, scb,
                                    ALU.add, ALU.mult).then_inc(s_dq.h, 1)
                    if gt >= 1:
                        emit_evac(gt - 1)
            emit_evac(NHC * 32 - 1)
        s_dq.v += NHC * 32
        s_ev.v += NHC * 32

        @blk.tensor
        def _(t):
            for hc in range(NHC):
                if hc >= 2:
                    t.wait_ge(dstg.h(hc), dstg.tgt(hc - 2, 32))
                for j in range(32):
                    gt = hc * 32 + j
                    t.wait_ge(s_dq.h, dq0 + gt + 1)
                    if gt >= 2:
                        t.wait_ge(s_ev.h, ev0 + gt - 1)
                    t.transpose(tp_ps[gt % 2][:], dq_sb[gt % 2][:],
                                ident_sb[:]).then_inc(s_tp.h, 1)
        s_tp.v += NHC * 32
    p.pop()
    return





def phase_ffn_main(p):
    nc = p.nc
    x1 = p.inp("x1", [T, D], F32)
    WupT = p.inp("WupT", [NHC, 128, D], BF16)
    WdnT = p.inp("WdnT", [NHC, 128, D], BF16)
    bup = p.inp("bup_cols", [128, NHC], F32)
    Aup = p.inp("L0_ffn_w_up_lora_a", [D, RK], F32)
    Bup = p.inp("L0_ffn_w_up_lora_b", [RK, HF], F32)
    Adn = p.inp("L0_ffn_w_down_lora_a", [HF, RK], F32)
    Bdn = p.inp("bdnl_ext", [RK + 1, D], F32)
    onesrow = p.inp("c_onesrow", [1, 512], BF16)
    ident = p.inp("c_ident", [128, 128], BF16)
    ones1 = p.inp("c_ones1", [1, 128], BF16)
    x2out = p.outp("x2", [T, D], F32)

    TB = 512
    NTB = T // TB
    NTC = TB // 128

    p.push()
    ident_sb = p.sb("id", [128, 128], BF16)
    ones1_sb = p.sb("on", [1, 128], BF16)
    x_fm = p.sb("xfm", [128, ND, T], BF16)
    x2_fm = p.sb("x2", [128, NHC, TB], BF16)
    wup_sb = [p.sb("wu", [128, D], BF16) for _ in range(2)]
    wdn_sb = [p.sb("wd", [128, 512], BF16) for _ in range(2)]
    aup_sb = p.sb("aup", [128, ND * RK], BF16)
    bupl_sb = p.sb("bupl", [RK, HF], BF16)
    adn_sb = p.sb("adn", [128, NHC * RK], BF16)
    bdnl_sb = p.sb("bdnl", [RK + 1, D], BF16)
    bup_sb = p.sb("bup", [128, NHC], F32)
    xaup_sb = p.sb("xaup", [RK, T], BF16)
    xadn_sb = p.sb("xadn", [RK + 1, TB], BF16)
    xr_sb = [p.sb("xr", [128, D], F32) for _ in range(4)]
    ssum4 = p.sb("ssum4", [128, 4], F32)
    ssq = p.sb("ssq", [128, 1], F32)
    ssum = p.sb("ssum", [128, 1], F32)
    mean = p.sb("mean", [128, 1], F32)
    var = p.sb("var", [128, 1], F32)
    sd = p.sb("sd", [128, 1], F32)
    rstd = p.sb("rstd", [128, 1], F32)

    s_c = p.sem("d2c", global_=True)
    s_cg = p.sem("d2cg", global_=True)
    with nc.Block() as blk:
        @blk.sync
        def _(s):
            s.dma_start(ident_sb[:], ident).then_inc(s_c.h, 16)
            s.dma_start(ones1_sb[:], ones1).then_inc(s_c.h, 16)
            s.dma_start(bup_sb[:], bup).then_inc(s_c.h, 16)
            s.wait_ge(s_c.h, s_c.v + 48)
        s_c.v += 48

        @blk.gpsimd
        def _(g):
            g.dma_start(xadn_sb[RK:RK + 1, :], onesrow[0:1, 0:TB]).then_inc(s_cg.h, 16)
            g.dma_start(bupl_sb[:], Bup).then_inc(s_cg.h, 16)
            g.dma_start(bdnl_sb[:], Bdn).then_inc(s_cg.h, 16)
            for dc in range(ND):
                g.dma_start(aup_sb[:, dc * RK:(dc + 1) * RK],
                            Aup[dc * 128:(dc + 1) * 128, :]).then_inc(s_cg.h, 16)
            for hc in range(NHC):
                g.dma_start(adn_sb[:, hc * RK:(hc + 1) * RK],
                            Adn[hc * 128:(hc + 1) * 128, :]).then_inc(s_cg.h, 16)
            g.wait_ge(s_cg.h, s_cg.v + 16 * (3 + ND + NHC))
        s_cg.v += 16 * (3 + ND + NHC)

    emit_load_xfm(p, x1, x_fm, ident_sb, bufs=wup_sb)

    p.push()
    xaup_ps = p.ps("xaupp", [RK, T], F32)
    s_u, s_ue = p.sem("u"), p.sem("ue")
    with nc.Block() as blk:
        @blk.tensor
        def _(t):
            last = None
            for dc in range(ND):
                for n in range(4):
                    last = t.matmul(xaup_ps[:, n * 512:(n + 1) * 512],
                                    aup_sb[:, dc * RK:(dc + 1) * RK],
                                    x_fm[:, dc, n * 512:(n + 1) * 512],
                                    start=(dc == 0), stop=(dc == ND - 1))
            last.then_inc(s_u.h, 1)
        s_u.v += 1

        @blk.vector
        def _(v):
            v.wait_ge(s_u.h, s_u.v)
            v.tensor_copy(xaup_sb[:], xaup_ps[:]).then_inc(s_ue.h, 1)
        s_ue.v += 1
    p.pop()

    up_ps = [p.ps("up", [128, TB], F32) for _ in range(2)]
    xa_ps = p.ps("xa", [RK, TB], F32)
    dn_ps = [p.ps("dn", [128, 512], F32) for _ in range(4)]

    s_up, s_rel, s_xa, s_xae, s_dn, s_dnw = (p.sem(x) for x in
        ("up", "rel", "xa", "xae", "dn", "dnw"))
    dwu = p.gdsem("d2wu", 2)
    dwd = p.gdsem("d2wd", 2)
    dxl = p.gdsem("d2xl", 4)
    dxs = p.gdsem("d2xs", 4)
    s_r1, s_sq2, s_v2, s_sd2, s_rc, s_nm = (p.sem(x) for x in
        ("r1", "sq2", "v2", "sd2", "rc", "nm"))

    for tb in range(NTB):
        up0, rel0, xa0 = s_up.v, s_rel.v, s_xa.v
        with nc.Block() as blk:
            @blk.sync
            def _(s):
                for hc in range(NHC):
                    k = tb * NHC + hc
                    if k >= 2:
                        s.wait_ge(s_up.h, k - 1)
                    s.dma_start(wup_sb[hc % 2][:], WupT[hc]).then_inc(dwu.inc(k), 16)

            @blk.tensor
            def _(t):
                t.wait_ge(s_ue.h, s_ue.v)
                for hc in range(NHC):
                    k = tb * NHC + hc
                    t.wait_ge(dwu.h(k), dwu.tgt(k))
                    if hc >= 2:
                        t.wait_ge(s_rel.h, rel0 + hc - 1)
                    last = None
                    for dc in range(ND):
                        last = t.matmul(up_ps[hc % 2][:],
                                        wup_sb[hc % 2][:, dc * 128:(dc + 1) * 128],
                                        x_fm[:, dc, tb * TB:(tb + 1) * TB],
                                        start=(dc == 0), stop=False)
                    last = t.matmul(up_ps[hc % 2][:],
                                    bupl_sb[:, hc * 128:(hc + 1) * 128],
                                    xaup_sb[:, tb * TB:(tb + 1) * TB],
                                    start=False, stop=True)
                    last.then_inc(s_up.h, 1)
                    if hc >= 1:
                        t.wait_ge(s_rel.h, rel0 + hc)
                        t.matmul(xa_ps[:], adn_sb[:, (hc - 1) * RK:hc * RK],
                                 x2_fm[:, hc - 1, :],
                                 start=(hc == 1), stop=False)
                t.wait_ge(s_rel.h, rel0 + NHC)
                t.matmul(xa_ps[:], adn_sb[:, (NHC - 1) * RK:NHC * RK],
                         x2_fm[:, NHC - 1, :],
                         start=False, stop=True).then_inc(s_xa.h, 1)
            s_up.v += NHC
            s_xa.v += 1

            @blk.scalar
            def _(sc):
                for hc in range(NHC):
                    sc.wait_ge(s_up.h, up0 + hc + 1)
                    sc.activation(x2_fm[:, hc, :], up_ps[hc % 2][:], AF.Relu,
                                  bias=bup_sb[:, hc:hc + 1]).then_inc(s_rel.h, 1)
            s_rel.v += NHC

            @blk.vector
            def _(v):
                v.wait_ge(s_xa.h, xa0 + 1)
                v.tensor_copy(xadn_sb[0:RK, :], xa_ps[:]).then_inc(s_xae.h, 1)
            s_xae.v += 1

        dn0, dnw0 = s_dn.v, s_dnw.v
        r10, sq20, v20, sd20, rc0, nm0 = (s.v for s in
            (s_r1, s_sq2, s_v2, s_sd2, s_rc, s_nm))
        with nc.Block() as blk:
            @blk.sync
            def _(s):
                for tc in range(NTC):
                    qcg = tb * NTC + tc
                    if tb > 0:
                        s.wait_ge(dxs.h(qcg), dxs.tgt(qcg - NTC))
                    s.dma_start(xr_sb[tc][:],
                                x1[qcg * 128:(qcg + 1) * 128, :]
                                ).then_inc(dxl.inc(qcg), 16)
                for dh in range(4):
                    for hc in range(NHC):
                        i = dh * NHC + hc
                        k = tb * 4 * NHC + i
                        if k >= 2:
                            s.wait_ge(s_dnw.h, k - 1)
                        s.dma_start(wdn_sb[i % 2][:],
                                    WdnT[hc][:, dh * 512:(dh + 1) * 512]).then_inc(dwd.inc(k), 16)

            @blk.tensor
            def _(t):
                for dh in range(4):
                    if dh > 0:
                        t.wait_ge(s_r1.h, r10 + dh * NTC)
                    for hc in range(NHC):
                        i = dh * NHC + hc
                        k = tb * 4 * NHC + i
                        t.wait_ge(dwd.h(k), dwd.tgt(k))
                        for tc in range(NTC):
                            ins = t.matmul(dn_ps[tc][:],
                                           x2_fm[:, hc, tc * 128:(tc + 1) * 128],
                                           wdn_sb[i % 2][:],
                                           start=(hc == 0), stop=False)
                            if tc == NTC - 1:
                                ins.then_inc(s_dnw.h, 1)
                    t.wait_ge(s_xae.h, s_xae.v)
                    for tc in range(NTC):
                        t.matmul(dn_ps[tc][:], xadn_sb[:, tc * 128:(tc + 1) * 128],
                                 bdnl_sb[:, dh * 512:(dh + 1) * 512],
                                 start=False, stop=True).then_inc(s_dn.h, 1)
            s_dn.v += 4 * NTC
            s_dnw.v += 4 * NHC

            @blk.vector
            def _(v):
                for dh in range(4):
                    for tc in range(NTC):
                        v.wait_ge(s_dn.h, dn0 + dh * NTC + tc + 1)
                        v.wait_ge(dxl.h(tb * NTC + tc), dxl.tgt(tb * NTC + tc))
                        v.scalar_tensor_tensor(
                            xr_sb[tc][:, dh * 512:(dh + 1) * 512],
                            dn_ps[tc][:], 1.0,
                            xr_sb[tc][:, dh * 512:(dh + 1) * 512],
                            ALU.mult, ALU.add,
                            accum_out=ssum4[:, dh:dh + 1]).then_inc(s_r1.h, 1)
            s_r1.v += 16

        with nc.Block() as blk:
            @blk.vector
            def _(v):
                for tc in range(NTC):
                    v.reduce_sum(ssum[:], xr_sb[tc][:], axis=AX.X).then_inc(s_r1.h, 1)
                    v.wait_ge(s_sq2.h, sq20 + tc + 1)
                    v.tensor_scalar_mul(mean[:], ssum[:], INV_D)
                    v.tensor_scalar_mul(var[:], ssq[:], INV_D)
                    v.tensor_mul(rstd[:], mean[:], mean[:])
                    v.tensor_sub(var[:], var[:], rstd[:])
                    v.tensor_scalar_add(var[:], var[:], LN_EPS).then_inc(s_v2.h, 1)
                    v.wait_ge(s_sd2.h, sd20 + tc + 1)
                    v.reciprocal(rstd[:], sd[:]).then_inc(s_rc.h, 1)
                    v.wait_ge(s_rc.h, rc0 + tc + 1)
                    v.tensor_scalar(xr_sb[tc][:], xr_sb[tc][:], mean[:], rstd[:],
                                    ALU.subtract, ALU.mult).then_inc(s_nm.h, 1)
            s_r1.v += NTC
            s_v2.v += NTC
            s_rc.v += NTC
            s_nm.v += NTC

            @blk.scalar
            def _(sc):
                for tc in range(NTC):
                    sc.wait_ge(s_r1.h, r10 + 16 + tc + 1)
                    sc.activation(x2_fm[:, 0:4, :], xr_sb[tc][:], AF.Square,
                                  accum_out=ssq[:]).then_inc(s_sq2.h, 1)
                    sc.wait_ge(s_v2.h, v20 + tc + 1)
                    sc.activation(sd[:], var[:], AF.Sqrt).then_inc(s_sd2.h, 1)
            s_sq2.v += NTC
            s_sd2.v += NTC

            @blk.gpsimd
            def _(g):
                for tc in range(NTC):
                    g.wait_ge(s_nm.h, nm0 + tc + 1)
                    qc = tb * NTC + tc
                    g.dma_start(x2out[qc * 128:(qc + 1) * 128, :],
                                xr_sb[tc][:]).then_inc(dxs.inc(qc), 16)
                if tb == NTB - 1:
                    for tc in range(NTC):
                        g.wait_ge(dxs.h(tc), dxs.tgt((NTB - 1) * NTC + tc))
    p.pop()
    return



def phase_qkv1(p):
    nc = p.nc
    x2 = p.inp("x2", [T, D], F32)
    Wqkv = p.inp("L1_attn_W_qkv", [3 * D, D], F32)
    ident = p.inp("c_ident", [128, 128], BF16)
    Qo = p.outp("Q1_fm", [D, T], BF16)
    Ko = p.outp("K1_fm", [D, T], BF16)
    Vo = p.outp("V1_tm", [T, D], BF16)

    p.push()
    ident_sb = p.sb("id", [128, 128], BF16)
    x_fm = p.sb("xfm", [128, ND, T], BF16)
    wtmp = [p.sb("wt", [128, D], BF16) for _ in range(2)]
    wt_sb = p.sb("wts", [128, ND, 128], BF16)
    WTv_sb = p.sb("wtv", [128, ND, D], BF16)
    ev_sb = [p.sb("ev", [128, T], BF16) for _ in range(2)]
    wt_ps = [p.ps("wtp", [128, 128], BF16) for _ in range(2)]
    pj_ps = p.ps("pj", [128, T], F32)

    s_c = p.sem("ec", global_=True)
    with nc.Block() as blk:
        @blk.sync
        def _(s):
            s.dma_start(ident_sb[:], ident).then_inc(s_c.h, 16)
            s.wait_ge(s_c.h, s_c.v + 16)
        s_c.v += 16

    emit_load_xfm(p, x2, x_fm, ident_sb)

    s_tt, s_te = p.sem("tt"), p.sem("te")
    dwv = p.gdsem("ewv", 2)
    tt0, te0 = s_tt.v, s_te.v
    with nc.Block() as blk:
        @blk.gpsimd
        def _(g):
            for i, ec in enumerate(range(2 * ND, 3 * ND)):
                if i >= 2:
                    g.wait_ge(s_te.h, te0 + (i - 1) * ND)
                g.dma_start(wtmp[i % 2][:], Wqkv[ec * 128:(ec + 1) * 128, :]).then_inc(dwv.inc(i), 16)

        @blk.tensor
        def _(t):
            for i in range(ND):
                t.wait_ge(dwv.h(i), dwv.tgt(i))
                for dc in range(ND):
                    j = i * ND + dc
                    if j >= 2:
                        t.wait_ge(s_te.h, te0 + j - 1)
                    t.transpose(wt_ps[j % 2][:], wtmp[i % 2][:, dc * 128:(dc + 1) * 128],
                                ident_sb[:]).then_inc(s_tt.h, 1)
        s_tt.v += ND * ND

        @blk.vector
        def _(v):
            for i in range(ND):
                for dc in range(ND):
                    j = i * ND + dc
                    v.wait_ge(s_tt.h, tt0 + j + 1)
                    v.tensor_copy(WTv_sb[:, dc, i * 128:(i + 1) * 128],
                                  wt_ps[j % 2][:]).then_inc(s_te.h, 1)
        s_te.v += ND * ND

    s_vm, s_ve = p.sem("vm"), p.sem("ve")
    dvs = p.gdsem("evs", 2)
    vm0, ve0 = s_vm.v, s_ve.v
    with nc.Block() as blk:
        @blk.tensor
        def _(t):
            for tc in range(NT):
                if tc > 0:
                    t.wait_ge(s_ve.h, ve0 + tc)
                last = None
                for dc in range(ND):
                    for n in range(4):
                        last = t.matmul(pj_ps[:, n * 512:(n + 1) * 512],
                                        x_fm[:, dc, tc * 128:(tc + 1) * 128],
                                        WTv_sb[:, dc, n * 512:(n + 1) * 512],
                                        start=(dc == 0), stop=(dc == ND - 1))
                last.then_inc(s_vm.h, 1)
        s_vm.v += NT

        @blk.vector
        def _(v):
            for tc in range(NT):
                v.wait_ge(s_vm.h, vm0 + tc + 1)
                if tc >= 2:
                    v.wait_ge(dvs.h(tc), dvs.tgt(tc - 2))
                v.tensor_copy(ev_sb[tc % 2][:], pj_ps[:]).then_inc(s_ve.h, 1)
        s_ve.v += NT

        @blk.gpsimd
        def _(g):
            for tc in range(NT):
                g.wait_ge(s_ve.h, ve0 + tc + 1)
                g.dma_start(Vo[tc * 128:(tc + 1) * 128, :], ev_sb[tc % 2][:]).then_inc(dvs.inc(tc), 16)
            g.wait_ge(dvs.h(NT - 2), dvs.tgt(NT - 2))
            g.wait_ge(dvs.h(NT - 1), dvs.tgt(NT - 1))

    s_t2, s_e2, s_qm, s_qe = (p.sem(x) for x in ("t2", "e2", "qm", "qe"))
    dw2 = p.gdsem("ew2", 2)
    dqs = p.gdsem("eqs", 2)
    t20, e20, qm0, qe0 = (s.v for s in (s_t2, s_e2, s_qm, s_qe))
    with nc.Block() as blk:
        @blk.gpsimd
        def _(g):
            def qstore(e):
                g.wait_ge(s_qe.h, qe0 + e + 1)
                dst = Qo if e < ND else Ko
                r = e % ND
                g.dma_start(dst[r * 128:(r + 1) * 128, :], ev_sb[e % 2][:]).then_inc(dqs.inc(e), 16)
            for ec in range(2 * ND):
                if ec >= 2:
                    g.wait_ge(s_qm.h, qm0 + ec - 1)
                g.dma_start(wtmp[ec % 2][:], Wqkv[ec * 128:(ec + 1) * 128, :]).then_inc(dw2.inc(ec), 16)
                if ec >= 2:
                    qstore(ec - 2)
            qstore(2 * ND - 2)
            qstore(2 * ND - 1)
            g.wait_ge(dqs.h(2 * ND - 2), dqs.tgt(2 * ND - 2))
            g.wait_ge(dqs.h(2 * ND - 1), dqs.tgt(2 * ND - 1))

        @blk.tensor
        def _(t):
            for ec in range(2 * ND):
                t.wait_ge(dw2.h(ec), dw2.tgt(ec))
                for dc in range(ND):
                    j = ec * ND + dc
                    if j >= 2:
                        t.wait_ge(s_e2.h, e20 + j - 1)
                    t.transpose(wt_ps[j % 2][:], wtmp[ec % 2][:, dc * 128:(dc + 1) * 128],
                                ident_sb[:]).then_inc(s_t2.h, 1)
                t.wait_ge(s_e2.h, e20 + (ec + 1) * ND)
                if ec > 0:
                    t.wait_ge(s_qe.h, qe0 + ec)
                last = None
                for dc in range(ND):
                    for n in range(4):
                        last = t.matmul(pj_ps[:, n * 512:(n + 1) * 512],
                                        wt_sb[:, dc, :],
                                        x_fm[:, dc, n * 512:(n + 1) * 512],
                                        start=(dc == 0), stop=(dc == ND - 1))
                last.then_inc(s_qm.h, 1)
        s_t2.v += 2 * ND * ND
        s_qm.v += 2 * ND

        @blk.vector
        def _(v):
            for ec in range(2 * ND):
                for dc in range(ND):
                    j = ec * ND + dc
                    v.wait_ge(s_t2.h, t20 + j + 1)
                    v.tensor_copy(wt_sb[:, dc, :], wt_ps[j % 2][:]).then_inc(s_e2.h, 1)
                v.wait_ge(s_qm.h, qm0 + ec + 1)
                if ec >= 2:
                    v.wait_ge(dqs.h(ec), dqs.tgt(ec - 2))
                v.tensor_copy(ev_sb[ec % 2][:], pj_ps[:]).then_inc(s_qe.h, 1)
        s_e2.v += 2 * ND * ND
        s_qe.v += 2 * ND
    p.pop()
    return



def phase_bffn(p, xres_name="x3", out_name="x4"):
    nc = p.nc
    x3 = p.inp(xres_name, [T, D], F32)
    Wt = p.inp("L1_ffn_weight", [D, D], F32)
    brow = p.inp("bffn_row", [1, D], F32)
    ident = p.inp("c_ident", [128, 128], BF16)
    ones1 = p.inp("c_ones1", [1, 128], BF16)
    OUT = p.outp(out_name, [T, D], F32)

    p.push()
    ident_sb = p.sb("id", [128, 128], BF16)
    ones1_sb = p.sb("on", [1, 128], BF16)
    brow_sb = p.sb("br", [1, D], BF16)
    x_fm = p.sb("xfm", [128, ND, T], BF16)
    bwT_sb = p.sb("bwt", [128, ND, D], BF16)
    wtmp = [p.sb("wt", [128, D], BF16) for _ in range(2)]
    sgn_sb = [p.sb("sg", [128, D], BF16) for _ in range(2)]
    asum = p.sb("as", [128, 1], F32)
    alpha = p.sb("al", [128, 1], F32)
    wt_ps = [p.ps("wtp", [128, 128], BF16) for _ in range(2)]
    ps = p.ps("pj", [128, D], F32)
    xr = [p.sb("xr", [128, D], F32) for _ in range(2)]
    xo = [p.sb("xo", [128, D], F32) for _ in range(2)]
    ssum = p.sb("ssum", [128, 1], F32)
    ssq = p.sb("ssq", [128, 1], F32)
    mean = p.sb("mean", [128, 1], F32)
    var = p.sb("var", [128, 1], F32)
    sd = p.sb("sd", [128, 1], F32)
    rstd = p.sb("rstd", [128, 1], F32)

    s_c = p.sem("hc2", global_=True)
    s_cg = p.sem("hcg", global_=True)
    with nc.Block() as blk:
        @blk.sync
        def _(s):
            s.dma_start(ident_sb[:], ident).then_inc(s_c.h, 16)
            s.dma_start(ones1_sb[:], ones1).then_inc(s_c.h, 16)
            s.wait_ge(s_c.h, s_c.v + 32)
        s_c.v += 32

        @blk.gpsimd
        def _(g):
            g.dma_start(brow_sb[:], brow).then_inc(s_cg.h, 16)
            g.wait_ge(s_cg.h, s_cg.v + 16)
        s_cg.v += 16

    emit_load_xfm(p, x3, x_fm, ident_sb)

    s_al, s_sg, s_bw, s_tt, s_te, s_ab = (p.sem(x) for x in
        ("al", "sg", "bw", "tt", "te", "ab"))
    dwl = p.gdsem("hwl", 2)
    al0, sg0, bw0, tt0, te0, ab0 = (s.v for s in (s_al, s_sg, s_bw, s_tt, s_te, s_ab))
    with nc.Block() as blk:
        @blk.gpsimd
        def _(g):
            for oc in range(ND):
                if oc >= 2:
                    g.wait_ge(s_bw.h, bw0 + oc - 1)
                g.dma_start(wtmp[oc % 2][:], Wt[oc * 128:(oc + 1) * 128, :]).then_inc(dwl.inc(oc), 16)

        @blk.vector
        def _(v):
            for oc in range(ND):
                v.wait_ge(s_ab.h, ab0 + oc + 1)
                v.tensor_scalar_mul(alpha[:], asum[:], INV_D).then_inc(s_al.h, 1)
                v.wait_ge(s_sg.h, sg0 + oc + 1)
                v.tensor_scalar_mul(sgn_sb[oc % 2][:], sgn_sb[oc % 2][:],
                                    alpha[:]).then_inc(s_bw.h, 1)
                for dc in range(ND):
                    j = oc * ND + dc
                    v.wait_ge(s_tt.h, tt0 + j + 1)
                    v.tensor_copy(bwT_sb[:, dc, oc * 128:(oc + 1) * 128],
                                  wt_ps[j % 2][:]).then_inc(s_te.h, 1)
        s_ab.v += ND
        s_al.v += ND
        s_bw.v += ND
        s_te.v += ND * ND

        @blk.scalar
        def _(sc):
            for oc in range(ND):
                sc.wait_ge(dwl.h(oc), dwl.tgt(oc))
                if oc >= 2:
                    sc.wait_ge(s_tt.h, tt0 + (oc - 1) * ND)
                sc.activation(sgn_sb[oc % 2][:], wtmp[oc % 2][:], AF.Abs,
                              accum_out=asum[:]).then_inc(s_ab.h, 1)
                sc.sign(sgn_sb[oc % 2][:], wtmp[oc % 2][:]).then_inc(s_sg.h, 1)
        s_sg.v += ND

        @blk.tensor
        def _(t):
            for oc in range(ND):
                t.wait_ge(s_bw.h, bw0 + oc + 1)
                for dc in range(ND):
                    j = oc * ND + dc
                    if j >= 2:
                        t.wait_ge(s_te.h, te0 + j - 1)
                    t.transpose(wt_ps[j % 2][:], sgn_sb[oc % 2][:, dc * 128:(dc + 1) * 128],
                                ident_sb[:]).then_inc(s_tt.h, 1)
        s_tt.v += ND * ND

    _emit_proj_res_ln(p, x_fm, bwT_sb, (ones1_sb, brow_sb), x3, OUT, ps,
                      xr, xo, ssum, ssq, mean, var, sd, rstd)
    p.pop()
    return



NCORES = 8

INTERNAL_NAMES = {"Q_fm", "K_fm", "V_tm", "O_fm", "x1", "WupT", "WdnT", "x2",
                  "Q1_fm", "K1_fm", "V1_tm", "O1_fm", "x3"}


def build_fused(n_phases=9, out_name=None):
    p = Prog()
    p.internal = set(INTERNAL_NAMES)
    if out_name:
        p.internal.discard(out_name)
    phases = [
        (phase_qkv0, {}),
        (phase_attn, {"causal": False}),
        (phase_proj_ln, {"layer": 0, "bname": "bo_row",
                         "xres_name": "x_in", "xout_name": "x1"}),
        (phase_ffn_prep, {}),
        (phase_ffn_main, {}),
        (phase_qkv1, {}),
        (phase_attn, {"causal": True, "qn": "Q1_fm", "kn": "K1_fm",
                      "vn": "V1_tm", "on": "O1_fm"}),
        (phase_proj_ln, {"layer": 1, "on": "O1_fm", "wname": "L1_attn_W_o",
                         "bname": None, "xres_name": "x2", "xout_name": "x3",
                         "w_transposed": True}),
        (phase_bffn, {}),
    ]
    for fn, kw in phases[:n_phases]:
        p.push()
        p.phase_sems = []
        fn(p, **kw)
        used = list(p.phase_sems)
        p.pop()
        with p.nc.Block() as blk:
            @blk.gpsimd
            def _(g):
                for h in used:
                    g.sem_clear(h)
    p.close()
    return p


def host_maps(inputs, cores):
    consts = host_consts()
    x_full = np.ascontiguousarray(inputs["L0_attn_x"])
    shared = {**consts,
              "L0_attn_Wq": inputs["L0_attn_Wq"], "L0_attn_Wk": inputs["L0_attn_Wk"],
              "L0_attn_Wv": inputs["L0_attn_Wv"], "L0_attn_Wo": inputs["L0_attn_Wo"],
              "bq_cols": np.ascontiguousarray(inputs["L0_attn_bq"].reshape(ND, 128).T),
              "bk_cols": np.ascontiguousarray(inputs["L0_attn_bk"].reshape(ND, 128).T),
              "bv_row": np.ascontiguousarray(inputs["L0_attn_bv"].reshape(1, D)),
              "bo_row": np.ascontiguousarray(inputs["L0_attn_bo"].reshape(1, D)),
              "L0_ffn_w_up_q": inputs["L0_ffn_w_up_q"],
              "L0_ffn_w_up_scale": inputs["L0_ffn_w_up_scale"],
              "L0_ffn_w_down_q": inputs["L0_ffn_w_down_q"],
              "L0_ffn_w_down_scale": inputs["L0_ffn_w_down_scale"],
              "bup_cols": np.ascontiguousarray(inputs["L0_ffn_b_up"].reshape(NHC, 128).T),
              "bdnl_ext": np.ascontiguousarray(np.concatenate(
                  [inputs["L0_ffn_w_down_lora_b"],
                   inputs["L0_ffn_b_down"].reshape(1, D)], axis=0)),
              "L0_ffn_w_up_lora_a": inputs["L0_ffn_w_up_lora_a"],
              "L0_ffn_w_up_lora_b": inputs["L0_ffn_w_up_lora_b"],
              "L0_ffn_w_down_lora_a": inputs["L0_ffn_w_down_lora_a"],
              "L1_attn_W_qkv": inputs["L1_attn_W_qkv"],
              "L1_attn_W_o": inputs["L1_attn_W_o"],
              "L1_ffn_weight": inputs["L1_ffn_weight"],
              "bffn_row": np.ascontiguousarray(inputs["L1_ffn_bias"].reshape(1, D))}
    maps = []
    for b in cores:
        m = dict(shared)
        m["x_in"] = np.ascontiguousarray(x_full[b])
        maps.append(m)
    return maps


def kernel(**inputs):
    cores = list(range(NCORES))
    p = build_fused()
    maps = host_maps(inputs, cores)
    res = run_bass_kernel_spmd(p.nc, maps, core_ids=cores)
    return np.stack([res.results[b]["x4"] for b in cores], axis=0)


def _launch(phase_fn, kwargs, per_core_maps, core_ids):
    p = Prog()
    phase_fn(p, **kwargs)
    p.close()
    res = run_bass_kernel_spmd(p.nc, per_core_maps, core_ids=core_ids)
    return res.results


def kernel_multilaunch(**inputs):
    consts = host_consts()
    cores = list(range(NCORES))
    x_full = np.ascontiguousarray(inputs["L0_attn_x"])

    def rep(d):
        return [dict(d) for _ in cores]

    def percore(results, key):
        return [r[key] for r in results]

    bq = inputs["L0_attn_bq"]
    bk = inputs["L0_attn_bk"]
    bv = inputs["L0_attn_bv"]
    bo = inputs["L0_attn_bo"]

    maps = rep({**consts,
                "L0_attn_Wq": inputs["L0_attn_Wq"], "L0_attn_Wk": inputs["L0_attn_Wk"],
                "L0_attn_Wv": inputs["L0_attn_Wv"],
                "bq_cols": np.ascontiguousarray(bq.reshape(ND, 128).T),
                "bk_cols": np.ascontiguousarray(bk.reshape(ND, 128).T),
                "bv_row": np.ascontiguousarray(bv.reshape(1, D))})
    for b in cores:
        maps[b]["x_in"] = np.ascontiguousarray(x_full[b])
    rA = _launch(phase_qkv0, {}, maps, cores)

    maps = rep(consts)
    for b in cores:
        maps[b].update(Q_fm=rA[b]["Q_fm"], K_fm=rA[b]["K_fm"], V_tm=rA[b]["V_tm"])
    rB = _launch(phase_attn, {"causal": False}, maps, cores)

    maps = rep({**consts, "L0_attn_Wo": inputs["L0_attn_Wo"],
                "bo_row": np.ascontiguousarray(bo.reshape(1, D))})
    for b in cores:
        maps[b].update(O_fm=rB[b]["O_fm"], x_in=np.ascontiguousarray(x_full[b]))
    rC = _launch(phase_proj_ln, {"layer": 0, "bname": "bo_row",
                                 "xres_name": "x_in", "xout_name": "x1"}, maps, cores)

    maps = rep({**consts,
                "L0_ffn_w_up_q": inputs["L0_ffn_w_up_q"],
                "L0_ffn_w_up_scale": inputs["L0_ffn_w_up_scale"],
                "L0_ffn_w_down_q": inputs["L0_ffn_w_down_q"],
                "L0_ffn_w_down_scale": inputs["L0_ffn_w_down_scale"]})
    rD1 = _launch(phase_ffn_prep, {}, maps, cores)

    maps = rep({**consts,
                "bup_cols": np.ascontiguousarray(inputs["L0_ffn_b_up"].reshape(NHC, 128).T),
                "bdnl_ext": np.ascontiguousarray(np.concatenate(
                    [inputs["L0_ffn_w_down_lora_b"],
                     inputs["L0_ffn_b_down"].reshape(1, D)], axis=0)),
                "L0_ffn_w_up_lora_a": inputs["L0_ffn_w_up_lora_a"],
                "L0_ffn_w_up_lora_b": inputs["L0_ffn_w_up_lora_b"],
                "L0_ffn_w_down_lora_a": inputs["L0_ffn_w_down_lora_a"]})
    for b in cores:
        maps[b].update(x1=rC[b]["x1"], WupT=rD1[b]["WupT"], WdnT=rD1[b]["WdnT"])
    rD2 = _launch(phase_ffn_main, {}, maps, cores)

    maps = rep({**consts, "L1_attn_W_qkv": inputs["L1_attn_W_qkv"]})
    for b in cores:
        maps[b]["x2"] = rD2[b]["x2"]
    rE = _launch(phase_qkv1, {}, maps, cores)

    maps = rep(consts)
    for b in cores:
        maps[b].update(Q1_fm=rE[b]["Q1_fm"], K1_fm=rE[b]["K1_fm"], V1_tm=rE[b]["V1_tm"])
    rF = _launch(phase_attn, {"causal": True, "qn": "Q1_fm", "kn": "K1_fm",
                              "vn": "V1_tm", "on": "O1_fm"}, maps, cores)

    maps = rep({**consts, "L1_attn_W_o": inputs["L1_attn_W_o"]})
    for b in cores:
        maps[b].update(O1_fm=rF[b]["O1_fm"], x2=rD2[b]["x2"])
    rG = _launch(phase_proj_ln, {"layer": 1, "on": "O1_fm", "wname": "L1_attn_W_o",
                                 "bname": None, "xres_name": "x2", "xout_name": "x3",
                                 "w_transposed": True}, maps, cores)

    maps = rep({**consts, "L1_ffn_weight": inputs["L1_ffn_weight"],
                "bffn_row": np.ascontiguousarray(inputs["L1_ffn_bias"].reshape(1, D))})
    for b in cores:
        maps[b]["x3"] = rG[b]["x3"]
    rH = _launch(phase_bffn, {}, maps, cores)

    out = np.stack([rH[b]["x4"] for b in cores], axis=0)
    return out


# version 04
# speedup vs baseline: 1.1171x; 1.1171x over previous
import math
import numpy as np
import ml_dtypes
from contextlib import ExitStack

import concourse.bass as bass
import concourse.mybir as mybir
from concourse.bass_utils import run_bass_kernel_spmd

F32 = mybir.dt.float32
BF16 = mybir.dt.bfloat16
I32 = mybir.dt.int32
AF = mybir.ActivationFunctionType
ALU = mybir.AluOpType
AX = mybir.AxisListType

T = 2048
D = 2048
NH = 16
HS = 128
HF = 8192
RK = 16
NT = T // 128
ND = D // 128
NHC = HF // 128
SM_SCALE = 1.0 / math.sqrt(HS)
LN_EPS = 1e-5
INV_D = 1.0 / D

BF = ml_dtypes.bfloat16


class Sem:
    __slots__ = ("h", "v")

    def __init__(self, h):
        self.h = h
        self.v = 0


class DSem:

    def __init__(self, p, name, n):
        self.n = n
        self.s = [p.sem(f"{name}_sl{i}", global_=True) for i in range(n)]
        self.base = [x.v for x in self.s]

    def rebase(self):
        self.base = [x.v for x in self.s]

    def inc(self, k, amt=16):
        ss = self.s[k % self.n]
        ss.v += amt
        return ss.h

    def h(self, k):
        return self.s[k % self.n].h

    def tgt(self, k, amt=16):
        return self.base[k % self.n] + amt * (k // self.n + 1)


class Prog:
    def __init__(self, name="prog"):
        self.nc = bass.Bass("TRN2", target_bir_lowering=False,
                            detect_race_conditions=False)
        self.sem_st = ExitStack()
        self.stacks = [ExitStack()]
        self._n = 0

    def uid(self):
        self._n += 1
        return self._n

    def sem(self, name, global_=False):
        if global_:
            if not hasattr(self, "_gsems"):
                self._gsems = {}
            if name not in self._gsems:
                h = self.stacks[0].enter_context(self.nc.semaphore(name))
                self._gsems[name] = Sem(h)
            return self._gsems[name]
        h = self.stacks[-1].enter_context(self.nc.semaphore(f"{name}{self.uid()}"))
        if not hasattr(self, "phase_sems"):
            self.phase_sems = []
        self.phase_sems.append(h)
        return Sem(h)

    def gdsem(self, name, n):
        if not hasattr(self, "_gds"):
            self._gds = {}
        if name not in self._gds:
            self._gds[name] = DSem(self, name, n)
        d = self._gds[name]
        assert d.n == n
        d.rebase()
        return d

    def sb(self, name, shape, dt):
        return self.stacks[-1].enter_context(
            self.nc.sbuf_tensor(f"{name}{self.uid()}", shape, dt))

    def ps(self, name, shape, dt):
        return self.stacks[-1].enter_context(
            self.nc.psum_tensor(f"{name}{self.uid()}", shape, dt))

    def push(self):
        self.stacks.append(ExitStack())

    def pop(self):
        self.stacks.pop().close()

    def inp(self, name, shape, dt):
        return self._dram(name, shape, dt, "ExternalInput")

    def outp(self, name, shape, dt):
        return self._dram(name, shape, dt, "ExternalOutput")

    def _dram(self, name, shape, dt, kind):
        if not hasattr(self, "_dram_cache"):
            self._dram_cache = {}
        if name in self._dram_cache:
            return self._dram_cache[name]
        if getattr(self, "internal", None) and name in self.internal:
            kind = "Internal"
        ap = self.nc.dram_tensor(name, shape, dt, kind=kind).ap()
        self._dram_cache[name] = ap
        return ap

    def close(self):
        while self.stacks:
            self.stacks.pop().close()
        self.sem_st.close()



def host_consts():
    c = {}
    c["c_ident"] = np.eye(128, dtype=BF)
    c["c_tril"] = np.tril(np.ones((128, 128), dtype=np.float32)).astype(BF)
    triu_neg = np.triu(np.full((128, 128), -1e30, dtype=np.float32), k=1)
    c["c_triuneg"] = triu_neg
    c["c_ones1"] = np.ones((1, 128), dtype=BF)
    c["c_onesrow"] = np.ones((1, 512), dtype=BF)
    t = np.arange(T, dtype=np.float32)
    inv_freq = (1.0 / (10000.0 ** (np.arange(0, HS, 2, dtype=np.float32) / HS))).astype(np.float32)
    ang = t[None, :] * inv_freq[:, None]
    cos = np.cos(ang).astype(np.float32)
    sin = np.sin(ang).astype(np.float32)
    C = np.empty((128, T), dtype=np.float32)
    S = np.empty((128, T), dtype=np.float32)
    C[0::2] = cos
    C[1::2] = cos
    S[0::2] = -sin
    S[1::2] = sin
    c["c_ropeC"] = C
    c["c_ropeS"] = S
    return c


SWAP_MASK = []
for _i in range(16):
    SWAP_MASK += [2 * _i + 1, 2 * _i]



def emit_load_xfm(p, x_dram, x_fm, ident_sb, bufs=None):
    nc = p.nc
    p.push()
    xbf = bufs if bufs is not None else [p.sb("xbf", [128, D], BF16) for _ in range(2)]
    pt = [p.ps("xtp", [128, 128], BF16) for _ in range(2)]
    dld = p.gdsem("xld", 2)
    s_tp, s_ev = p.sem("xtp"), p.sem("xev")
    tp0, ev0 = s_tp.v, s_ev.v
    with nc.Block() as blk:
        @blk.gpsimd
        def _(g):
            for tc in range(NT):
                if tc >= 2:
                    g.wait_ge(s_ev.h, ev0 + (tc - 1) * ND)
                g.dma_start(xbf[tc % 2][:], x_dram[tc * 128:(tc + 1) * 128, :]).then_inc(dld.inc(tc), 16)

        @blk.tensor
        def _(t):
            for tc in range(NT):
                t.wait_ge(dld.h(tc), dld.tgt(tc))
                for dc in range(ND):
                    i = tc * ND + dc
                    if i >= 2:
                        t.wait_ge(s_ev.h, ev0 + i - 1)
                    t.transpose(pt[i % 2][:], xbf[tc % 2][:, dc * 128:(dc + 1) * 128],
                                ident_sb[:]).then_inc(s_tp.h, 1)
        s_tp.v += NT * ND

        @blk.vector
        def _(v):
            for tc in range(NT):
                for dc in range(ND):
                    i = tc * ND + dc
                    v.wait_ge(s_tp.h, tp0 + i + 1)
                    v.tensor_copy(x_fm[:, dc, tc * 128:(tc + 1) * 128], pt[i % 2][:]).then_inc(s_ev.h, 1)
        s_ev.v += NT * ND
    p.pop()


def emit_res_ln(p, blk_engines, delta_ps, xres_sb, xout_sb, tmp):
    raise NotImplementedError



def phase_qkv0(p):
    nc = p.nc
    x = p.inp("x_in", [T, D], F32)
    Wq = p.inp("L0_attn_Wq", [D, D], F32)
    Wk = p.inp("L0_attn_Wk", [D, D], F32)
    Wv = p.inp("L0_attn_Wv", [D, D], F32)
    bq = p.inp("bq_cols", [128, ND], F32)
    bk = p.inp("bk_cols", [128, ND], F32)
    bv = p.inp("bv_row", [1, D], F32)
    ident = p.inp("c_ident", [128, 128], BF16)
    ropeC = p.inp("c_ropeC", [128, T], F32)
    ropeS = p.inp("c_ropeS", [128, T], F32)
    ones1 = p.inp("c_ones1", [1, 128], BF16)
    Qo = p.outp("Q_fm", [D, T], BF16)
    Ko = p.outp("K_fm", [D, T], BF16)
    Vo = p.outp("V_tm", [T, D], BF16)

    ident_sb = p.sb("ident", [128, 128], BF16)
    ropeC_sb = p.sb("ropeC", [128, T], F32)
    ropeS_sb = p.sb("ropeS", [128, T], F32)
    ones1_sb = p.sb("ones1", [1, 128], BF16)
    bq_sb = p.sb("bq", [128, ND], F32)
    bk_sb = p.sb("bk", [128, ND], F32)
    bv_sb = p.sb("bv", [1, D], BF16)
    x_fm = p.sb("xfm", [128, ND, T], BF16)
    w_sb = p.sb("wsb", [128, ND, D], BF16)

    s_c = p.sem("qk0c", global_=True)
    s_cg = p.sem("qk0cg", global_=True)
    c0 = s_c.v
    with nc.Block() as blk:
        @blk.sync
        def _(s):
            s.dma_start(ident_sb[:], ident).then_inc(s_c.h, 16)
            s.dma_start(ropeC_sb[:], ropeC).then_inc(s_c.h, 16)
            s.dma_start(ropeS_sb[:], ropeS).then_inc(s_c.h, 16)
            s.dma_start(ones1_sb[:], ones1).then_inc(s_c.h, 16)
            s.dma_start(bq_sb[:], bq).then_inc(s_c.h, 16)
            s.dma_start(bk_sb[:], bk).then_inc(s_c.h, 16)
            s.wait_ge(s_c.h, c0 + 96)
        s_c.v += 96

        @blk.gpsimd
        def _(g):
            g.dma_start(bv_sb[:], bv).then_inc(s_cg.h, 16)
            g.wait_ge(s_cg.h, s_cg.v + 16)
        s_cg.v += 16

    emit_load_xfm(p, x, x_fm, ident_sb)

    proj_ps = p.ps("pj", [128, T], F32)
    for W, bias_cols, OUT, roped in ((Wq, bq_sb, Qo, True), (Wk, bk_sb, Ko, True)):
        _emit_proj_fm(p, proj_ps, W, w_sb, x_fm, bias_cols, OUT,
                      rope=(ropeC_sb, ropeS_sb) if roped else None)

    _emit_proj_tm_v(p, proj_ps, Wv, w_sb, x_fm, bv_sb, ones1_sb, Vo)
    return ["x_in", "L0_attn_Wq", "L0_attn_Wk", "L0_attn_Wv", "bq_cols",
            "bk_cols", "bv_row", "c_ident", "c_ropeC", "c_ropeS", "c_ones1"]


def _emit_proj_fm(p, ps, W, w_sb, x_fm, bias_cols, OUT, rope=None):
    nc = p.nc
    p.push()
    raw = [p.sb("qraw", [128, T], BF16) for _ in range(2)]
    sw = p.sb("qsw", [128, T], BF16)
    t1 = p.sb("qt1", [128, T], BF16)
    out_sb = [p.sb("qout", [128, T], BF16) for _ in range(2)]
    s_w = p.sem("pfw", global_=True)
    s_mm, s_ev, s_rp = (p.sem(x) for x in ("mm", "ev", "rp"))
    dst = p.gdsem("pfst", 2)
    w0, mm0, ev0, rp0 = s_w.v, s_mm.v, s_ev.v, s_rp.v
    with nc.Block() as blk:
        @blk.gpsimd
        def _(g):
            for dc in range(ND):
                g.dma_start(w_sb[:, dc, :], W[dc * 128:(dc + 1) * 128, :]).then_inc(s_w.h, 16)
            for ec in range(ND):
                g.wait_ge(s_rp.h, rp0 + ec + 1)
                g.dma_start(OUT[ec * 128:(ec + 1) * 128, :], out_sb[ec % 2][:]).then_inc(dst.inc(ec), 16)
            g.wait_ge(dst.h(ND - 2), dst.tgt(ND - 2))
            g.wait_ge(dst.h(ND - 1), dst.tgt(ND - 1))
        s_w.v += 16 * ND

        @blk.tensor
        def _(t):
            t.wait_ge(s_w.h, w0 + 16 * ND)
            for ec in range(ND):
                if ec > 0:
                    t.wait_ge(s_ev.h, ev0 + ec)
                for dc in range(ND):
                    for n in range(4):
                        ins = t.matmul(ps[:, n * 512:(n + 1) * 512],
                                       w_sb[:, dc, ec * 128:(ec + 1) * 128],
                                       x_fm[:, dc, n * 512:(n + 1) * 512],
                                       start=(dc == 0), stop=(dc == ND - 1))
                        if dc == ND - 1 and n == 3:
                            ins.then_inc(s_mm.h, 1)
        s_mm.v += ND

        @blk.scalar
        def _(sc):
            for ec in range(ND):
                sc.wait_ge(s_mm.h, mm0 + ec + 1)
                if ec >= 2:
                    sc.wait_ge(s_rp.h, rp0 + ec - 1)
                sc.activation(raw[ec % 2][:], ps[:], AF.Identity,
                              bias=bias_cols[:, ec:ec + 1], scale=1.0).then_inc(s_ev.h, 1)
        s_ev.v += ND

        @blk.vector
        def _(v):
            C, S = rope
            for ec in range(ND):
                v.wait_ge(s_ev.h, ev0 + ec + 1)
                if ec >= 2:
                    v.wait_ge(dst.h(ec), dst.tgt(ec - 2))
                r = raw[ec % 2]
                v.stream_shuffle(sw[:], r[:], SWAP_MASK)
                v.tensor_mul(t1[:], r[:], C[:])
                v.tensor_mul(sw[:], sw[:], S[:])
                v.tensor_add(out_sb[ec % 2][:], t1[:], sw[:]).then_inc(s_rp.h, 1)
        s_rp.v += ND
    p.pop()


def _emit_proj_tm_v(p, ps, W, w_sb, x_fm, bv_sb, ones1_sb, OUT):
    nc = p.nc
    p.push()
    out_sb = [p.sb("vout", [128, D], BF16) for _ in range(2)]
    s_w = p.sem("tvw", global_=True)
    s_mm, s_ev = (p.sem(x) for x in ("mm", "ev"))
    dst = p.gdsem("tvst", 2)
    w0, mm0, ev0 = s_w.v, s_mm.v, s_ev.v
    with nc.Block() as blk:
        @blk.gpsimd
        def _(g):
            for dc in range(ND):
                g.dma_start(w_sb[:, dc, :], W[dc * 128:(dc + 1) * 128, :]).then_inc(s_w.h, 16)
            for tc in range(NT):
                g.wait_ge(s_ev.h, ev0 + tc + 1)
                g.dma_start(OUT[tc * 128:(tc + 1) * 128, :], out_sb[tc % 2][:]).then_inc(dst.inc(tc), 16)
            g.wait_ge(dst.h(NT - 2), dst.tgt(NT - 2))
            g.wait_ge(dst.h(NT - 1), dst.tgt(NT - 1))
        s_w.v += 16 * ND

        @blk.tensor
        def _(t):
            t.wait_ge(s_w.h, w0 + 16 * ND)
            for tc in range(NT):
                if tc > 0:
                    t.wait_ge(s_ev.h, ev0 + tc)
                for dc in range(ND):
                    for n in range(4):
                        t.matmul(ps[:, n * 512:(n + 1) * 512],
                                 x_fm[:, dc, tc * 128:(tc + 1) * 128],
                                 w_sb[:, dc, n * 512:(n + 1) * 512],
                                 start=(dc == 0), stop=False)
                for n in range(4):
                    ins = t.matmul(ps[:, n * 512:(n + 1) * 512],
                                   ones1_sb[:],
                                   bv_sb[:, n * 512:(n + 1) * 512],
                                   start=False, stop=True)
                    if n == 3:
                        ins.then_inc(s_mm.h, 1)
        s_mm.v += NT

        @blk.vector
        def _(v):
            for tc in range(NT):
                v.wait_ge(s_mm.h, mm0 + tc + 1)
                if tc >= 2:
                    v.wait_ge(dst.h(tc), dst.tgt(tc - 2))
                v.tensor_copy(out_sb[tc % 2][:], ps[:]).then_inc(s_ev.h, 1)
        s_ev.v += NT
    p.pop()



def phase_attn(p, causal, qn="Q_fm", kn="K_fm", vn="V_tm", on="O_fm"):
    nc = p.nc
    Q = p.inp(qn, [D, T], BF16)
    K = p.inp(kn, [D, T], BF16)
    V = p.inp(vn, [T, D], BF16)
    O = p.outp(on, [D, T], BF16)
    ident = p.inp("c_ident", [128, 128], BF16)
    if causal:
        mask_in = p.inp("c_triuneg", [128, 128], F32)
    else:
        mask_in = p.inp("c_tril", [128, 128], BF16)

    p.push()
    V_sb = p.sb("v", [128, NT, D], BF16)
    K_sb = [p.sb("k", [128, T], BF16) for _ in range(2)]
    Q_sb = [p.sb("q", [128, T], BF16) for _ in range(2)]
    A_sb = p.sb("a", [128, T], BF16)
    AT_sb = p.sb("at", [128, T], BF16)
    O_sb = [p.sb("o", [128, T], BF16) for _ in range(2)]
    ident_sb = p.sb("id", [128, 128], BF16)
    mask_sb = p.sb("mk", [128, 128], F32 if causal else BF16)
    n_col = p.sb("ncol", [128, 1], F32)
    r_col = p.sb("rcol", [128, 1], F32)
    s_ps = p.ps("sps", [128, T], F32)
    at_ps = [p.ps("atps", [128, 128], BF16) for _ in range(2)]
    o_ps = p.ps("ops", [128, 128], F32)

    s_v = p.sem("attv", global_=True)
    s_S, s_msk, s_exp, s_anorm = (p.sem(x) for x in ("S", "msk", "exp", "an"))
    s_tp, s_tpev, s_av, s_oev = (p.sem(x) for x in ("tp", "tpev", "av", "oev"))
    s_rr = p.sem("rr")
    dkq = p.gdsem("attkq", 2)
    dost = p.gdsem("attost", 2)
    v0, S0, msk0, exp0, an0 = s_v.v, s_S.v, s_msk.v, s_exp.v, s_anorm.v
    rr0 = s_rr.v
    tp0, tpev0, av0, oev0 = s_tp.v, s_tpev.v, s_av.v, s_oev.v

    def cum_tp(g):
        h, qt = divmod(g, NT)
        return h * (NT * (NT + 1) // 2) + qt * (qt + 1) // 2

    with nc.Block() as blk:
        @blk.sync
        def _(s):
            s.dma_start(ident_sb[:], ident).then_inc(s_v.h, 16)
            s.dma_start(mask_sb[:], mask_in).then_inc(s_v.h, 16)
            for tc in range(NT):
                s.dma_start(V_sb[:, tc, :], V[tc * 128:(tc + 1) * 128, :]).then_inc(s_v.h, 16)
            for h in range(NH):
                if h >= 2:
                    s.wait_ge(s_S.h, S0 + NT * (h - 1))
                s.dma_start(K_sb[h % 2][:], K[h * 128:(h + 1) * 128, :]).then_inc(dkq.inc(h), 16)
                s.dma_start(Q_sb[h % 2][:], Q[h * 128:(h + 1) * 128, :]).then_inc(dkq.inc(h), 16)
        s_v.v += 16 * (NT + 2)

        @blk.tensor
        def _(t):
            t.wait_ge(s_v.h, v0 + 16 * (NT + 2))
            for h in range(NH):
                t.wait_ge(dkq.h(h), dkq.tgt(h, 32))
                for qt in range(NT):
                    g = h * NT + qt
                    W = (qt + 1) * 128 if causal else T
                    if g > 0:
                        t.wait_ge(s_exp.h, exp0 + g)
                    nfull = (W + 511) // 512
                    for n in range(nfull):
                        lo, hi = n * 512, min(W, (n + 1) * 512)
                        ins = t.matmul(s_ps[:, lo:hi],
                                       Q_sb[h % 2][:, qt * 128:(qt + 1) * 128],
                                       K_sb[h % 2][:, lo:hi],
                                       start=True, stop=True)
                        if hi == W:
                            ins.then_inc(s_S.h, 1)
                    t.wait_ge(s_anorm.h, an0 + g + 1)
                    base = tp0 + cum_tp(g)
                    for kc in range(qt + 1):
                        if kc >= 2:
                            t.wait_ge(s_tpev.h, tpev0 + cum_tp(g) + kc - 1)
                        t.transpose(at_ps[kc % 2][:], A_sb[:, kc * 128:(kc + 1) * 128],
                                    ident_sb[:]).then_inc(s_tp.h, 1)
                    t.wait_ge(s_tpev.h, tpev0 + cum_tp(g) + qt + 1)
                    t.wait_ge(s_oev.h, oev0 + g)
                    for kc in range(qt + 1):
                        ins = t.matmul(o_ps[:], V_sb[:, kc, h * 128:(h + 1) * 128],
                                       AT_sb[:, kc * 128:(kc + 1) * 128],
                                       start=(kc == 0), stop=(kc == qt))
                        if kc == qt:
                            ins.then_inc(s_av.h, 1)
        s_S.v += NH * NT
        s_tp.v += NH * (NT * (NT + 1) // 2)
        s_av.v += NH * NT

        @blk.scalar
        def _(sc):
            for h in range(NH):
                for qt in range(NT):
                    g = h * NT + qt
                    W = (qt + 1) * 128 if causal else T
                    if causal:
                        sc.wait_ge(s_msk.h, msk0 + g + 1)
                    else:
                        sc.wait_ge(s_S.h, S0 + g + 1)
                    if g > 0:
                        sc.wait_ge(s_anorm.h, an0 + g)
                    sc.activation(A_sb[:, 0:W], s_ps[:, 0:W], AF.Exp,
                                  scale=SM_SCALE, accum_out=n_col[:]).then_inc(s_exp.h, 1)
        s_exp.v += NH * NT

        @blk.vector
        def _(v):
            for h in range(NH):
                for qt in range(NT):
                    g = h * NT + qt
                    W = (qt + 1) * 128 if causal else T
                    if causal:
                        v.wait_ge(s_S.h, S0 + g + 1)
                        v.tensor_add(s_ps[:, qt * 128:(qt + 1) * 128],
                                     s_ps[:, qt * 128:(qt + 1) * 128],
                                     mask_sb[:]).then_inc(s_msk.h, 1)
                    v.wait_ge(s_exp.h, exp0 + g + 1)
                    v.reciprocal(r_col[:], n_col[:]).then_inc(s_rr.h, 1)
                    v.wait_ge(s_rr.h, rr0 + g + 1)
                    ins = v.tensor_scalar_mul(A_sb[:, 0:W], A_sb[:, 0:W], r_col[:])
                    if not causal:
                        ins = v.tensor_mul(A_sb[:, qt * 128:(qt + 1) * 128],
                                           A_sb[:, qt * 128:(qt + 1) * 128],
                                           mask_sb[:])
                    ins.then_inc(s_anorm.h, 1)
                    for kc in range(qt + 1):
                        v.wait_ge(s_tp.h, tp0 + cum_tp(g) + kc + 1)
                        v.tensor_copy(AT_sb[:, kc * 128:(kc + 1) * 128],
                                      at_ps[kc % 2][:]).then_inc(s_tpev.h, 1)
                    v.wait_ge(s_av.h, av0 + g + 1)
                    if qt == 0 and h >= 2:
                        v.wait_ge(dost.h(h), dost.tgt(h - 2))
                    v.tensor_copy(O_sb[h % 2][:, qt * 128:(qt + 1) * 128],
                                  o_ps[:]).then_inc(s_oev.h, 1)
        if causal:
            s_msk.v += NH * NT
        s_rr.v += NH * NT
        s_anorm.v += NH * NT
        s_tpev.v += NH * (NT * (NT + 1) // 2)
        s_oev.v += NH * NT

        @blk.gpsimd
        def _(g):
            for h in range(NH):
                g.wait_ge(s_oev.h, oev0 + NT * (h + 1))
                g.dma_start(O[h * 128:(h + 1) * 128, :], O_sb[h % 2][:]).then_inc(dost.inc(h), 16)
            g.wait_ge(dost.h(NH - 2), dost.tgt(NH - 2))
            g.wait_ge(dost.h(NH - 1), dost.tgt(NH - 1))
    p.pop()



def phase_proj_ln(p, layer, on="O_fm", wname="L0_attn_Wo", bname=None,
                  xres_name="x_in", xout_name="x1", w_transposed=False):
    nc = p.nc
    O = p.inp(on, [D, T], BF16)
    Wo = p.inp(wname, [D, D], F32)
    bo = p.inp(bname, [1, D], F32) if bname else None
    xres = p.inp(xres_name, [T, D], F32)
    xout = p.outp(xout_name, [T, D], F32)
    ident = p.inp("c_ident", [128, 128], BF16)
    ones1 = p.inp("c_ones1", [1, 128], BF16)

    p.push()
    O_sb = p.sb("osb", [128, ND, T], BF16)
    W_sb = p.sb("wsb", [128, ND, D], BF16)
    ident_sb = p.sb("id", [128, 128], BF16)
    ones1_sb = p.sb("on", [1, 128], BF16)
    bo_sb = p.sb("bo", [1, D], BF16) if bname else None
    xr = [p.sb("xr", [128, D], F32) for _ in range(2)]
    xo = [p.sb("xo", [128, D], F32) for _ in range(2)]
    ssum = p.sb("ssum", [128, 1], F32)
    ssq = p.sb("ssq", [128, 1], F32)
    mean = p.sb("mean", [128, 1], F32)
    var = p.sb("var", [128, 1], F32)
    sd = p.sb("sd", [128, 1], F32)
    rstd = p.sb("rstd", [128, 1], F32)
    ps = p.ps("pj", [128, D], F32)

    s_c = p.sem("plc", global_=True)
    s_w = p.sem("plw", global_=True)
    c0, w0 = s_c.v, s_w.v

    with nc.Block() as blk:
        @blk.sync
        def _(s):
            s.dma_start(ident_sb[:], ident).then_inc(s_c.h, 16)
            s.dma_start(ones1_sb[:], ones1).then_inc(s_c.h, 16)
            for dc in range(ND):
                s.dma_start(O_sb[:, dc, :], O[dc * 128:(dc + 1) * 128, :]).then_inc(s_c.h, 16)
            s.wait_ge(s_c.h, c0 + 16 * (ND + 2))
        s_c.v += 16 * (ND + 2)

        @blk.gpsimd
        def _(g):
            if bname:
                g.dma_start(bo_sb[:], bo).then_inc(s_w.h, 16)
                g.wait_ge(s_w.h, w0 + 16)
        if bname:
            s_w.v += 16

    if not w_transposed:
        w0b = s_w.v
        with nc.Block() as blk:
            @blk.gpsimd
            def _(g):
                for dc in range(ND):
                    g.dma_start(W_sb[:, dc, :], Wo[dc * 128:(dc + 1) * 128, :]).then_inc(s_w.h, 16)
                g.wait_ge(s_w.h, w0b + 16 * ND)
            s_w.v += 16 * ND
    else:
        wtmp = [p.sb("wt", [128, D], BF16) for _ in range(2)]
        wt_ps = [p.ps("wtp", [128, 128], BF16) for _ in range(2)]
        s_tt, s_te = p.sem("tt"), p.sem("te")
        dwl = p.gdsem("plwl", 2)
        tt0, te0 = s_tt.v, s_te.v
        with nc.Block() as blk:
            @blk.gpsimd
            def _(g):
                for ec in range(ND):
                    if ec >= 2:
                        g.wait_ge(s_te.h, te0 + (ec - 1) * ND)
                    g.dma_start(wtmp[ec % 2][:], Wo[ec * 128:(ec + 1) * 128, :]).then_inc(dwl.inc(ec), 16)

            @blk.tensor
            def _(t):
                for ec in range(ND):
                    t.wait_ge(dwl.h(ec), dwl.tgt(ec))
                    for dc in range(ND):
                        i = ec * ND + dc
                        if i >= 2:
                            t.wait_ge(s_te.h, te0 + i - 1)
                        t.transpose(wt_ps[i % 2][:], wtmp[ec % 2][:, dc * 128:(dc + 1) * 128],
                                    ident_sb[:]).then_inc(s_tt.h, 1)
            s_tt.v += ND * ND

            @blk.vector
            def _(v):
                for ec in range(ND):
                    for dc in range(ND):
                        i = ec * ND + dc
                        v.wait_ge(s_tt.h, tt0 + i + 1)
                        v.tensor_copy(W_sb[:, dc, ec * 128:(ec + 1) * 128],
                                      wt_ps[i % 2][:]).then_inc(s_te.h, 1)
            s_te.v += ND * ND

    _emit_proj_res_ln(p, O_sb, W_sb,
                      (ones1_sb, bo_sb) if bname else None, xres, xout, ps,
                      xr, xo, ssum, ssq, mean, var, sd, rstd)
    p.pop()
    return


def _emit_proj_res_ln(p, lhsT_sb, W_sb, bias_pair, xres, xout, ps,
                      xr, xo, ssum, ssq, mean, var, sd, rstd):
    nc = p.nc
    s_mm, s_r1, s_sq, s_nm = (p.sem(x) for x in ("mm", "r1", "sq", "nm"))
    s_v2, s_sd, s_rc = p.sem("v2"), p.sem("sd"), p.sem("rc")
    dx = p.gdsem("lnx", 2)
    dst = p.gdsem("lnst", 2)
    mm0, r10, sq0, nm0 = (s.v for s in (s_mm, s_r1, s_sq, s_nm))
    v20, sd0, rc0 = s_v2.v, s_sd.v, s_rc.v
    with nc.Block() as blk:
        @blk.sync
        def _(s):
            for qc in range(NT):
                if qc >= 2:
                    s.wait_ge(s_nm.h, nm0 + qc - 1)
                s.dma_start(xr[qc % 2][:], xres[qc * 128:(qc + 1) * 128, :]).then_inc(dx.inc(qc), 16)

        @blk.tensor
        def _(t):
            for qc in range(NT):
                if qc > 0:
                    t.wait_ge(s_sq.h, sq0 + qc)
                last = None
                for dc in range(ND):
                    for n in range(4):
                        last = t.matmul(ps[:, n * 512:(n + 1) * 512],
                                        lhsT_sb[:, dc, qc * 128:(qc + 1) * 128],
                                        W_sb[:, dc, n * 512:(n + 1) * 512],
                                        start=(dc == 0),
                                        stop=(False if bias_pair else dc == ND - 1))
                if bias_pair:
                    ones1_sb, bo_sb = bias_pair
                    for n in range(4):
                        last = t.matmul(ps[:, n * 512:(n + 1) * 512], ones1_sb[:],
                                        bo_sb[:, n * 512:(n + 1) * 512],
                                        start=False, stop=True)
                last.then_inc(s_mm.h, 1)
        s_mm.v += NT

        @blk.vector
        def _(v):
            for qc in range(NT):
                v.wait_ge(s_mm.h, mm0 + qc + 1)
                v.wait_ge(dx.h(qc), dx.tgt(qc))
                v.scalar_tensor_tensor(xr[qc % 2][:], ps[:], 1.0, xr[qc % 2][:],
                                       ALU.mult, ALU.add,
                                       accum_out=ssum[:]).then_inc(s_r1.h, 1)
                v.wait_ge(s_sq.h, sq0 + qc + 1)
                v.tensor_scalar_mul(mean[:], ssum[:], INV_D)
                v.tensor_scalar_mul(var[:], ssq[:], INV_D)
                v.tensor_mul(rstd[:], mean[:], mean[:])
                v.tensor_sub(var[:], var[:], rstd[:])
                v.tensor_scalar_add(var[:], var[:], LN_EPS).then_inc(s_v2.h, 1)
                v.wait_ge(s_sd.h, sd0 + qc + 1)
                if qc >= 2:
                    v.wait_ge(dst.h(qc), dst.tgt(qc - 2))
                v.reciprocal(rstd[:], sd[:]).then_inc(s_rc.h, 1)
                v.wait_ge(s_rc.h, rc0 + qc + 1)
                v.tensor_scalar(xo[qc % 2][:], xr[qc % 2][:], mean[:], rstd[:],
                                ALU.subtract, ALU.mult).then_inc(s_nm.h, 1)
        s_r1.v += NT
        s_v2.v += NT
        s_rc.v += NT
        s_nm.v += NT

        @blk.scalar
        def _(sc):
            for qc in range(NT):
                sc.wait_ge(s_r1.h, r10 + qc + 1)
                sc.activation(ps[:], xr[qc % 2][:], AF.Square,
                              accum_out=ssq[:]).then_inc(s_sq.h, 1)
                sc.wait_ge(s_v2.h, v20 + qc + 1)
                sc.activation(sd[:], var[:], AF.Sqrt).then_inc(s_sd.h, 1)
        s_sq.v += NT
        s_sd.v += NT

        @blk.gpsimd
        def _(g):
            for qc in range(NT):
                g.wait_ge(s_nm.h, nm0 + qc + 1)
                g.dma_start(xout[qc * 128:(qc + 1) * 128, :], xo[qc % 2][:]).then_inc(dst.inc(qc), 16)
            g.wait_ge(dst.h(NT - 2), dst.tgt(NT - 2))
            g.wait_ge(dst.h(NT - 1), dst.tgt(NT - 1))



def phase_ffn_prep(p):
    nc = p.nc
    wup_q = p.inp("L0_ffn_w_up_q", [HF, D], I32)
    wup_s = p.inp("L0_ffn_w_up_scale", [HF, 32], F32)
    wdn_q = p.inp("L0_ffn_w_down_q", [D, HF], I32)
    wdn_s = p.inp("L0_ffn_w_down_scale", [D, HF // 64], F32)
    ident = p.inp("c_ident", [128, 128], BF16)
    WupT = p.outp("WupT", [NHC, 128, D], BF16)
    WdnT = p.outp("WdnT", [NHC, 128, D], BF16)

    p.push()
    ident_sb = p.sb("id", [128, 128], BF16)
    qrow = [p.sb("qr", [128, D], I32) for _ in range(4)]
    dqr = [p.sb("dqr", [128, D], BF16) for _ in range(4)]
    wus_sb = [p.sb("wus", [128, 32], F32) for _ in range(2)]
    wds_sb = p.sb("wds", [128, ND, HF // 64], F32)
    upstage = [p.sb("ups", [128, D], BF16) for _ in range(2)]
    dnstage = [p.sb("dns", [128, D], BF16) for _ in range(2)]
    tp_ps = [p.ps("tp", [128, 128], BF16) for _ in range(4)]

    s_c = p.sem("d1c", global_=True)
    s_dq, s_tp, s_ev = (p.sem(x) for x in ("dq", "tp", "ev"))
    dld = p.gdsem("d1ld", 4)
    dsc = p.gdsem("d1sc", 2)
    dstg = p.gdsem("d1stg", 2)
    c0, dq0, tp0, ev0 = (x.v for x in (s_c, s_dq, s_tp, s_ev))

    wdn_qv = wdn_q.rearrange("(dc p) h -> p dc h", p=128)

    with nc.Block() as blk:
        @blk.sync
        def _(s):
            s.dma_start(ident_sb[:], ident).then_inc(s_c.h, 16)
            for dc in range(ND):
                s.dma_start(wds_sb[:, dc, :],
                            wdn_s[dc * 128:(dc + 1) * 128, :]).then_inc(s_c.h, 16)
            s.wait_ge(s_c.h, c0 + 16 * (ND + 1))
        s_c.v += 16 * (ND + 1)

    with nc.Block() as blk:
        @blk.sync
        def _(s):
            for hc in range(NHC):
                for j in range(2):
                    r = hc * 2 + j
                    if r >= 4:
                        s.wait_ge(s_dq.h, dq0 + r - 3)
                    if j == 0:
                        src = wup_q[hc * 128:(hc + 1) * 128, :]
                    else:
                        src = wdn_qv[:, :, hc * 128:(hc + 1) * 128]
                    s.dma_start(qrow[r % 4][:], src).then_inc(dld.inc(r), 16)

        @blk.gpsimd
        def _(g):
            def stores(hcs):
                g.wait_ge(s_ev.h, ev0 + hcs * 32 + ND)
                g.dma_start(WupT[hcs], upstage[hcs % 2][:]).then_inc(dstg.inc(hcs), 16)
                g.wait_ge(s_ev.h, ev0 + hcs * 32 + 32)
                g.dma_start(WdnT[hcs], dnstage[hcs % 2][:]).then_inc(dstg.inc(hcs), 16)
            for hc in range(NHC):
                if hc >= 2:
                    g.wait_ge(s_dq.h, dq0 + 2 * hc - 3)
                g.dma_start(wus_sb[hc % 2][:],
                            wup_s[hc * 128:(hc + 1) * 128, :]).then_inc(dsc.inc(hc), 16)
                if hc >= 1:
                    stores(hc - 1)
            stores(NHC - 1)
            g.wait_ge(dstg.h(NHC - 2), dstg.tgt(NHC - 2, 32))
            g.wait_ge(dstg.h(NHC - 1), dstg.tgt(NHC - 1, 32))

        @blk.vector
        def _(v):
            for hc in range(NHC):
                for j in range(2):
                    r = hc * 2 + j
                    v.wait_ge(dld.h(r), dld.tgt(r))
                    if r >= 4:
                        v.wait_ge(s_tp.h, tp0 + (r - 3) * ND)
                    q = qrow[r % 4]
                    d = dqr[r % 4]
                    if j == 0:
                        v.wait_ge(dsc.h(hc), dsc.tgt(hc))
                        for b in range(32):
                            ins = v.tensor_scalar(d[:, b * 64:(b + 1) * 64],
                                                  q[:, b * 64:(b + 1) * 64], -7.5,
                                                  wus_sb[hc % 2][:, b:b + 1],
                                                  ALU.add, ALU.mult)
                        ins.then_inc(s_dq.h, 1)
                    else:
                        for dc in range(ND):
                            for hh in range(2):
                                ins = v.tensor_scalar(
                                    d[:, dc * 128 + hh * 64:dc * 128 + (hh + 1) * 64],
                                    q[:, dc * 128 + hh * 64:dc * 128 + (hh + 1) * 64],
                                    -7.5, wds_sb[:, dc, 2 * hc + hh:2 * hc + hh + 1],
                                    ALU.add, ALU.mult)
                        ins.then_inc(s_dq.h, 1)
                    if r >= 1:
                        _d1_evacs(v, r - 1, s_tp, tp0, s_ev, tp_ps, upstage, dnstage)
            _d1_evacs(v, 2 * NHC - 1, s_tp, tp0, s_ev, tp_ps, upstage, dnstage)
        s_dq.v += 2 * NHC
        s_ev.v += 2 * NHC * ND

        @blk.tensor
        def _(t):
            for hc in range(NHC):
                if hc >= 2:
                    t.wait_ge(dstg.h(hc), dstg.tgt(hc - 2, 32))
                for j in range(2):
                    r = hc * 2 + j
                    t.wait_ge(s_dq.h, dq0 + r + 1)
                    for dc in range(ND):
                        i = r * ND + dc
                        if i >= 4:
                            t.wait_ge(s_ev.h, ev0 + i - 3)
                        t.transpose(tp_ps[i % 4][:],
                                    dqr[r % 4][:, dc * 128:(dc + 1) * 128],
                                    ident_sb[:]).then_inc(s_tp.h, 1)
        s_tp.v += 2 * NHC * ND
    p.pop()
    return


def _d1_evacs(v, r, s_tp, tp0, s_ev, tp_ps, upstage, dnstage):
    hc, j = divmod(r, 2)
    stage = upstage[hc % 2] if j == 0 else dnstage[hc % 2]
    for dc in range(ND):
        i = r * ND + dc
        v.wait_ge(s_tp.h, tp0 + i + 1)
        v.tensor_copy(stage[:, dc * 128:(dc + 1) * 128],
                      tp_ps[i % 4][:]).then_inc(s_ev.h, 1)





def phase_ffn_main(p):
    nc = p.nc
    x1 = p.inp("x1", [T, D], F32)
    WupT = p.inp("WupT", [NHC, 128, D], BF16)
    WdnT = p.inp("WdnT", [NHC, 128, D], BF16)
    bup = p.inp("bup_cols", [128, NHC], F32)
    Aup = p.inp("L0_ffn_w_up_lora_a", [D, RK], F32)
    Bup = p.inp("L0_ffn_w_up_lora_b", [RK, HF], F32)
    Adn = p.inp("L0_ffn_w_down_lora_a", [HF, RK], F32)
    Bdn = p.inp("bdnl_ext", [RK + 1, D], F32)
    onesrow = p.inp("c_onesrow", [1, 512], BF16)
    ident = p.inp("c_ident", [128, 128], BF16)
    ones1 = p.inp("c_ones1", [1, 128], BF16)
    x2out = p.outp("x2", [T, D], F32)

    TB = 512
    NTB = T // TB
    NTC = TB // 128

    p.push()
    ident_sb = p.sb("id", [128, 128], BF16)
    ones1_sb = p.sb("on", [1, 128], BF16)
    x_fm = p.sb("xfm", [128, ND, T], BF16)
    x2_fm = p.sb("x2", [128, NHC, TB], BF16)
    wup_sb = [p.sb("wu", [128, D], BF16) for _ in range(2)]
    wdn_sb = [p.sb("wd", [128, 512], BF16) for _ in range(4)]
    aup_sb = p.sb("aup", [128, ND * RK], BF16)
    bupl_sb = p.sb("bupl", [RK, HF], BF16)
    adn_sb = p.sb("adn", [128, NHC * RK], BF16)
    bdnl_sb = p.sb("bdnl", [RK + 1, D], BF16)
    bup_sb = p.sb("bup", [128, NHC], F32)
    xaup_sb = p.sb("xaup", [RK, T], BF16)
    xadn_sb = p.sb("xadn", [RK + 1, TB], BF16)
    xr_sb = [p.sb("xr", [128, D], F32) for _ in range(4)]
    ssum4 = p.sb("ssum4", [128, 4], F32)
    ssq = p.sb("ssq", [128, 1], F32)
    ssum = p.sb("ssum", [128, 1], F32)
    mean = p.sb("mean", [128, 1], F32)
    var = p.sb("var", [128, 1], F32)
    sd = p.sb("sd", [128, 1], F32)
    rstd = p.sb("rstd", [128, 1], F32)

    s_c = p.sem("d2c", global_=True)
    s_cg = p.sem("d2cg", global_=True)
    with nc.Block() as blk:
        @blk.sync
        def _(s):
            s.dma_start(ident_sb[:], ident).then_inc(s_c.h, 16)
            s.dma_start(ones1_sb[:], ones1).then_inc(s_c.h, 16)
            s.dma_start(bup_sb[:], bup).then_inc(s_c.h, 16)
            s.wait_ge(s_c.h, s_c.v + 48)
        s_c.v += 48

        @blk.gpsimd
        def _(g):
            g.dma_start(xadn_sb[RK:RK + 1, :], onesrow[0:1, 0:TB]).then_inc(s_cg.h, 16)
            g.dma_start(bupl_sb[:], Bup).then_inc(s_cg.h, 16)
            g.dma_start(bdnl_sb[:], Bdn).then_inc(s_cg.h, 16)
            for dc in range(ND):
                g.dma_start(aup_sb[:, dc * RK:(dc + 1) * RK],
                            Aup[dc * 128:(dc + 1) * 128, :]).then_inc(s_cg.h, 16)
            for hc in range(NHC):
                g.dma_start(adn_sb[:, hc * RK:(hc + 1) * RK],
                            Adn[hc * 128:(hc + 1) * 128, :]).then_inc(s_cg.h, 16)
            g.wait_ge(s_cg.h, s_cg.v + 16 * (3 + ND + NHC))
        s_cg.v += 16 * (3 + ND + NHC)

    emit_load_xfm(p, x1, x_fm, ident_sb, bufs=wup_sb)

    p.push()
    xaup_ps = p.ps("xaupp", [RK, T], F32)
    s_u, s_ue = p.sem("u"), p.sem("ue")
    with nc.Block() as blk:
        @blk.tensor
        def _(t):
            last = None
            for dc in range(ND):
                for n in range(4):
                    last = t.matmul(xaup_ps[:, n * 512:(n + 1) * 512],
                                    aup_sb[:, dc * RK:(dc + 1) * RK],
                                    x_fm[:, dc, n * 512:(n + 1) * 512],
                                    start=(dc == 0), stop=(dc == ND - 1))
            last.then_inc(s_u.h, 1)
        s_u.v += 1

        @blk.vector
        def _(v):
            v.wait_ge(s_u.h, s_u.v)
            v.tensor_copy(xaup_sb[:], xaup_ps[:]).then_inc(s_ue.h, 1)
        s_ue.v += 1
    p.pop()

    up_ps = [p.ps("up", [128, TB], F32) for _ in range(2)]
    xa_ps = p.ps("xa", [RK, TB], F32)
    dn_ps = [p.ps("dn", [128, 512], F32) for _ in range(4)]

    s_up, s_rel, s_xa, s_xae, s_dn, s_dnw = (p.sem(x) for x in
        ("up", "rel", "xa", "xae", "dn", "dnw"))
    dwu = p.gdsem("d2wu", 2)
    dwd = p.gdsem("d2wd", 4)
    dxl = p.gdsem("d2xl", 4)
    dxs = p.gdsem("d2xs", 4)
    s_r1, s_sq2, s_v2, s_sd2, s_rc, s_nm = (p.sem(x) for x in
        ("r1", "sq2", "v2", "sd2", "rc", "nm"))

    for tb in range(NTB):
        up0, rel0, xa0 = s_up.v, s_rel.v, s_xa.v
        with nc.Block() as blk:
            @blk.sync
            def _(s):
                for hc in range(NHC):
                    k = tb * NHC + hc
                    if k >= 2:
                        s.wait_ge(s_up.h, k - 1)
                    s.dma_start(wup_sb[hc % 2][:], WupT[hc]).then_inc(dwu.inc(k), 16)

            @blk.tensor
            def _(t):
                t.wait_ge(s_ue.h, s_ue.v)
                for hc in range(NHC):
                    k = tb * NHC + hc
                    t.wait_ge(dwu.h(k), dwu.tgt(k))
                    if hc >= 2:
                        t.wait_ge(s_rel.h, rel0 + hc - 1)
                    last = None
                    for dc in range(ND):
                        last = t.matmul(up_ps[hc % 2][:],
                                        wup_sb[hc % 2][:, dc * 128:(dc + 1) * 128],
                                        x_fm[:, dc, tb * TB:(tb + 1) * TB],
                                        start=(dc == 0), stop=False)
                    last = t.matmul(up_ps[hc % 2][:],
                                    bupl_sb[:, hc * 128:(hc + 1) * 128],
                                    xaup_sb[:, tb * TB:(tb + 1) * TB],
                                    start=False, stop=True)
                    last.then_inc(s_up.h, 1)
                    if hc >= 1:
                        t.wait_ge(s_rel.h, rel0 + hc)
                        t.matmul(xa_ps[:], adn_sb[:, (hc - 1) * RK:hc * RK],
                                 x2_fm[:, hc - 1, :],
                                 start=(hc == 1), stop=False)
                t.wait_ge(s_rel.h, rel0 + NHC)
                t.matmul(xa_ps[:], adn_sb[:, (NHC - 1) * RK:NHC * RK],
                         x2_fm[:, NHC - 1, :],
                         start=False, stop=True).then_inc(s_xa.h, 1)
            s_up.v += NHC
            s_xa.v += 1

            @blk.scalar
            def _(sc):
                for hc in range(NHC):
                    sc.wait_ge(s_up.h, up0 + hc + 1)
                    sc.activation(x2_fm[:, hc, :], up_ps[hc % 2][:], AF.Relu,
                                  bias=bup_sb[:, hc:hc + 1]).then_inc(s_rel.h, 1)
            s_rel.v += NHC

            @blk.vector
            def _(v):
                v.wait_ge(s_xa.h, xa0 + 1)
                v.tensor_copy(xadn_sb[0:RK, :], xa_ps[:]).then_inc(s_xae.h, 1)
            s_xae.v += 1

        dn0, dnw0 = s_dn.v, s_dnw.v
        r10, sq20, v20, sd20, rc0, nm0 = (s.v for s in
            (s_r1, s_sq2, s_v2, s_sd2, s_rc, s_nm))
        with nc.Block() as blk:
            @blk.sync
            def _(s):
                for tc in range(NTC):
                    qcg = tb * NTC + tc
                    if tb > 0:
                        s.wait_ge(dxs.h(qcg), dxs.tgt(qcg - NTC))
                    s.dma_start(xr_sb[tc][:],
                                x1[qcg * 128:(qcg + 1) * 128, :]
                                ).then_inc(dxl.inc(qcg), 16)
                for dh in range(4):
                    for hc in range(NHC):
                        i = dh * NHC + hc
                        k = tb * 4 * NHC + i
                        if k % 2 == 1:
                            continue
                        if k >= 4:
                            s.wait_ge(s_dnw.h, k - 3)
                        s.dma_start(wdn_sb[k % 4][:],
                                    WdnT[hc][:, dh * 512:(dh + 1) * 512]).then_inc(dwd.inc(k), 16)

            @blk.scalar
            def _(sc_e):
                for dh in range(4):
                    for hc in range(NHC):
                        i = dh * NHC + hc
                        k = tb * 4 * NHC + i
                        if k % 2 == 0:
                            continue
                        if k >= 4:
                            sc_e.wait_ge(s_dnw.h, k - 3)
                        sc_e.dma_start(wdn_sb[k % 4][:],
                                       WdnT[hc][:, dh * 512:(dh + 1) * 512]).then_inc(dwd.inc(k), 16)

            @blk.tensor
            def _(t):
                for dh in range(4):
                    if dh > 0:
                        t.wait_ge(s_r1.h, r10 + dh * NTC)
                    for hc in range(NHC):
                        i = dh * NHC + hc
                        k = tb * 4 * NHC + i
                        t.wait_ge(dwd.h(k), dwd.tgt(k))
                        for tc in range(NTC):
                            ins = t.matmul(dn_ps[tc][:],
                                           x2_fm[:, hc, tc * 128:(tc + 1) * 128],
                                           wdn_sb[k % 4][:],
                                           start=(hc == 0), stop=False)
                            if tc == NTC - 1:
                                ins.then_inc(s_dnw.h, 1)
                    t.wait_ge(s_xae.h, s_xae.v)
                    for tc in range(NTC):
                        t.matmul(dn_ps[tc][:], xadn_sb[:, tc * 128:(tc + 1) * 128],
                                 bdnl_sb[:, dh * 512:(dh + 1) * 512],
                                 start=False, stop=True).then_inc(s_dn.h, 1)
            s_dn.v += 4 * NTC
            s_dnw.v += 4 * NHC

            @blk.vector
            def _(v):
                for dh in range(4):
                    for tc in range(NTC):
                        v.wait_ge(s_dn.h, dn0 + dh * NTC + tc + 1)
                        v.wait_ge(dxl.h(tb * NTC + tc), dxl.tgt(tb * NTC + tc))
                        v.scalar_tensor_tensor(
                            xr_sb[tc][:, dh * 512:(dh + 1) * 512],
                            dn_ps[tc][:], 1.0,
                            xr_sb[tc][:, dh * 512:(dh + 1) * 512],
                            ALU.mult, ALU.add,
                            accum_out=ssum4[:, dh:dh + 1]).then_inc(s_r1.h, 1)
            s_r1.v += 16

        with nc.Block() as blk:
            @blk.vector
            def _(v):
                for tc in range(NTC):
                    v.reduce_sum(ssum[:], xr_sb[tc][:], axis=AX.X).then_inc(s_r1.h, 1)
                    v.wait_ge(s_sq2.h, sq20 + tc + 1)
                    v.tensor_scalar_mul(mean[:], ssum[:], INV_D)
                    v.tensor_scalar_mul(var[:], ssq[:], INV_D)
                    v.tensor_mul(rstd[:], mean[:], mean[:])
                    v.tensor_sub(var[:], var[:], rstd[:])
                    v.tensor_scalar_add(var[:], var[:], LN_EPS).then_inc(s_v2.h, 1)
                    v.wait_ge(s_sd2.h, sd20 + tc + 1)
                    v.reciprocal(rstd[:], sd[:]).then_inc(s_rc.h, 1)
                    v.wait_ge(s_rc.h, rc0 + tc + 1)
                    v.tensor_scalar(xr_sb[tc][:], xr_sb[tc][:], mean[:], rstd[:],
                                    ALU.subtract, ALU.mult).then_inc(s_nm.h, 1)
            s_r1.v += NTC
            s_v2.v += NTC
            s_rc.v += NTC
            s_nm.v += NTC

            @blk.scalar
            def _(sc):
                for tc in range(NTC):
                    sc.wait_ge(s_r1.h, r10 + 16 + tc + 1)
                    sc.activation(x2_fm[:, 0:4, :], xr_sb[tc][:], AF.Square,
                                  accum_out=ssq[:]).then_inc(s_sq2.h, 1)
                    sc.wait_ge(s_v2.h, v20 + tc + 1)
                    sc.activation(sd[:], var[:], AF.Sqrt).then_inc(s_sd2.h, 1)
            s_sq2.v += NTC
            s_sd2.v += NTC

            @blk.gpsimd
            def _(g):
                for tc in range(NTC):
                    g.wait_ge(s_nm.h, nm0 + tc + 1)
                    qc = tb * NTC + tc
                    g.dma_start(x2out[qc * 128:(qc + 1) * 128, :],
                                xr_sb[tc][:]).then_inc(dxs.inc(qc), 16)
                if tb == NTB - 1:
                    for tc in range(NTC):
                        g.wait_ge(dxs.h(tc), dxs.tgt((NTB - 1) * NTC + tc))
    p.pop()
    return



def phase_qkv1(p):
    nc = p.nc
    x2 = p.inp("x2", [T, D], F32)
    Wqkv = p.inp("L1_attn_W_qkv", [3 * D, D], F32)
    ident = p.inp("c_ident", [128, 128], BF16)
    Qo = p.outp("Q1_fm", [D, T], BF16)
    Ko = p.outp("K1_fm", [D, T], BF16)
    Vo = p.outp("V1_tm", [T, D], BF16)

    p.push()
    ident_sb = p.sb("id", [128, 128], BF16)
    x_fm = p.sb("xfm", [128, ND, T], BF16)
    wtmp = [p.sb("wt", [128, D], BF16) for _ in range(2)]
    wt_sb = p.sb("wts", [128, ND, 128], BF16)
    WTv_sb = p.sb("wtv", [128, ND, D], BF16)
    ev_sb = [p.sb("ev", [128, T], BF16) for _ in range(2)]
    wt_ps = [p.ps("wtp", [128, 128], BF16) for _ in range(2)]
    pj_ps = p.ps("pj", [128, T], F32)

    s_c = p.sem("ec", global_=True)
    with nc.Block() as blk:
        @blk.sync
        def _(s):
            s.dma_start(ident_sb[:], ident).then_inc(s_c.h, 16)
            s.wait_ge(s_c.h, s_c.v + 16)
        s_c.v += 16

    emit_load_xfm(p, x2, x_fm, ident_sb)

    s_tt, s_te = p.sem("tt"), p.sem("te")
    dwv = p.gdsem("ewv", 2)
    tt0, te0 = s_tt.v, s_te.v
    with nc.Block() as blk:
        @blk.gpsimd
        def _(g):
            for i, ec in enumerate(range(2 * ND, 3 * ND)):
                if i >= 2:
                    g.wait_ge(s_te.h, te0 + (i - 1) * ND)
                g.dma_start(wtmp[i % 2][:], Wqkv[ec * 128:(ec + 1) * 128, :]).then_inc(dwv.inc(i), 16)

        @blk.tensor
        def _(t):
            for i in range(ND):
                t.wait_ge(dwv.h(i), dwv.tgt(i))
                for dc in range(ND):
                    j = i * ND + dc
                    if j >= 2:
                        t.wait_ge(s_te.h, te0 + j - 1)
                    t.transpose(wt_ps[j % 2][:], wtmp[i % 2][:, dc * 128:(dc + 1) * 128],
                                ident_sb[:]).then_inc(s_tt.h, 1)
        s_tt.v += ND * ND

        @blk.vector
        def _(v):
            for i in range(ND):
                for dc in range(ND):
                    j = i * ND + dc
                    v.wait_ge(s_tt.h, tt0 + j + 1)
                    v.tensor_copy(WTv_sb[:, dc, i * 128:(i + 1) * 128],
                                  wt_ps[j % 2][:]).then_inc(s_te.h, 1)
        s_te.v += ND * ND

    s_vm, s_ve = p.sem("vm"), p.sem("ve")
    dvs = p.gdsem("evs", 2)
    vm0, ve0 = s_vm.v, s_ve.v
    with nc.Block() as blk:
        @blk.tensor
        def _(t):
            for tc in range(NT):
                if tc > 0:
                    t.wait_ge(s_ve.h, ve0 + tc)
                last = None
                for dc in range(ND):
                    for n in range(4):
                        last = t.matmul(pj_ps[:, n * 512:(n + 1) * 512],
                                        x_fm[:, dc, tc * 128:(tc + 1) * 128],
                                        WTv_sb[:, dc, n * 512:(n + 1) * 512],
                                        start=(dc == 0), stop=(dc == ND - 1))
                last.then_inc(s_vm.h, 1)
        s_vm.v += NT

        @blk.vector
        def _(v):
            for tc in range(NT):
                v.wait_ge(s_vm.h, vm0 + tc + 1)
                if tc >= 2:
                    v.wait_ge(dvs.h(tc), dvs.tgt(tc - 2))
                v.tensor_copy(ev_sb[tc % 2][:], pj_ps[:]).then_inc(s_ve.h, 1)
        s_ve.v += NT

        @blk.gpsimd
        def _(g):
            for tc in range(NT):
                g.wait_ge(s_ve.h, ve0 + tc + 1)
                g.dma_start(Vo[tc * 128:(tc + 1) * 128, :], ev_sb[tc % 2][:]).then_inc(dvs.inc(tc), 16)
            g.wait_ge(dvs.h(NT - 2), dvs.tgt(NT - 2))
            g.wait_ge(dvs.h(NT - 1), dvs.tgt(NT - 1))

    s_t2, s_e2, s_qm, s_qe = (p.sem(x) for x in ("t2", "e2", "qm", "qe"))
    dw2 = p.gdsem("ew2", 2)
    dqs = p.gdsem("eqs", 2)
    t20, e20, qm0, qe0 = (s.v for s in (s_t2, s_e2, s_qm, s_qe))
    with nc.Block() as blk:
        @blk.gpsimd
        def _(g):
            def qstore(e):
                g.wait_ge(s_qe.h, qe0 + e + 1)
                dst = Qo if e < ND else Ko
                r = e % ND
                g.dma_start(dst[r * 128:(r + 1) * 128, :], ev_sb[e % 2][:]).then_inc(dqs.inc(e), 16)
            for ec in range(2 * ND):
                if ec >= 2:
                    g.wait_ge(s_qm.h, qm0 + ec - 1)
                g.dma_start(wtmp[ec % 2][:], Wqkv[ec * 128:(ec + 1) * 128, :]).then_inc(dw2.inc(ec), 16)
                if ec >= 2:
                    qstore(ec - 2)
            qstore(2 * ND - 2)
            qstore(2 * ND - 1)
            g.wait_ge(dqs.h(2 * ND - 2), dqs.tgt(2 * ND - 2))
            g.wait_ge(dqs.h(2 * ND - 1), dqs.tgt(2 * ND - 1))

        @blk.tensor
        def _(t):
            for ec in range(2 * ND):
                t.wait_ge(dw2.h(ec), dw2.tgt(ec))
                for dc in range(ND):
                    j = ec * ND + dc
                    if j >= 2:
                        t.wait_ge(s_e2.h, e20 + j - 1)
                    t.transpose(wt_ps[j % 2][:], wtmp[ec % 2][:, dc * 128:(dc + 1) * 128],
                                ident_sb[:]).then_inc(s_t2.h, 1)
                t.wait_ge(s_e2.h, e20 + (ec + 1) * ND)
                if ec > 0:
                    t.wait_ge(s_qe.h, qe0 + ec)
                last = None
                for dc in range(ND):
                    for n in range(4):
                        last = t.matmul(pj_ps[:, n * 512:(n + 1) * 512],
                                        wt_sb[:, dc, :],
                                        x_fm[:, dc, n * 512:(n + 1) * 512],
                                        start=(dc == 0), stop=(dc == ND - 1))
                last.then_inc(s_qm.h, 1)
        s_t2.v += 2 * ND * ND
        s_qm.v += 2 * ND

        @blk.vector
        def _(v):
            for ec in range(2 * ND):
                for dc in range(ND):
                    j = ec * ND + dc
                    v.wait_ge(s_t2.h, t20 + j + 1)
                    v.tensor_copy(wt_sb[:, dc, :], wt_ps[j % 2][:]).then_inc(s_e2.h, 1)
                v.wait_ge(s_qm.h, qm0 + ec + 1)
                if ec >= 2:
                    v.wait_ge(dqs.h(ec), dqs.tgt(ec - 2))
                v.tensor_copy(ev_sb[ec % 2][:], pj_ps[:]).then_inc(s_qe.h, 1)
        s_e2.v += 2 * ND * ND
        s_qe.v += 2 * ND
    p.pop()
    return



def phase_bffn(p, xres_name="x3", out_name="x4"):
    nc = p.nc
    x3 = p.inp(xres_name, [T, D], F32)
    Wt = p.inp("L1_ffn_weight", [D, D], F32)
    brow = p.inp("bffn_row", [1, D], F32)
    ident = p.inp("c_ident", [128, 128], BF16)
    ones1 = p.inp("c_ones1", [1, 128], BF16)
    OUT = p.outp(out_name, [T, D], F32)

    p.push()
    ident_sb = p.sb("id", [128, 128], BF16)
    ones1_sb = p.sb("on", [1, 128], BF16)
    brow_sb = p.sb("br", [1, D], BF16)
    x_fm = p.sb("xfm", [128, ND, T], BF16)
    bwT_sb = p.sb("bwt", [128, ND, D], BF16)
    wtmp = [p.sb("wt", [128, D], BF16) for _ in range(2)]
    sgn_sb = [p.sb("sg", [128, D], BF16) for _ in range(2)]
    asum = p.sb("as", [128, 1], F32)
    alpha = p.sb("al", [128, 1], F32)
    wt_ps = [p.ps("wtp", [128, 128], BF16) for _ in range(2)]
    ps = p.ps("pj", [128, D], F32)
    xr = [p.sb("xr", [128, D], F32) for _ in range(2)]
    xo = [p.sb("xo", [128, D], F32) for _ in range(2)]
    ssum = p.sb("ssum", [128, 1], F32)
    ssq = p.sb("ssq", [128, 1], F32)
    mean = p.sb("mean", [128, 1], F32)
    var = p.sb("var", [128, 1], F32)
    sd = p.sb("sd", [128, 1], F32)
    rstd = p.sb("rstd", [128, 1], F32)

    s_c = p.sem("hc2", global_=True)
    s_cg = p.sem("hcg", global_=True)
    with nc.Block() as blk:
        @blk.sync
        def _(s):
            s.dma_start(ident_sb[:], ident).then_inc(s_c.h, 16)
            s.dma_start(ones1_sb[:], ones1).then_inc(s_c.h, 16)
            s.wait_ge(s_c.h, s_c.v + 32)
        s_c.v += 32

        @blk.gpsimd
        def _(g):
            g.dma_start(brow_sb[:], brow).then_inc(s_cg.h, 16)
            g.wait_ge(s_cg.h, s_cg.v + 16)
        s_cg.v += 16

    emit_load_xfm(p, x3, x_fm, ident_sb)

    s_al, s_sg, s_bw, s_tt, s_te, s_ab = (p.sem(x) for x in
        ("al", "sg", "bw", "tt", "te", "ab"))
    dwl = p.gdsem("hwl", 2)
    al0, sg0, bw0, tt0, te0, ab0 = (s.v for s in (s_al, s_sg, s_bw, s_tt, s_te, s_ab))
    with nc.Block() as blk:
        @blk.gpsimd
        def _(g):
            for oc in range(ND):
                if oc >= 2:
                    g.wait_ge(s_bw.h, bw0 + oc - 1)
                g.dma_start(wtmp[oc % 2][:], Wt[oc * 128:(oc + 1) * 128, :]).then_inc(dwl.inc(oc), 16)

        @blk.vector
        def _(v):
            for oc in range(ND):
                v.wait_ge(s_ab.h, ab0 + oc + 1)
                v.tensor_scalar_mul(alpha[:], asum[:], INV_D).then_inc(s_al.h, 1)
                v.wait_ge(s_sg.h, sg0 + oc + 1)
                v.tensor_scalar_mul(sgn_sb[oc % 2][:], sgn_sb[oc % 2][:],
                                    alpha[:]).then_inc(s_bw.h, 1)
                for dc in range(ND):
                    j = oc * ND + dc
                    v.wait_ge(s_tt.h, tt0 + j + 1)
                    v.tensor_copy(bwT_sb[:, dc, oc * 128:(oc + 1) * 128],
                                  wt_ps[j % 2][:]).then_inc(s_te.h, 1)
        s_ab.v += ND
        s_al.v += ND
        s_bw.v += ND
        s_te.v += ND * ND

        @blk.scalar
        def _(sc):
            for oc in range(ND):
                sc.wait_ge(dwl.h(oc), dwl.tgt(oc))
                if oc >= 2:
                    sc.wait_ge(s_tt.h, tt0 + (oc - 1) * ND)
                sc.activation(sgn_sb[oc % 2][:], wtmp[oc % 2][:], AF.Abs,
                              accum_out=asum[:]).then_inc(s_ab.h, 1)
                sc.sign(sgn_sb[oc % 2][:], wtmp[oc % 2][:]).then_inc(s_sg.h, 1)
        s_sg.v += ND

        @blk.tensor
        def _(t):
            for oc in range(ND):
                t.wait_ge(s_bw.h, bw0 + oc + 1)
                for dc in range(ND):
                    j = oc * ND + dc
                    if j >= 2:
                        t.wait_ge(s_te.h, te0 + j - 1)
                    t.transpose(wt_ps[j % 2][:], sgn_sb[oc % 2][:, dc * 128:(dc + 1) * 128],
                                ident_sb[:]).then_inc(s_tt.h, 1)
        s_tt.v += ND * ND

    _emit_proj_res_ln(p, x_fm, bwT_sb, (ones1_sb, brow_sb), x3, OUT, ps,
                      xr, xo, ssum, ssq, mean, var, sd, rstd)
    p.pop()
    return



NCORES = 8

INTERNAL_NAMES = {"Q_fm", "K_fm", "V_tm", "O_fm", "x1", "WupT", "WdnT", "x2",
                  "Q1_fm", "K1_fm", "V1_tm", "O1_fm", "x3"}


def build_fused(n_phases=9, out_name=None):
    p = Prog()
    p.internal = set(INTERNAL_NAMES)
    if out_name:
        p.internal.discard(out_name)
    phases = [
        (phase_qkv0, {}),
        (phase_attn, {"causal": False}),
        (phase_proj_ln, {"layer": 0, "bname": "bo_row",
                         "xres_name": "x_in", "xout_name": "x1"}),
        (phase_ffn_prep, {}),
        (phase_ffn_main, {}),
        (phase_qkv1, {}),
        (phase_attn, {"causal": True, "qn": "Q1_fm", "kn": "K1_fm",
                      "vn": "V1_tm", "on": "O1_fm"}),
        (phase_proj_ln, {"layer": 1, "on": "O1_fm", "wname": "L1_attn_W_o",
                         "bname": None, "xres_name": "x2", "xout_name": "x3",
                         "w_transposed": True}),
        (phase_bffn, {}),
    ]
    for fn, kw in phases[:n_phases]:
        p.push()
        p.phase_sems = []
        fn(p, **kw)
        used = list(p.phase_sems)
        p.pop()
        with p.nc.Block() as blk:
            @blk.gpsimd
            def _(g):
                for h in used:
                    g.sem_clear(h)
    p.close()
    return p


def host_maps(inputs, cores):
    consts = host_consts()
    x_full = np.ascontiguousarray(inputs["L0_attn_x"])
    shared = {**consts,
              "L0_attn_Wq": inputs["L0_attn_Wq"], "L0_attn_Wk": inputs["L0_attn_Wk"],
              "L0_attn_Wv": inputs["L0_attn_Wv"], "L0_attn_Wo": inputs["L0_attn_Wo"],
              "bq_cols": np.ascontiguousarray(inputs["L0_attn_bq"].reshape(ND, 128).T),
              "bk_cols": np.ascontiguousarray(inputs["L0_attn_bk"].reshape(ND, 128).T),
              "bv_row": np.ascontiguousarray(inputs["L0_attn_bv"].reshape(1, D)),
              "bo_row": np.ascontiguousarray(inputs["L0_attn_bo"].reshape(1, D)),
              "L0_ffn_w_up_q": inputs["L0_ffn_w_up_q"],
              "L0_ffn_w_up_scale": inputs["L0_ffn_w_up_scale"],
              "L0_ffn_w_down_q": inputs["L0_ffn_w_down_q"],
              "L0_ffn_w_down_scale": inputs["L0_ffn_w_down_scale"],
              "bup_cols": np.ascontiguousarray(inputs["L0_ffn_b_up"].reshape(NHC, 128).T),
              "bdnl_ext": np.ascontiguousarray(np.concatenate(
                  [inputs["L0_ffn_w_down_lora_b"],
                   inputs["L0_ffn_b_down"].reshape(1, D)], axis=0)),
              "L0_ffn_w_up_lora_a": inputs["L0_ffn_w_up_lora_a"],
              "L0_ffn_w_up_lora_b": inputs["L0_ffn_w_up_lora_b"],
              "L0_ffn_w_down_lora_a": inputs["L0_ffn_w_down_lora_a"],
              "L1_attn_W_qkv": inputs["L1_attn_W_qkv"],
              "L1_attn_W_o": inputs["L1_attn_W_o"],
              "L1_ffn_weight": inputs["L1_ffn_weight"],
              "bffn_row": np.ascontiguousarray(inputs["L1_ffn_bias"].reshape(1, D))}
    maps = []
    for b in cores:
        m = dict(shared)
        m["x_in"] = np.ascontiguousarray(x_full[b])
        maps.append(m)
    return maps


def kernel(**inputs):
    cores = list(range(NCORES))
    p = build_fused()
    maps = host_maps(inputs, cores)
    res = run_bass_kernel_spmd(p.nc, maps, core_ids=cores)
    return np.stack([res.results[b]["x4"] for b in cores], axis=0)


def _launch(phase_fn, kwargs, per_core_maps, core_ids):
    p = Prog()
    phase_fn(p, **kwargs)
    p.close()
    res = run_bass_kernel_spmd(p.nc, per_core_maps, core_ids=core_ids)
    return res.results


def kernel_multilaunch(**inputs):
    consts = host_consts()
    cores = list(range(NCORES))
    x_full = np.ascontiguousarray(inputs["L0_attn_x"])

    def rep(d):
        return [dict(d) for _ in cores]

    def percore(results, key):
        return [r[key] for r in results]

    bq = inputs["L0_attn_bq"]
    bk = inputs["L0_attn_bk"]
    bv = inputs["L0_attn_bv"]
    bo = inputs["L0_attn_bo"]

    maps = rep({**consts,
                "L0_attn_Wq": inputs["L0_attn_Wq"], "L0_attn_Wk": inputs["L0_attn_Wk"],
                "L0_attn_Wv": inputs["L0_attn_Wv"],
                "bq_cols": np.ascontiguousarray(bq.reshape(ND, 128).T),
                "bk_cols": np.ascontiguousarray(bk.reshape(ND, 128).T),
                "bv_row": np.ascontiguousarray(bv.reshape(1, D))})
    for b in cores:
        maps[b]["x_in"] = np.ascontiguousarray(x_full[b])
    rA = _launch(phase_qkv0, {}, maps, cores)

    maps = rep(consts)
    for b in cores:
        maps[b].update(Q_fm=rA[b]["Q_fm"], K_fm=rA[b]["K_fm"], V_tm=rA[b]["V_tm"])
    rB = _launch(phase_attn, {"causal": False}, maps, cores)

    maps = rep({**consts, "L0_attn_Wo": inputs["L0_attn_Wo"],
                "bo_row": np.ascontiguousarray(bo.reshape(1, D))})
    for b in cores:
        maps[b].update(O_fm=rB[b]["O_fm"], x_in=np.ascontiguousarray(x_full[b]))
    rC = _launch(phase_proj_ln, {"layer": 0, "bname": "bo_row",
                                 "xres_name": "x_in", "xout_name": "x1"}, maps, cores)

    maps = rep({**consts,
                "L0_ffn_w_up_q": inputs["L0_ffn_w_up_q"],
                "L0_ffn_w_up_scale": inputs["L0_ffn_w_up_scale"],
                "L0_ffn_w_down_q": inputs["L0_ffn_w_down_q"],
                "L0_ffn_w_down_scale": inputs["L0_ffn_w_down_scale"]})
    rD1 = _launch(phase_ffn_prep, {}, maps, cores)

    maps = rep({**consts,
                "bup_cols": np.ascontiguousarray(inputs["L0_ffn_b_up"].reshape(NHC, 128).T),
                "bdnl_ext": np.ascontiguousarray(np.concatenate(
                    [inputs["L0_ffn_w_down_lora_b"],
                     inputs["L0_ffn_b_down"].reshape(1, D)], axis=0)),
                "L0_ffn_w_up_lora_a": inputs["L0_ffn_w_up_lora_a"],
                "L0_ffn_w_up_lora_b": inputs["L0_ffn_w_up_lora_b"],
                "L0_ffn_w_down_lora_a": inputs["L0_ffn_w_down_lora_a"]})
    for b in cores:
        maps[b].update(x1=rC[b]["x1"], WupT=rD1[b]["WupT"], WdnT=rD1[b]["WdnT"])
    rD2 = _launch(phase_ffn_main, {}, maps, cores)

    maps = rep({**consts, "L1_attn_W_qkv": inputs["L1_attn_W_qkv"]})
    for b in cores:
        maps[b]["x2"] = rD2[b]["x2"]
    rE = _launch(phase_qkv1, {}, maps, cores)

    maps = rep(consts)
    for b in cores:
        maps[b].update(Q1_fm=rE[b]["Q1_fm"], K1_fm=rE[b]["K1_fm"], V1_tm=rE[b]["V1_tm"])
    rF = _launch(phase_attn, {"causal": True, "qn": "Q1_fm", "kn": "K1_fm",
                              "vn": "V1_tm", "on": "O1_fm"}, maps, cores)

    maps = rep({**consts, "L1_attn_W_o": inputs["L1_attn_W_o"]})
    for b in cores:
        maps[b].update(O1_fm=rF[b]["O1_fm"], x2=rD2[b]["x2"])
    rG = _launch(phase_proj_ln, {"layer": 1, "on": "O1_fm", "wname": "L1_attn_W_o",
                                 "bname": None, "xres_name": "x2", "xout_name": "x3",
                                 "w_transposed": True}, maps, cores)

    maps = rep({**consts, "L1_ffn_weight": inputs["L1_ffn_weight"],
                "bffn_row": np.ascontiguousarray(inputs["L1_ffn_bias"].reshape(1, D))})
    for b in cores:
        maps[b]["x3"] = rG[b]["x3"]
    rH = _launch(phase_bffn, {}, maps, cores)

    out = np.stack([rH[b]["x4"] for b in cores], axis=0)
    return out


# version 06
# speedup vs baseline: 1.2407x; 1.1106x over previous
import math
import numpy as np
import ml_dtypes
from contextlib import ExitStack

import concourse.bass as bass
import concourse.mybir as mybir
from concourse.bass_utils import run_bass_kernel_spmd

F32 = mybir.dt.float32
BF16 = mybir.dt.bfloat16
I32 = mybir.dt.int32
AF = mybir.ActivationFunctionType
ALU = mybir.AluOpType
AX = mybir.AxisListType

T = 2048
D = 2048
NH = 16
HS = 128
HF = 8192
RK = 16
NT = T // 128
ND = D // 128
NHC = HF // 128
SM_SCALE = 1.0 / math.sqrt(HS)
LN_EPS = 1e-5
INV_D = 1.0 / D

BF = ml_dtypes.bfloat16


class Sem:
    __slots__ = ("h", "v")

    def __init__(self, h):
        self.h = h
        self.v = 0


class DSem:

    def __init__(self, p, name, n):
        self.n = n
        self.s = [p.sem(f"{name}_sl{i}", global_=True) for i in range(n)]
        self.base = [x.v for x in self.s]

    def rebase(self):
        self.base = [x.v for x in self.s]

    def inc(self, k, amt=16):
        ss = self.s[k % self.n]
        ss.v += amt
        return ss.h

    def h(self, k):
        return self.s[k % self.n].h

    def tgt(self, k, amt=16):
        return self.base[k % self.n] + amt * (k // self.n + 1)


class Prog:
    def __init__(self, name="prog"):
        self.nc = bass.Bass("TRN2", target_bir_lowering=False,
                            detect_race_conditions=False)
        self.sem_st = ExitStack()
        self.stacks = [ExitStack()]
        self._n = 0

    def uid(self):
        self._n += 1
        return self._n

    def sem(self, name, global_=False):
        if global_:
            if not hasattr(self, "_gsems"):
                self._gsems = {}
            if name not in self._gsems:
                h = self.stacks[0].enter_context(self.nc.semaphore(name))
                self._gsems[name] = Sem(h)
            return self._gsems[name]
        h = self.stacks[-1].enter_context(self.nc.semaphore(f"{name}{self.uid()}"))
        if not hasattr(self, "phase_sems"):
            self.phase_sems = []
        self.phase_sems.append(h)
        return Sem(h)

    def gdsem(self, name, n):
        if not hasattr(self, "_gds"):
            self._gds = {}
        if name not in self._gds:
            self._gds[name] = DSem(self, name, n)
        d = self._gds[name]
        assert d.n == n
        d.rebase()
        return d

    def sb(self, name, shape, dt):
        return self.stacks[-1].enter_context(
            self.nc.sbuf_tensor(f"{name}{self.uid()}", shape, dt))

    def ps(self, name, shape, dt):
        return self.stacks[-1].enter_context(
            self.nc.psum_tensor(f"{name}{self.uid()}", shape, dt))

    def push(self):
        self.stacks.append(ExitStack())

    def pop(self):
        self.stacks.pop().close()

    def inp(self, name, shape, dt):
        return self._dram(name, shape, dt, "ExternalInput")

    def outp(self, name, shape, dt):
        return self._dram(name, shape, dt, "ExternalOutput")

    def _dram(self, name, shape, dt, kind):
        if not hasattr(self, "_dram_cache"):
            self._dram_cache = {}
        if name in self._dram_cache:
            return self._dram_cache[name]
        if getattr(self, "internal", None) and name in self.internal:
            kind = "Internal"
        ap = self.nc.dram_tensor(name, shape, dt, kind=kind).ap()
        self._dram_cache[name] = ap
        return ap

    def close(self):
        while self.stacks:
            self.stacks.pop().close()
        self.sem_st.close()



def host_consts():
    c = {}
    c["c_ident"] = np.eye(128, dtype=BF)
    c["c_tril"] = np.tril(np.ones((128, 128), dtype=np.float32)).astype(BF)
    triu_neg = np.triu(np.full((128, 128), -1e30, dtype=np.float32), k=1)
    c["c_triuneg"] = triu_neg
    c["c_ones1"] = np.ones((1, 128), dtype=BF)
    c["c_onesrow"] = np.ones((1, 512), dtype=BF)
    t = np.arange(T, dtype=np.float32)
    inv_freq = (1.0 / (10000.0 ** (np.arange(0, HS, 2, dtype=np.float32) / HS))).astype(np.float32)
    ang = t[None, :] * inv_freq[:, None]
    cos = np.cos(ang).astype(np.float32)
    sin = np.sin(ang).astype(np.float32)
    C = np.empty((128, T), dtype=np.float32)
    S = np.empty((128, T), dtype=np.float32)
    C[0::2] = cos
    C[1::2] = cos
    S[0::2] = -sin
    S[1::2] = sin
    c["c_ropeC"] = C
    c["c_ropeS"] = S
    return c


SWAP_MASK = []
for _i in range(16):
    SWAP_MASK += [2 * _i + 1, 2 * _i]



def emit_load_xfm(p, x_dram, x_fm, ident_sb, bufs=None):
    nc = p.nc
    p.push()
    xbf = bufs if bufs is not None else [p.sb("xbf", [128, D], BF16) for _ in range(2)]
    pt = [p.ps("xtp", [128, 128], BF16) for _ in range(2)]
    dld = p.gdsem("xld", 2)
    s_tp, s_ev = p.sem("xtp"), p.sem("xev")
    tp0, ev0 = s_tp.v, s_ev.v
    with nc.Block() as blk:
        @blk.gpsimd
        def _(g):
            for tc in range(NT):
                if tc >= 2:
                    g.wait_ge(s_ev.h, ev0 + (tc - 1) * ND)
                g.dma_start(xbf[tc % 2][:], x_dram[tc * 128:(tc + 1) * 128, :]).then_inc(dld.inc(tc), 16)

        @blk.tensor
        def _(t):
            for tc in range(NT):
                t.wait_ge(dld.h(tc), dld.tgt(tc))
                for dc in range(ND):
                    i = tc * ND + dc
                    if i >= 2:
                        t.wait_ge(s_ev.h, ev0 + i - 1)
                    t.transpose(pt[i % 2][:], xbf[tc % 2][:, dc * 128:(dc + 1) * 128],
                                ident_sb[:]).then_inc(s_tp.h, 1)
        s_tp.v += NT * ND

        @blk.vector
        def _(v):
            for tc in range(NT):
                for dc in range(ND):
                    i = tc * ND + dc
                    v.wait_ge(s_tp.h, tp0 + i + 1)
                    v.tensor_copy(x_fm[:, dc, tc * 128:(tc + 1) * 128], pt[i % 2][:]).then_inc(s_ev.h, 1)
        s_ev.v += NT * ND
    p.pop()


def emit_res_ln(p, blk_engines, delta_ps, xres_sb, xout_sb, tmp):
    raise NotImplementedError



def phase_qkv0(p):
    nc = p.nc
    x = p.inp("x_in", [T, D], F32)
    Wq = p.inp("L0_attn_Wq", [D, D], F32)
    Wk = p.inp("L0_attn_Wk", [D, D], F32)
    Wv = p.inp("L0_attn_Wv", [D, D], F32)
    bq = p.inp("bq_cols", [128, ND], F32)
    bk = p.inp("bk_cols", [128, ND], F32)
    bv = p.inp("bv_row", [1, D], F32)
    ident = p.inp("c_ident", [128, 128], BF16)
    ropeC = p.inp("c_ropeC", [128, T], F32)
    ropeS = p.inp("c_ropeS", [128, T], F32)
    ones1 = p.inp("c_ones1", [1, 128], BF16)
    Qo = p.outp("Q_fm", [D, T], BF16)
    Ko = p.outp("K_fm", [D, T], BF16)
    Vo = p.outp("V_tm", [T, D], BF16)

    ident_sb = p.sb("ident", [128, 128], BF16)
    ropeC_sb = p.sb("ropeC", [128, T], F32)
    ropeS_sb = p.sb("ropeS", [128, T], F32)
    ones1_sb = p.sb("ones1", [1, 128], BF16)
    bq_sb = p.sb("bq", [128, ND], F32)
    bk_sb = p.sb("bk", [128, ND], F32)
    bv_sb = p.sb("bv", [1, D], BF16)
    x_fm = p.sb("xfm", [128, ND, T], BF16)
    w_sb = p.sb("wsb", [128, ND, D], BF16)

    s_c = p.sem("qk0c", global_=True)
    s_cg = p.sem("qk0cg", global_=True)
    c0 = s_c.v
    with nc.Block() as blk:
        @blk.sync
        def _(s):
            s.dma_start(ident_sb[:], ident).then_inc(s_c.h, 16)
            s.dma_start(ropeC_sb[:], ropeC).then_inc(s_c.h, 16)
            s.dma_start(ropeS_sb[:], ropeS).then_inc(s_c.h, 16)
            s.dma_start(ones1_sb[:], ones1).then_inc(s_c.h, 16)
            s.dma_start(bq_sb[:], bq).then_inc(s_c.h, 16)
            s.dma_start(bk_sb[:], bk).then_inc(s_c.h, 16)
            s.wait_ge(s_c.h, c0 + 96)
        s_c.v += 96

        @blk.gpsimd
        def _(g):
            g.dma_start(bv_sb[:], bv).then_inc(s_cg.h, 16)
            g.wait_ge(s_cg.h, s_cg.v + 16)
        s_cg.v += 16

    emit_load_xfm(p, x, x_fm, ident_sb)

    proj_ps = p.ps("pj", [128, T], F32)
    for W, bias_cols, OUT, roped in ((Wq, bq_sb, Qo, True), (Wk, bk_sb, Ko, True)):
        _emit_proj_fm(p, proj_ps, W, w_sb, x_fm, bias_cols, OUT,
                      rope=(ropeC_sb, ropeS_sb) if roped else None)

    _emit_proj_tm_v(p, proj_ps, Wv, w_sb, x_fm, bv_sb, ones1_sb, Vo)
    return ["x_in", "L0_attn_Wq", "L0_attn_Wk", "L0_attn_Wv", "bq_cols",
            "bk_cols", "bv_row", "c_ident", "c_ropeC", "c_ropeS", "c_ones1"]


def _emit_proj_fm(p, ps, W, w_sb, x_fm, bias_cols, OUT, rope=None):
    nc = p.nc
    p.push()
    raw = [p.sb("qraw", [128, T], BF16) for _ in range(2)]
    sw = p.sb("qsw", [128, T], BF16)
    t1 = p.sb("qt1", [128, T], BF16)
    out_sb = [p.sb("qout", [128, T], BF16) for _ in range(2)]
    s_w = p.sem("pfw", global_=True)
    s_mm, s_ev, s_rp = (p.sem(x) for x in ("mm", "ev", "rp"))
    dst = p.gdsem("pfst", 2)
    w0, mm0, ev0, rp0 = s_w.v, s_mm.v, s_ev.v, s_rp.v
    with nc.Block() as blk:
        @blk.gpsimd
        def _(g):
            for dc in range(ND):
                g.dma_start(w_sb[:, dc, :], W[dc * 128:(dc + 1) * 128, :]).then_inc(s_w.h, 16)
            for ec in range(ND):
                g.wait_ge(s_rp.h, rp0 + ec + 1)
                g.dma_start(OUT[ec * 128:(ec + 1) * 128, :], out_sb[ec % 2][:]).then_inc(dst.inc(ec), 16)
            g.wait_ge(dst.h(ND - 2), dst.tgt(ND - 2))
            g.wait_ge(dst.h(ND - 1), dst.tgt(ND - 1))
        s_w.v += 16 * ND

        @blk.tensor
        def _(t):
            t.wait_ge(s_w.h, w0 + 16 * ND)
            for ec in range(ND):
                if ec > 0:
                    t.wait_ge(s_ev.h, ev0 + ec)
                for dc in range(ND):
                    for n in range(4):
                        ins = t.matmul(ps[:, n * 512:(n + 1) * 512],
                                       w_sb[:, dc, ec * 128:(ec + 1) * 128],
                                       x_fm[:, dc, n * 512:(n + 1) * 512],
                                       start=(dc == 0), stop=(dc == ND - 1))
                        if dc == ND - 1 and n == 3:
                            ins.then_inc(s_mm.h, 1)
        s_mm.v += ND

        @blk.scalar
        def _(sc):
            for ec in range(ND):
                sc.wait_ge(s_mm.h, mm0 + ec + 1)
                if ec >= 2:
                    sc.wait_ge(s_rp.h, rp0 + ec - 1)
                sc.activation(raw[ec % 2][:], ps[:], AF.Identity,
                              bias=bias_cols[:, ec:ec + 1], scale=1.0).then_inc(s_ev.h, 1)
        s_ev.v += ND

        @blk.vector
        def _(v):
            C, S = rope
            for ec in range(ND):
                v.wait_ge(s_ev.h, ev0 + ec + 1)
                if ec >= 2:
                    v.wait_ge(dst.h(ec), dst.tgt(ec - 2))
                r = raw[ec % 2]
                v.stream_shuffle(sw[:], r[:], SWAP_MASK)
                v.tensor_mul(t1[:], r[:], C[:])
                v.tensor_mul(sw[:], sw[:], S[:])
                v.tensor_add(out_sb[ec % 2][:], t1[:], sw[:]).then_inc(s_rp.h, 1)
        s_rp.v += ND
    p.pop()


def _emit_proj_tm_v(p, ps, W, w_sb, x_fm, bv_sb, ones1_sb, OUT):
    nc = p.nc
    p.push()
    out_sb = [p.sb("vout", [128, D], BF16) for _ in range(2)]
    s_w = p.sem("tvw", global_=True)
    s_mm, s_ev = (p.sem(x) for x in ("mm", "ev"))
    dst = p.gdsem("tvst", 2)
    w0, mm0, ev0 = s_w.v, s_mm.v, s_ev.v
    with nc.Block() as blk:
        @blk.gpsimd
        def _(g):
            for dc in range(ND):
                g.dma_start(w_sb[:, dc, :], W[dc * 128:(dc + 1) * 128, :]).then_inc(s_w.h, 16)
            for tc in range(NT):
                g.wait_ge(s_ev.h, ev0 + tc + 1)
                g.dma_start(OUT[tc * 128:(tc + 1) * 128, :], out_sb[tc % 2][:]).then_inc(dst.inc(tc), 16)
            g.wait_ge(dst.h(NT - 2), dst.tgt(NT - 2))
            g.wait_ge(dst.h(NT - 1), dst.tgt(NT - 1))
        s_w.v += 16 * ND

        @blk.tensor
        def _(t):
            t.wait_ge(s_w.h, w0 + 16 * ND)
            for tc in range(NT):
                if tc > 0:
                    t.wait_ge(s_ev.h, ev0 + tc)
                for dc in range(ND):
                    for n in range(4):
                        t.matmul(ps[:, n * 512:(n + 1) * 512],
                                 x_fm[:, dc, tc * 128:(tc + 1) * 128],
                                 w_sb[:, dc, n * 512:(n + 1) * 512],
                                 start=(dc == 0), stop=False)
                for n in range(4):
                    ins = t.matmul(ps[:, n * 512:(n + 1) * 512],
                                   ones1_sb[:],
                                   bv_sb[:, n * 512:(n + 1) * 512],
                                   start=False, stop=True)
                    if n == 3:
                        ins.then_inc(s_mm.h, 1)
        s_mm.v += NT

        @blk.vector
        def _(v):
            for tc in range(NT):
                v.wait_ge(s_mm.h, mm0 + tc + 1)
                if tc >= 2:
                    v.wait_ge(dst.h(tc), dst.tgt(tc - 2))
                v.tensor_copy(out_sb[tc % 2][:], ps[:]).then_inc(s_ev.h, 1)
        s_ev.v += NT
    p.pop()



def phase_attn(p, causal, qn="Q_fm", kn="K_fm", vn="V_tm", on="O_fm"):
    nc = p.nc
    Q = p.inp(qn, [D, T], BF16)
    K = p.inp(kn, [D, T], BF16)
    V = p.inp(vn, [T, D], BF16)
    O = p.outp(on, [D, T], BF16)
    ident = p.inp("c_ident", [128, 128], BF16)
    if causal:
        mask_in = p.inp("c_triuneg", [128, 128], F32)
    else:
        mask_in = p.inp("c_tril", [128, 128], BF16)

    p.push()
    V_sb = p.sb("v", [128, NT, D], BF16)
    K_sb = [p.sb("k", [128, T], BF16) for _ in range(2)]
    Q_sb = [p.sb("q", [128, T], BF16) for _ in range(2)]
    A_sb = [p.sb("a", [128, T], BF16) for _ in range(2)]
    AT_sb = [p.sb("at", [128, NT, 512], BF16) for _ in range(2)]
    O_sb = [p.sb("o", [128, T], BF16) for _ in range(2)]
    ident_sb = p.sb("id", [128, 128], BF16)
    mask_sb = p.sb("mk", [128, 128], F32 if causal else BF16)
    ncol = [p.sb("ncol", [128, 2], F32) for _ in range(2)]
    nflt = [p.sb("nflt", [128, 2], F32) for _ in range(2)]
    ntot = p.sb("ntot", [128, 1], F32)
    r_col = p.sb("rcol", [128, 1], F32)
    s_ps = [p.ps("sps", [128, 1024], F32) for _ in range(2)]
    at_ps = [p.ps("atps", [128, 128], BF16) for _ in range(2)]
    o_ps = p.ps("ops", [128, 512], F32)

    s_v = p.sem("attv", global_=True)
    s_Sh, s_msk, s_exph, s_anorm = (p.sem(x) for x in ("Sh", "msk", "exph", "an"))
    s_ncp = p.sem("ncp")
    s_na = p.sem("na")
    s_tp, s_atw, s_av, s_oev = (p.sem(x) for x in ("tp", "atw", "av", "oev"))
    s_rr = p.sem("rr")
    dkq = p.gdsem("attkq", 2)
    dost = p.gdsem("attost", 2)
    v0, Sh0, msk0, exph0, an0 = s_v.v, s_Sh.v, s_msk.v, s_exph.v, s_anorm.v
    ncp0 = s_ncp.v
    na0 = s_na.v
    tp0, atw0, av0, oev0, rr0 = s_tp.v, s_atw.v, s_av.v, s_oev.v, s_rr.v

    def halves(qt):
        W = (qt + 1) * 128 if causal else T
        if W <= 1024:
            return [(0, W)]
        return [(0, 1024), (1024, W)]

    def nhalves(qt):
        return len(halves(qt))

    def halfcum(g):
        h, qt = divmod(g, NT)
        per_head = sum(nhalves(q) for q in range(NT))
        return h * per_head + sum(nhalves(q) for q in range(qt))

    def cum_tp(g):
        h, qt = divmod(g, NT)
        return h * (NT * (NT + 1) // 2) + qt * (qt + 1) // 2

    def cum_atw_grp(h, qg):
        return h * 160 + 8 * qg * (qg + 1)

    def atw_before(g):
        h, qt = divmod(g, NT)
        qg, qi = divmod(qt, 4)
        return cum_atw_grp(h, qg) + sum(q + 1 for q in range(4 * qg, qt))

    def nmask(g):
        return g + 1

    GMAX = NH * NT

    def emit_S(t, g):
        h, qt = divmod(g, NT)
        for hi, (lo, hiW) in enumerate(halves(qt)):
            hc = halfcum(g) + hi
            if hc >= 2:
                t.wait_ge(s_exph.h, exph0 + hc - 1)
            width = hiW - lo
            nfull = (width + 511) // 512
            for n in range(nfull):
                a, b = lo + n * 512, min(hiW, lo + (n + 1) * 512)
                ins = t.matmul(s_ps[hc % 2][:, a - lo:b - lo],
                               Q_sb[h % 2][:, qt * 128:(qt + 1) * 128],
                               K_sb[h % 2][:, a:b],
                               start=True, stop=True)
            ins.then_inc(s_Sh.h, 1)

    def emit_work(t, g):
        h, qt = divmod(g, NT)
        qg, qi = divmod(qt, 4)
        t.wait_ge(s_anorm.h, an0 + g + 1)
        base_w = atw_before(g)
        for kc in range(qt + 1):
            if kc >= 2:
                t.wait_ge(s_atw.h, atw0 + base_w + kc - 1)
            elif g >= 1:
                t.wait_ge(s_atw.h, atw0 + base_w)
            t.transpose(at_ps[kc % 2][:], A_sb[g % 2][:, kc * 128:(kc + 1) * 128],
                        ident_sb[:]).then_inc(s_tp.h, 1)
        if qi == 3:
            t.wait_ge(s_atw.h, atw0 + cum_atw_grp(h, qg) + 16 * (qg + 1))
            t.wait_ge(s_oev.h, oev0 + 4 * h + qg)
            kmax = 4 * qg + 4
            for kc in range(kmax):
                ins = t.matmul(o_ps[:], V_sb[:, kc, h * 128:(h + 1) * 128],
                               AT_sb[qg % 2][:, kc, :],
                               start=(kc == 0), stop=(kc == kmax - 1))
            ins.then_inc(s_av.h, 1)

    with nc.Block() as blk:
        @blk.sync
        def _(s):
            s.dma_start(ident_sb[:], ident).then_inc(s_v.h, 16)
            s.dma_start(mask_sb[:], mask_in).then_inc(s_v.h, 16)
            for tc in range(NT):
                s.dma_start(V_sb[:, tc, :], V[tc * 128:(tc + 1) * 128, :]).then_inc(s_v.h, 16)
            for h in range(NH):
                if h >= 2:
                    s.wait_ge(s_Sh.h, Sh0 + halfcum(NT * (h - 1)))
                s.dma_start(K_sb[h % 2][:], K[h * 128:(h + 1) * 128, :]).then_inc(dkq.inc(h), 16)
                s.dma_start(Q_sb[h % 2][:], Q[h * 128:(h + 1) * 128, :]).then_inc(dkq.inc(h), 16)
        s_v.v += 16 * (NT + 2)

        @blk.tensor
        def _(t):
            t.wait_ge(s_v.h, v0 + 16 * (NT + 2))
            t.wait_ge(dkq.h(0), dkq.tgt(0, 32))
            emit_S(t, 0)
            for g in range(GMAX):
                if g + 1 < GMAX:
                    h1 = (g + 1) // NT
                    if (g + 1) % NT == 0:
                        t.wait_ge(dkq.h(h1), dkq.tgt(h1, 32))
                    emit_S(t, g + 1)
                emit_work(t, g)
        s_Sh.v += halfcum(GMAX - 1) + nhalves(NT - 1)
        s_tp.v += NH * (NT * (NT + 1) // 2)
        s_av.v += NH * 4

        @blk.scalar
        def _(sc):
            for g in range(GMAX):
                h, qt = divmod(g, NT)
                if g >= 2:
                    sc.wait_ge(s_tp.h, tp0 + cum_tp(g - 1))
                for hi, (lo, hiW) in enumerate(halves(qt)):
                    hc = halfcum(g) + hi
                    sc.wait_ge(s_Sh.h, Sh0 + hc + 1)
                    if causal and hi == len(halves(qt)) - 1:
                        sc.wait_ge(s_msk.h, msk0 + nmask(g))
                    sc.activation(A_sb[g % 2][:, lo:hiW], s_ps[hc % 2][:, 0:hiW - lo],
                                  AF.Exp, scale=SM_SCALE,
                                  accum_out=ncol[g % 2][:, hi:hi + 1]).then_inc(s_exph.h, 1)
                sc.copy(nflt[g % 2][:], ncol[g % 2][:]).then_inc(s_ncp.h, 1)
        s_exph.v += halfcum(GMAX - 1) + nhalves(NT - 1)
        s_ncp.v += GMAX

        @blk.vector
        def _(v):
            for g in range(GMAX):
                h, qt = divmod(g, NT)
                qg, qi = divmod(qt, 4)
                W = (qt + 1) * 128 if causal else T
                nh = nhalves(qt)
                if causal:
                    hcl = halfcum(g) + nh - 1
                    lo = halves(qt)[-1][0]
                    v.wait_ge(s_Sh.h, Sh0 + hcl + 1)
                    v.tensor_add(s_ps[hcl % 2][:, qt * 128 - lo:(qt + 1) * 128 - lo],
                                 s_ps[hcl % 2][:, qt * 128 - lo:(qt + 1) * 128 - lo],
                                 mask_sb[:]).then_inc(s_msk.h, 1)
                v.wait_ge(s_ncp.h, ncp0 + g + 1)
                if nh == 2:
                    v.tensor_add(ntot[:], nflt[g % 2][:, 0:1],
                                 nflt[g % 2][:, 1:2]).then_inc(s_na.h, 1)
                else:
                    v.tensor_copy(ntot[:], nflt[g % 2][:, 0:1]).then_inc(s_na.h, 1)
                v.wait_ge(s_na.h, na0 + g + 1)
                v.reciprocal(r_col[:], ntot[:]).then_inc(s_rr.h, 1)
                v.wait_ge(s_rr.h, rr0 + g + 1)
                ins = v.tensor_scalar_mul(A_sb[g % 2][:, 0:W], A_sb[g % 2][:, 0:W], r_col[:])
                if not causal:
                    ins = v.tensor_mul(A_sb[g % 2][:, qt * 128:(qt + 1) * 128],
                                       A_sb[g % 2][:, qt * 128:(qt + 1) * 128],
                                       mask_sb[:])
                ins.then_inc(s_anorm.h, 1)
                base_w = atw_before(g)
                for kc in range(qt + 1):
                    if qi == 0 and kc == 0 and qg >= 2:
                        v.wait_ge(s_av.h, av0 + 4 * h + qg - 1)
                    if qi == 0 and kc == 0 and qg < 2 and h >= 1:
                        v.wait_ge(s_av.h, av0 + 4 * (h - 1) + qg + 3)
                    v.wait_ge(s_tp.h, tp0 + cum_tp(g) + kc + 1)
                    v.tensor_copy(AT_sb[qg % 2][:, kc, qi * 128:(qi + 1) * 128],
                                  at_ps[kc % 2][:]).then_inc(s_atw.h, 1)
                if qi == 3:
                    for qt2 in range(4 * qg, 4 * qg + 4):
                        for kc in range(qt2 + 1, 4 * qg + 4):
                            v.memset(AT_sb[qg % 2][:, kc, (qt2 % 4) * 128:(qt2 % 4 + 1) * 128],
                                     0.0).then_inc(s_atw.h, 1)
                    v.wait_ge(s_av.h, av0 + 4 * h + qg + 1)
                    if qg == 0 and h >= 2:
                        v.wait_ge(dost.h(h), dost.tgt(h - 2))
                    v.tensor_copy(O_sb[h % 2][:, qg * 512:(qg + 1) * 512],
                                  o_ps[:]).then_inc(s_oev.h, 1)
        if causal:
            s_msk.v += GMAX
        s_na.v += GMAX
        s_rr.v += GMAX
        s_anorm.v += GMAX
        s_atw.v += NH * 160
        s_oev.v += NH * 4

        @blk.gpsimd
        def _(g):
            for h in range(NH):
                g.wait_ge(s_oev.h, oev0 + 4 * (h + 1))
                g.dma_start(O[h * 128:(h + 1) * 128, :], O_sb[h % 2][:]).then_inc(dost.inc(h), 16)
            g.wait_ge(dost.h(NH - 2), dost.tgt(NH - 2))
            g.wait_ge(dost.h(NH - 1), dost.tgt(NH - 1))
    p.pop()



def phase_proj_ln(p, layer, on="O_fm", wname="L0_attn_Wo", bname=None,
                  xres_name="x_in", xout_name="x1", w_transposed=False):
    nc = p.nc
    O = p.inp(on, [D, T], BF16)
    Wo = p.inp(wname, [D, D], F32)
    bo = p.inp(bname, [1, D], F32) if bname else None
    xres = p.inp(xres_name, [T, D], F32)
    xout = p.outp(xout_name, [T, D], F32)
    ident = p.inp("c_ident", [128, 128], BF16)
    ones1 = p.inp("c_ones1", [1, 128], BF16)

    p.push()
    O_sb = p.sb("osb", [128, ND, T], BF16)
    W_sb = p.sb("wsb", [128, ND, D], BF16)
    ident_sb = p.sb("id", [128, 128], BF16)
    ones1_sb = p.sb("on", [1, 128], BF16)
    bo_sb = p.sb("bo", [1, D], BF16) if bname else None
    xr = [p.sb("xr", [128, D], F32) for _ in range(2)]
    xo = [p.sb("xo", [128, D], F32) for _ in range(2)]
    ssum = p.sb("ssum", [128, 1], F32)
    ssq = p.sb("ssq", [128, 1], F32)
    mean = p.sb("mean", [128, 1], F32)
    var = p.sb("var", [128, 1], F32)
    sd = p.sb("sd", [128, 1], F32)
    rstd = p.sb("rstd", [128, 1], F32)
    ps = p.ps("pj", [128, D], F32)

    s_c = p.sem("plc", global_=True)
    s_w = p.sem("plw", global_=True)
    c0, w0 = s_c.v, s_w.v

    with nc.Block() as blk:
        @blk.sync
        def _(s):
            s.dma_start(ident_sb[:], ident).then_inc(s_c.h, 16)
            s.dma_start(ones1_sb[:], ones1).then_inc(s_c.h, 16)
            for dc in range(ND):
                s.dma_start(O_sb[:, dc, :], O[dc * 128:(dc + 1) * 128, :]).then_inc(s_c.h, 16)
            s.wait_ge(s_c.h, c0 + 16 * (ND + 2))
        s_c.v += 16 * (ND + 2)

        @blk.gpsimd
        def _(g):
            if bname:
                g.dma_start(bo_sb[:], bo).then_inc(s_w.h, 16)
                g.wait_ge(s_w.h, w0 + 16)
        if bname:
            s_w.v += 16

    if not w_transposed:
        w0b = s_w.v
        with nc.Block() as blk:
            @blk.gpsimd
            def _(g):
                for dc in range(ND):
                    g.dma_start(W_sb[:, dc, :], Wo[dc * 128:(dc + 1) * 128, :]).then_inc(s_w.h, 16)
                g.wait_ge(s_w.h, w0b + 16 * ND)
            s_w.v += 16 * ND
    else:
        wtmp = [p.sb("wt", [128, D], BF16) for _ in range(2)]
        wt_ps = [p.ps("wtp", [128, 128], BF16) for _ in range(2)]
        s_tt, s_te = p.sem("tt"), p.sem("te")
        dwl = p.gdsem("plwl", 2)
        tt0, te0 = s_tt.v, s_te.v
        with nc.Block() as blk:
            @blk.gpsimd
            def _(g):
                for ec in range(ND):
                    if ec >= 2:
                        g.wait_ge(s_te.h, te0 + (ec - 1) * ND)
                    g.dma_start(wtmp[ec % 2][:], Wo[ec * 128:(ec + 1) * 128, :]).then_inc(dwl.inc(ec), 16)

            @blk.tensor
            def _(t):
                for ec in range(ND):
                    t.wait_ge(dwl.h(ec), dwl.tgt(ec))
                    for dc in range(ND):
                        i = ec * ND + dc
                        if i >= 2:
                            t.wait_ge(s_te.h, te0 + i - 1)
                        t.transpose(wt_ps[i % 2][:], wtmp[ec % 2][:, dc * 128:(dc + 1) * 128],
                                    ident_sb[:]).then_inc(s_tt.h, 1)
            s_tt.v += ND * ND

            @blk.vector
            def _(v):
                for ec in range(ND):
                    for dc in range(ND):
                        i = ec * ND + dc
                        v.wait_ge(s_tt.h, tt0 + i + 1)
                        v.tensor_copy(W_sb[:, dc, ec * 128:(ec + 1) * 128],
                                      wt_ps[i % 2][:]).then_inc(s_te.h, 1)
            s_te.v += ND * ND

    _emit_proj_res_ln(p, O_sb, W_sb,
                      (ones1_sb, bo_sb) if bname else None, xres, xout, ps,
                      xr, xo, ssum, ssq, mean, var, sd, rstd)
    p.pop()
    return


def _emit_proj_res_ln(p, lhsT_sb, W_sb, bias_pair, xres, xout, ps,
                      xr, xo, ssum, ssq, mean, var, sd, rstd):
    nc = p.nc
    s_mm, s_r1, s_sq, s_nm = (p.sem(x) for x in ("mm", "r1", "sq", "nm"))
    s_v2, s_sd, s_rc = p.sem("v2"), p.sem("sd"), p.sem("rc")
    dx = p.gdsem("lnx", 2)
    dst = p.gdsem("lnst", 2)
    mm0, r10, sq0, nm0 = (s.v for s in (s_mm, s_r1, s_sq, s_nm))
    v20, sd0, rc0 = s_v2.v, s_sd.v, s_rc.v
    with nc.Block() as blk:
        @blk.sync
        def _(s):
            for qc in range(NT):
                if qc >= 2:
                    s.wait_ge(s_nm.h, nm0 + qc - 1)
                s.dma_start(xr[qc % 2][:], xres[qc * 128:(qc + 1) * 128, :]).then_inc(dx.inc(qc), 16)

        @blk.tensor
        def _(t):
            for qc in range(NT):
                if qc > 0:
                    t.wait_ge(s_sq.h, sq0 + qc)
                last = None
                for dc in range(ND):
                    for n in range(4):
                        last = t.matmul(ps[:, n * 512:(n + 1) * 512],
                                        lhsT_sb[:, dc, qc * 128:(qc + 1) * 128],
                                        W_sb[:, dc, n * 512:(n + 1) * 512],
                                        start=(dc == 0),
                                        stop=(False if bias_pair else dc == ND - 1))
                if bias_pair:
                    ones1_sb, bo_sb = bias_pair
                    for n in range(4):
                        last = t.matmul(ps[:, n * 512:(n + 1) * 512], ones1_sb[:],
                                        bo_sb[:, n * 512:(n + 1) * 512],
                                        start=False, stop=True)
                last.then_inc(s_mm.h, 1)
        s_mm.v += NT

        @blk.vector
        def _(v):
            for qc in range(NT):
                v.wait_ge(s_mm.h, mm0 + qc + 1)
                v.wait_ge(dx.h(qc), dx.tgt(qc))
                v.scalar_tensor_tensor(xr[qc % 2][:], ps[:], 1.0, xr[qc % 2][:],
                                       ALU.mult, ALU.add,
                                       accum_out=ssum[:]).then_inc(s_r1.h, 1)
                v.wait_ge(s_sq.h, sq0 + qc + 1)
                v.tensor_scalar_mul(mean[:], ssum[:], INV_D)
                v.tensor_scalar_mul(var[:], ssq[:], INV_D)
                v.tensor_mul(rstd[:], mean[:], mean[:])
                v.tensor_sub(var[:], var[:], rstd[:])
                v.tensor_scalar_add(var[:], var[:], LN_EPS).then_inc(s_v2.h, 1)
                v.wait_ge(s_sd.h, sd0 + qc + 1)
                if qc >= 2:
                    v.wait_ge(dst.h(qc), dst.tgt(qc - 2))
                v.reciprocal(rstd[:], sd[:]).then_inc(s_rc.h, 1)
                v.wait_ge(s_rc.h, rc0 + qc + 1)
                v.tensor_scalar(xo[qc % 2][:], xr[qc % 2][:], mean[:], rstd[:],
                                ALU.subtract, ALU.mult).then_inc(s_nm.h, 1)
        s_r1.v += NT
        s_v2.v += NT
        s_rc.v += NT
        s_nm.v += NT

        @blk.scalar
        def _(sc):
            for qc in range(NT):
                sc.wait_ge(s_r1.h, r10 + qc + 1)
                sc.activation(ps[:], xr[qc % 2][:], AF.Square,
                              accum_out=ssq[:]).then_inc(s_sq.h, 1)
                sc.wait_ge(s_v2.h, v20 + qc + 1)
                sc.activation(sd[:], var[:], AF.Sqrt).then_inc(s_sd.h, 1)
        s_sq.v += NT
        s_sd.v += NT

        @blk.gpsimd
        def _(g):
            for qc in range(NT):
                g.wait_ge(s_nm.h, nm0 + qc + 1)
                g.dma_start(xout[qc * 128:(qc + 1) * 128, :], xo[qc % 2][:]).then_inc(dst.inc(qc), 16)
            g.wait_ge(dst.h(NT - 2), dst.tgt(NT - 2))
            g.wait_ge(dst.h(NT - 1), dst.tgt(NT - 1))



def phase_ffn_prep(p):
    nc = p.nc
    wup_q = p.inp("L0_ffn_w_up_q", [HF, D], I32)
    wup_s = p.inp("L0_ffn_w_up_scale", [HF, 32], F32)
    wdn_q = p.inp("L0_ffn_w_down_q", [D, HF], I32)
    wdn_s = p.inp("L0_ffn_w_down_scale", [D, HF // 64], F32)
    ident = p.inp("c_ident", [128, 128], BF16)
    WupT = p.outp("WupT", [NHC, 128, D], BF16)
    WdnT = p.outp("WdnT", [NHC, 128, D], BF16)

    p.push()
    ident_sb = p.sb("id", [128, 128], BF16)
    qrow = [p.sb("qr", [128, D], I32) for _ in range(4)]
    dqr = [p.sb("dqr", [128, D], BF16) for _ in range(4)]
    wus_sb = [p.sb("wus", [128, 32], F32) for _ in range(2)]
    wds_sb = p.sb("wds", [128, ND, HF // 64], F32)
    upstage = [p.sb("ups", [128, D], BF16) for _ in range(2)]
    dnstage = [p.sb("dns", [128, D], BF16) for _ in range(2)]
    tp_ps = [p.ps("tp", [128, 128], BF16) for _ in range(4)]

    s_c = p.sem("d1c", global_=True)
    s_dq, s_tp, s_ev = (p.sem(x) for x in ("dq", "tp", "ev"))
    dld = p.gdsem("d1ld", 4)
    dsc = p.gdsem("d1sc", 2)
    dstg = p.gdsem("d1stg", 2)
    c0, dq0, tp0, ev0 = (x.v for x in (s_c, s_dq, s_tp, s_ev))

    wdn_qv = wdn_q.rearrange("(dc p) h -> p dc h", p=128)

    with nc.Block() as blk:
        @blk.sync
        def _(s):
            s.dma_start(ident_sb[:], ident).then_inc(s_c.h, 16)
            for dc in range(ND):
                s.dma_start(wds_sb[:, dc, :],
                            wdn_s[dc * 128:(dc + 1) * 128, :]).then_inc(s_c.h, 16)
            s.wait_ge(s_c.h, c0 + 16 * (ND + 1))
        s_c.v += 16 * (ND + 1)

    with nc.Block() as blk:
        @blk.sync
        def _(s):
            for hc in range(NHC):
                for j in range(2):
                    r = hc * 2 + j
                    if r >= 4:
                        s.wait_ge(s_dq.h, dq0 + r - 3)
                    if j == 0:
                        src = wup_q[hc * 128:(hc + 1) * 128, :]
                    else:
                        src = wdn_qv[:, :, hc * 128:(hc + 1) * 128]
                    s.dma_start(qrow[r % 4][:], src).then_inc(dld.inc(r), 16)

        @blk.gpsimd
        def _(g):
            def stores(hcs):
                g.wait_ge(s_ev.h, ev0 + hcs * 32 + ND)
                g.dma_start(WupT[hcs], upstage[hcs % 2][:]).then_inc(dstg.inc(hcs), 16)
                g.wait_ge(s_ev.h, ev0 + hcs * 32 + 32)
                g.dma_start(WdnT[hcs], dnstage[hcs % 2][:]).then_inc(dstg.inc(hcs), 16)
            for hc in range(NHC):
                if hc >= 2:
                    g.wait_ge(s_dq.h, dq0 + 2 * hc - 3)
                g.dma_start(wus_sb[hc % 2][:],
                            wup_s[hc * 128:(hc + 1) * 128, :]).then_inc(dsc.inc(hc), 16)
                if hc >= 1:
                    stores(hc - 1)
            stores(NHC - 1)
            g.wait_ge(dstg.h(NHC - 2), dstg.tgt(NHC - 2, 32))
            g.wait_ge(dstg.h(NHC - 1), dstg.tgt(NHC - 1, 32))

        @blk.vector
        def _(v):
            for hc in range(NHC):
                for j in range(2):
                    r = hc * 2 + j
                    v.wait_ge(dld.h(r), dld.tgt(r))
                    if r >= 4:
                        v.wait_ge(s_tp.h, tp0 + (r - 3) * ND)
                    q = qrow[r % 4]
                    d = dqr[r % 4]
                    if j == 0:
                        v.wait_ge(dsc.h(hc), dsc.tgt(hc))
                        for b in range(32):
                            ins = v.tensor_scalar(d[:, b * 64:(b + 1) * 64],
                                                  q[:, b * 64:(b + 1) * 64], -7.5,
                                                  wus_sb[hc % 2][:, b:b + 1],
                                                  ALU.add, ALU.mult)
                        ins.then_inc(s_dq.h, 1)
                    else:
                        for dc in range(ND):
                            for hh in range(2):
                                ins = v.tensor_scalar(
                                    d[:, dc * 128 + hh * 64:dc * 128 + (hh + 1) * 64],
                                    q[:, dc * 128 + hh * 64:dc * 128 + (hh + 1) * 64],
                                    -7.5, wds_sb[:, dc, 2 * hc + hh:2 * hc + hh + 1],
                                    ALU.add, ALU.mult)
                        ins.then_inc(s_dq.h, 1)
                    if r >= 1:
                        _d1_evacs(v, r - 1, s_tp, tp0, s_ev, tp_ps, upstage, dnstage)
            _d1_evacs(v, 2 * NHC - 1, s_tp, tp0, s_ev, tp_ps, upstage, dnstage)
        s_dq.v += 2 * NHC
        s_ev.v += 2 * NHC * ND

        @blk.tensor
        def _(t):
            for hc in range(NHC):
                if hc >= 2:
                    t.wait_ge(dstg.h(hc), dstg.tgt(hc - 2, 32))
                for j in range(2):
                    r = hc * 2 + j
                    t.wait_ge(s_dq.h, dq0 + r + 1)
                    for dc in range(ND):
                        i = r * ND + dc
                        if i >= 4:
                            t.wait_ge(s_ev.h, ev0 + i - 3)
                        t.transpose(tp_ps[i % 4][:],
                                    dqr[r % 4][:, dc * 128:(dc + 1) * 128],
                                    ident_sb[:]).then_inc(s_tp.h, 1)
        s_tp.v += 2 * NHC * ND
    p.pop()
    return


def _d1_evacs(v, r, s_tp, tp0, s_ev, tp_ps, upstage, dnstage):
    hc, j = divmod(r, 2)
    stage = upstage[hc % 2] if j == 0 else dnstage[hc % 2]
    for dc in range(ND):
        i = r * ND + dc
        v.wait_ge(s_tp.h, tp0 + i + 1)
        v.tensor_copy(stage[:, dc * 128:(dc + 1) * 128],
                      tp_ps[i % 4][:]).then_inc(s_ev.h, 1)





def phase_ffn_main(p):
    nc = p.nc
    x1 = p.inp("x1", [T, D], F32)
    WupT = p.inp("WupT", [NHC, 128, D], BF16)
    WdnT = p.inp("WdnT", [NHC, 128, D], BF16)
    bup = p.inp("bup_cols", [128, NHC], F32)
    Aup = p.inp("L0_ffn_w_up_lora_a", [D, RK], F32)
    Bup = p.inp("L0_ffn_w_up_lora_b", [RK, HF], F32)
    Adn = p.inp("L0_ffn_w_down_lora_a", [HF, RK], F32)
    Bdn = p.inp("bdnl_ext", [RK + 1, D], F32)
    onesrow = p.inp("c_onesrow", [1, 512], BF16)
    ident = p.inp("c_ident", [128, 128], BF16)
    ones1 = p.inp("c_ones1", [1, 128], BF16)
    x2out = p.outp("x2", [T, D], F32)

    TB = 512
    NTB = T // TB
    NTC = TB // 128

    p.push()
    ident_sb = p.sb("id", [128, 128], BF16)
    ones1_sb = p.sb("on", [1, 128], BF16)
    x_fm = p.sb("xfm", [128, ND, T], BF16)
    x2_fm = p.sb("x2", [128, NHC, TB], BF16)
    wup_sb = [p.sb("wu", [128, D], BF16) for _ in range(2)]
    wdn_sb = [p.sb("wd", [128, 512], BF16) for _ in range(4)]
    aup_sb = p.sb("aup", [128, ND * RK], BF16)
    bupl_sb = p.sb("bupl", [RK, HF], BF16)
    adn_sb = p.sb("adn", [128, NHC * RK], BF16)
    bdnl_sb = p.sb("bdnl", [RK + 1, D], BF16)
    bup_sb = p.sb("bup", [128, NHC], F32)
    xaup_sb = p.sb("xaup", [RK, T], BF16)
    xadn_sb = p.sb("xadn", [RK + 1, TB], BF16)
    xr_sb = [p.sb("xr", [128, D], F32) for _ in range(4)]
    ssum4 = p.sb("ssum4", [128, 4], F32)
    ssq = p.sb("ssq", [128, 1], F32)
    ssum = p.sb("ssum", [128, 1], F32)
    mean = p.sb("mean", [128, 1], F32)
    var = p.sb("var", [128, 1], F32)
    sd = p.sb("sd", [128, 1], F32)
    rstd = p.sb("rstd", [128, 1], F32)

    s_c = p.sem("d2c", global_=True)
    s_cg = p.sem("d2cg", global_=True)
    with nc.Block() as blk:
        @blk.sync
        def _(s):
            s.dma_start(ident_sb[:], ident).then_inc(s_c.h, 16)
            s.dma_start(ones1_sb[:], ones1).then_inc(s_c.h, 16)
            s.dma_start(bup_sb[:], bup).then_inc(s_c.h, 16)
            s.wait_ge(s_c.h, s_c.v + 48)
        s_c.v += 48

        @blk.gpsimd
        def _(g):
            g.dma_start(xadn_sb[RK:RK + 1, :], onesrow[0:1, 0:TB]).then_inc(s_cg.h, 16)
            g.dma_start(bupl_sb[:], Bup).then_inc(s_cg.h, 16)
            g.dma_start(bdnl_sb[:], Bdn).then_inc(s_cg.h, 16)
            for dc in range(ND):
                g.dma_start(aup_sb[:, dc * RK:(dc + 1) * RK],
                            Aup[dc * 128:(dc + 1) * 128, :]).then_inc(s_cg.h, 16)
            for hc in range(NHC):
                g.dma_start(adn_sb[:, hc * RK:(hc + 1) * RK],
                            Adn[hc * 128:(hc + 1) * 128, :]).then_inc(s_cg.h, 16)
            g.wait_ge(s_cg.h, s_cg.v + 16 * (3 + ND + NHC))
        s_cg.v += 16 * (3 + ND + NHC)

    emit_load_xfm(p, x1, x_fm, ident_sb, bufs=wup_sb)

    p.push()
    xaup_ps = p.ps("xaupp", [RK, T], F32)
    s_u, s_ue = p.sem("u"), p.sem("ue")
    with nc.Block() as blk:
        @blk.tensor
        def _(t):
            last = None
            for dc in range(ND):
                for n in range(4):
                    last = t.matmul(xaup_ps[:, n * 512:(n + 1) * 512],
                                    aup_sb[:, dc * RK:(dc + 1) * RK],
                                    x_fm[:, dc, n * 512:(n + 1) * 512],
                                    start=(dc == 0), stop=(dc == ND - 1))
            last.then_inc(s_u.h, 1)
        s_u.v += 1

        @blk.vector
        def _(v):
            v.wait_ge(s_u.h, s_u.v)
            v.tensor_copy(xaup_sb[:], xaup_ps[:]).then_inc(s_ue.h, 1)
        s_ue.v += 1
    p.pop()

    up_ps = [p.ps("up", [128, TB], F32) for _ in range(2)]
    xa_ps = p.ps("xa", [RK, TB], F32)
    dn_ps = [p.ps("dn", [128, 512], F32) for _ in range(4)]

    s_up, s_rel, s_xa, s_xae, s_dn, s_dnw = (p.sem(x) for x in
        ("up", "rel", "xa", "xae", "dn", "dnw"))
    dwu = p.gdsem("d2wu", 2)
    dwd = p.gdsem("d2wd", 4)
    dxl = p.gdsem("d2xl", 4)
    dxs = p.gdsem("d2xs", 4)
    s_r1, s_sq2, s_v2, s_sd2, s_rc, s_nm = (p.sem(x) for x in
        ("r1", "sq2", "v2", "sd2", "rc", "nm"))

    for tb in range(NTB):
        up0, rel0, xa0 = s_up.v, s_rel.v, s_xa.v
        with nc.Block() as blk:
            @blk.sync
            def _(s):
                for hc in range(NHC):
                    k = tb * NHC + hc
                    if k >= 2:
                        s.wait_ge(s_up.h, k - 1)
                    s.dma_start(wup_sb[hc % 2][:], WupT[hc]).then_inc(dwu.inc(k), 16)

            @blk.tensor
            def _(t):
                t.wait_ge(s_ue.h, s_ue.v)
                for hc in range(NHC):
                    k = tb * NHC + hc
                    t.wait_ge(dwu.h(k), dwu.tgt(k))
                    if hc >= 2:
                        t.wait_ge(s_rel.h, rel0 + hc - 1)
                    last = None
                    for dc in range(ND):
                        last = t.matmul(up_ps[hc % 2][:],
                                        wup_sb[hc % 2][:, dc * 128:(dc + 1) * 128],
                                        x_fm[:, dc, tb * TB:(tb + 1) * TB],
                                        start=(dc == 0), stop=False)
                    last = t.matmul(up_ps[hc % 2][:],
                                    bupl_sb[:, hc * 128:(hc + 1) * 128],
                                    xaup_sb[:, tb * TB:(tb + 1) * TB],
                                    start=False, stop=True)
                    last.then_inc(s_up.h, 1)
                    if hc >= 1:
                        t.wait_ge(s_rel.h, rel0 + hc)
                        t.matmul(xa_ps[:], adn_sb[:, (hc - 1) * RK:hc * RK],
                                 x2_fm[:, hc - 1, :],
                                 start=(hc == 1), stop=False)
                t.wait_ge(s_rel.h, rel0 + NHC)
                t.matmul(xa_ps[:], adn_sb[:, (NHC - 1) * RK:NHC * RK],
                         x2_fm[:, NHC - 1, :],
                         start=False, stop=True).then_inc(s_xa.h, 1)
            s_up.v += NHC
            s_xa.v += 1

            @blk.scalar
            def _(sc):
                for hc in range(NHC):
                    sc.wait_ge(s_up.h, up0 + hc + 1)
                    sc.activation(x2_fm[:, hc, :], up_ps[hc % 2][:], AF.Relu,
                                  bias=bup_sb[:, hc:hc + 1]).then_inc(s_rel.h, 1)
            s_rel.v += NHC

            @blk.vector
            def _(v):
                v.wait_ge(s_xa.h, xa0 + 1)
                v.tensor_copy(xadn_sb[0:RK, :], xa_ps[:]).then_inc(s_xae.h, 1)
            s_xae.v += 1

        dn0, dnw0 = s_dn.v, s_dnw.v
        r10, sq20, v20, sd20, rc0, nm0 = (s.v for s in
            (s_r1, s_sq2, s_v2, s_sd2, s_rc, s_nm))
        with nc.Block() as blk:
            @blk.sync
            def _(s):
                for tc in range(NTC):
                    qcg = tb * NTC + tc
                    if tb > 0:
                        s.wait_ge(dxs.h(qcg), dxs.tgt(qcg - NTC))
                    s.dma_start(xr_sb[tc][:],
                                x1[qcg * 128:(qcg + 1) * 128, :]
                                ).then_inc(dxl.inc(qcg), 16)
                for dh in range(4):
                    for hc in range(NHC):
                        i = dh * NHC + hc
                        k = tb * 4 * NHC + i
                        if k % 2 == 1:
                            continue
                        if k >= 4:
                            s.wait_ge(s_dnw.h, k - 3)
                        s.dma_start(wdn_sb[k % 4][:],
                                    WdnT[hc][:, dh * 512:(dh + 1) * 512]).then_inc(dwd.inc(k), 16)

            @blk.scalar
            def _(sc_e):
                for dh in range(4):
                    for hc in range(NHC):
                        i = dh * NHC + hc
                        k = tb * 4 * NHC + i
                        if k % 2 == 0:
                            continue
                        if k >= 4:
                            sc_e.wait_ge(s_dnw.h, k - 3)
                        sc_e.dma_start(wdn_sb[k % 4][:],
                                       WdnT[hc][:, dh * 512:(dh + 1) * 512]).then_inc(dwd.inc(k), 16)

            @blk.tensor
            def _(t):
                for dh in range(4):
                    if dh > 0:
                        t.wait_ge(s_r1.h, r10 + dh * NTC)
                    for hc in range(NHC):
                        i = dh * NHC + hc
                        k = tb * 4 * NHC + i
                        t.wait_ge(dwd.h(k), dwd.tgt(k))
                        for tc in range(NTC):
                            ins = t.matmul(dn_ps[tc][:],
                                           x2_fm[:, hc, tc * 128:(tc + 1) * 128],
                                           wdn_sb[k % 4][:],
                                           start=(hc == 0), stop=False)
                            if tc == NTC - 1:
                                ins.then_inc(s_dnw.h, 1)
                    t.wait_ge(s_xae.h, s_xae.v)
                    for tc in range(NTC):
                        t.matmul(dn_ps[tc][:], xadn_sb[:, tc * 128:(tc + 1) * 128],
                                 bdnl_sb[:, dh * 512:(dh + 1) * 512],
                                 start=False, stop=True).then_inc(s_dn.h, 1)
            s_dn.v += 4 * NTC
            s_dnw.v += 4 * NHC

            @blk.vector
            def _(v):
                for dh in range(4):
                    for tc in range(NTC):
                        v.wait_ge(s_dn.h, dn0 + dh * NTC + tc + 1)
                        v.wait_ge(dxl.h(tb * NTC + tc), dxl.tgt(tb * NTC + tc))
                        v.scalar_tensor_tensor(
                            xr_sb[tc][:, dh * 512:(dh + 1) * 512],
                            dn_ps[tc][:], 1.0,
                            xr_sb[tc][:, dh * 512:(dh + 1) * 512],
                            ALU.mult, ALU.add,
                            accum_out=ssum4[:, dh:dh + 1]).then_inc(s_r1.h, 1)
            s_r1.v += 16

        with nc.Block() as blk:
            @blk.vector
            def _(v):
                for tc in range(NTC):
                    v.reduce_sum(ssum[:], xr_sb[tc][:], axis=AX.X).then_inc(s_r1.h, 1)
                    v.wait_ge(s_sq2.h, sq20 + tc + 1)
                    v.tensor_scalar_mul(mean[:], ssum[:], INV_D)
                    v.tensor_scalar_mul(var[:], ssq[:], INV_D)
                    v.tensor_mul(rstd[:], mean[:], mean[:])
                    v.tensor_sub(var[:], var[:], rstd[:])
                    v.tensor_scalar_add(var[:], var[:], LN_EPS).then_inc(s_v2.h, 1)
                    v.wait_ge(s_sd2.h, sd20 + tc + 1)
                    v.reciprocal(rstd[:], sd[:]).then_inc(s_rc.h, 1)
                    v.wait_ge(s_rc.h, rc0 + tc + 1)
                    v.tensor_scalar(xr_sb[tc][:], xr_sb[tc][:], mean[:], rstd[:],
                                    ALU.subtract, ALU.mult).then_inc(s_nm.h, 1)
            s_r1.v += NTC
            s_v2.v += NTC
            s_rc.v += NTC
            s_nm.v += NTC

            @blk.scalar
            def _(sc):
                for tc in range(NTC):
                    sc.wait_ge(s_r1.h, r10 + 16 + tc + 1)
                    sc.activation(x2_fm[:, 0:4, :], xr_sb[tc][:], AF.Square,
                                  accum_out=ssq[:]).then_inc(s_sq2.h, 1)
                    sc.wait_ge(s_v2.h, v20 + tc + 1)
                    sc.activation(sd[:], var[:], AF.Sqrt).then_inc(s_sd2.h, 1)
            s_sq2.v += NTC
            s_sd2.v += NTC

            @blk.gpsimd
            def _(g):
                for tc in range(NTC):
                    g.wait_ge(s_nm.h, nm0 + tc + 1)
                    qc = tb * NTC + tc
                    g.dma_start(x2out[qc * 128:(qc + 1) * 128, :],
                                xr_sb[tc][:]).then_inc(dxs.inc(qc), 16)
                if tb == NTB - 1:
                    for tc in range(NTC):
                        g.wait_ge(dxs.h(tc), dxs.tgt((NTB - 1) * NTC + tc))
    p.pop()
    return



def phase_qkv1(p):
    nc = p.nc
    x2 = p.inp("x2", [T, D], F32)
    Wqkv = p.inp("L1_attn_W_qkv", [3 * D, D], F32)
    ident = p.inp("c_ident", [128, 128], BF16)
    Qo = p.outp("Q1_fm", [D, T], BF16)
    Ko = p.outp("K1_fm", [D, T], BF16)
    Vo = p.outp("V1_tm", [T, D], BF16)

    p.push()
    ident_sb = p.sb("id", [128, 128], BF16)
    x_fm = p.sb("xfm", [128, ND, T], BF16)
    wtmp = [p.sb("wt", [128, D], BF16) for _ in range(2)]
    wt_sb = p.sb("wts", [128, ND, 128], BF16)
    WTv_sb = p.sb("wtv", [128, ND, D], BF16)
    ev_sb = [p.sb("ev", [128, T], BF16) for _ in range(2)]
    wt_ps = [p.ps("wtp", [128, 128], BF16) for _ in range(2)]
    pj_ps = p.ps("pj", [128, T], F32)

    s_c = p.sem("ec", global_=True)
    with nc.Block() as blk:
        @blk.sync
        def _(s):
            s.dma_start(ident_sb[:], ident).then_inc(s_c.h, 16)
            s.wait_ge(s_c.h, s_c.v + 16)
        s_c.v += 16

    emit_load_xfm(p, x2, x_fm, ident_sb)

    s_tt, s_te = p.sem("tt"), p.sem("te")
    dwv = p.gdsem("ewv", 2)
    tt0, te0 = s_tt.v, s_te.v
    with nc.Block() as blk:
        @blk.gpsimd
        def _(g):
            for i, ec in enumerate(range(2 * ND, 3 * ND)):
                if i >= 2:
                    g.wait_ge(s_te.h, te0 + (i - 1) * ND)
                g.dma_start(wtmp[i % 2][:], Wqkv[ec * 128:(ec + 1) * 128, :]).then_inc(dwv.inc(i), 16)

        @blk.tensor
        def _(t):
            for i in range(ND):
                t.wait_ge(dwv.h(i), dwv.tgt(i))
                for dc in range(ND):
                    j = i * ND + dc
                    if j >= 2:
                        t.wait_ge(s_te.h, te0 + j - 1)
                    t.transpose(wt_ps[j % 2][:], wtmp[i % 2][:, dc * 128:(dc + 1) * 128],
                                ident_sb[:]).then_inc(s_tt.h, 1)
        s_tt.v += ND * ND

        @blk.vector
        def _(v):
            for i in range(ND):
                for dc in range(ND):
                    j = i * ND + dc
                    v.wait_ge(s_tt.h, tt0 + j + 1)
                    v.tensor_copy(WTv_sb[:, dc, i * 128:(i + 1) * 128],
                                  wt_ps[j % 2][:]).then_inc(s_te.h, 1)
        s_te.v += ND * ND

    s_vm, s_ve = p.sem("vm"), p.sem("ve")
    dvs = p.gdsem("evs", 2)
    vm0, ve0 = s_vm.v, s_ve.v
    with nc.Block() as blk:
        @blk.tensor
        def _(t):
            for tc in range(NT):
                if tc > 0:
                    t.wait_ge(s_ve.h, ve0 + tc)
                last = None
                for dc in range(ND):
                    for n in range(4):
                        last = t.matmul(pj_ps[:, n * 512:(n + 1) * 512],
                                        x_fm[:, dc, tc * 128:(tc + 1) * 128],
                                        WTv_sb[:, dc, n * 512:(n + 1) * 512],
                                        start=(dc == 0), stop=(dc == ND - 1))
                last.then_inc(s_vm.h, 1)
        s_vm.v += NT

        @blk.vector
        def _(v):
            for tc in range(NT):
                v.wait_ge(s_vm.h, vm0 + tc + 1)
                if tc >= 2:
                    v.wait_ge(dvs.h(tc), dvs.tgt(tc - 2))
                v.tensor_copy(ev_sb[tc % 2][:], pj_ps[:]).then_inc(s_ve.h, 1)
        s_ve.v += NT

        @blk.gpsimd
        def _(g):
            for tc in range(NT):
                g.wait_ge(s_ve.h, ve0 + tc + 1)
                g.dma_start(Vo[tc * 128:(tc + 1) * 128, :], ev_sb[tc % 2][:]).then_inc(dvs.inc(tc), 16)
            g.wait_ge(dvs.h(NT - 2), dvs.tgt(NT - 2))
            g.wait_ge(dvs.h(NT - 1), dvs.tgt(NT - 1))

    s_t2, s_e2, s_qm, s_qe = (p.sem(x) for x in ("t2", "e2", "qm", "qe"))
    dw2 = p.gdsem("ew2", 2)
    dqs = p.gdsem("eqs", 2)
    t20, e20, qm0, qe0 = (s.v for s in (s_t2, s_e2, s_qm, s_qe))
    with nc.Block() as blk:
        @blk.gpsimd
        def _(g):
            def qstore(e):
                g.wait_ge(s_qe.h, qe0 + e + 1)
                dst = Qo if e < ND else Ko
                r = e % ND
                g.dma_start(dst[r * 128:(r + 1) * 128, :], ev_sb[e % 2][:]).then_inc(dqs.inc(e), 16)
            for ec in range(2 * ND):
                if ec >= 2:
                    g.wait_ge(s_qm.h, qm0 + ec - 1)
                g.dma_start(wtmp[ec % 2][:], Wqkv[ec * 128:(ec + 1) * 128, :]).then_inc(dw2.inc(ec), 16)
                if ec >= 2:
                    qstore(ec - 2)
            qstore(2 * ND - 2)
            qstore(2 * ND - 1)
            g.wait_ge(dqs.h(2 * ND - 2), dqs.tgt(2 * ND - 2))
            g.wait_ge(dqs.h(2 * ND - 1), dqs.tgt(2 * ND - 1))

        @blk.tensor
        def _(t):
            for ec in range(2 * ND):
                t.wait_ge(dw2.h(ec), dw2.tgt(ec))
                for dc in range(ND):
                    j = ec * ND + dc
                    if j >= 2:
                        t.wait_ge(s_e2.h, e20 + j - 1)
                    t.transpose(wt_ps[j % 2][:], wtmp[ec % 2][:, dc * 128:(dc + 1) * 128],
                                ident_sb[:]).then_inc(s_t2.h, 1)
                t.wait_ge(s_e2.h, e20 + (ec + 1) * ND)
                if ec > 0:
                    t.wait_ge(s_qe.h, qe0 + ec)
                last = None
                for dc in range(ND):
                    for n in range(4):
                        last = t.matmul(pj_ps[:, n * 512:(n + 1) * 512],
                                        wt_sb[:, dc, :],
                                        x_fm[:, dc, n * 512:(n + 1) * 512],
                                        start=(dc == 0), stop=(dc == ND - 1))
                last.then_inc(s_qm.h, 1)
        s_t2.v += 2 * ND * ND
        s_qm.v += 2 * ND

        @blk.vector
        def _(v):
            for ec in range(2 * ND):
                for dc in range(ND):
                    j = ec * ND + dc
                    v.wait_ge(s_t2.h, t20 + j + 1)
                    v.tensor_copy(wt_sb[:, dc, :], wt_ps[j % 2][:]).then_inc(s_e2.h, 1)
                v.wait_ge(s_qm.h, qm0 + ec + 1)
                if ec >= 2:
                    v.wait_ge(dqs.h(ec), dqs.tgt(ec - 2))
                v.tensor_copy(ev_sb[ec % 2][:], pj_ps[:]).then_inc(s_qe.h, 1)
        s_e2.v += 2 * ND * ND
        s_qe.v += 2 * ND
    p.pop()
    return



def phase_bffn(p, xres_name="x3", out_name="x4"):
    nc = p.nc
    x3 = p.inp(xres_name, [T, D], F32)
    Wt = p.inp("L1_ffn_weight", [D, D], F32)
    brow = p.inp("bffn_row", [1, D], F32)
    ident = p.inp("c_ident", [128, 128], BF16)
    ones1 = p.inp("c_ones1", [1, 128], BF16)
    OUT = p.outp(out_name, [T, D], F32)

    p.push()
    ident_sb = p.sb("id", [128, 128], BF16)
    ones1_sb = p.sb("on", [1, 128], BF16)
    brow_sb = p.sb("br", [1, D], BF16)
    x_fm = p.sb("xfm", [128, ND, T], BF16)
    bwT_sb = p.sb("bwt", [128, ND, D], BF16)
    wtmp = [p.sb("wt", [128, D], BF16) for _ in range(2)]
    sgn_sb = [p.sb("sg", [128, D], BF16) for _ in range(2)]
    asum = p.sb("as", [128, 1], F32)
    asum2 = p.sb("as2", [128, 1], F32)
    alpha = p.sb("al", [128, 1], F32)
    wt_ps = [p.ps("wtp", [128, 128], BF16) for _ in range(2)]
    ps = p.ps("pj", [128, D], F32)
    xr = [p.sb("xr", [128, D], F32) for _ in range(2)]
    xo = [p.sb("xo", [128, D], F32) for _ in range(2)]
    ssum = p.sb("ssum", [128, 1], F32)
    ssq = p.sb("ssq", [128, 1], F32)
    mean = p.sb("mean", [128, 1], F32)
    var = p.sb("var", [128, 1], F32)
    sd = p.sb("sd", [128, 1], F32)
    rstd = p.sb("rstd", [128, 1], F32)

    s_c = p.sem("hc2", global_=True)
    s_cg = p.sem("hcg", global_=True)
    with nc.Block() as blk:
        @blk.sync
        def _(s):
            s.dma_start(ident_sb[:], ident).then_inc(s_c.h, 16)
            s.dma_start(ones1_sb[:], ones1).then_inc(s_c.h, 16)
            s.wait_ge(s_c.h, s_c.v + 32)
        s_c.v += 32

        @blk.gpsimd
        def _(g):
            g.dma_start(brow_sb[:], brow).then_inc(s_cg.h, 16)
            g.wait_ge(s_cg.h, s_cg.v + 16)
        s_cg.v += 16

    emit_load_xfm(p, x3, x_fm, ident_sb)

    s_al, s_sg, s_bw, s_tt, s_te, s_ab = (p.sem(x) for x in
        ("al", "sg", "bw", "tt", "te", "ab"))
    dwl = p.gdsem("hwl", 2)
    al0, sg0, bw0, tt0, te0, ab0 = (s.v for s in (s_al, s_sg, s_bw, s_tt, s_te, s_ab))
    with nc.Block() as blk:
        @blk.gpsimd
        def _(g):
            for oc in range(ND):
                if oc >= 2:
                    g.wait_ge(s_bw.h, bw0 + oc - 1)
                g.dma_start(wtmp[oc % 2][:], Wt[oc * 128:(oc + 1) * 128, :]).then_inc(dwl.inc(oc), 16)

        @blk.vector
        def _(v):
            for oc in range(ND):
                v.wait_ge(s_ab.h, ab0 + oc + 1)
                v.tensor_scalar_mul(alpha[:], asum2[:], INV_D).then_inc(s_al.h, 1)
                v.wait_ge(s_sg.h, sg0 + oc + 1)
                v.tensor_scalar_mul(sgn_sb[oc % 2][:], sgn_sb[oc % 2][:],
                                    alpha[:]).then_inc(s_bw.h, 1)
                for dc in range(ND):
                    j = oc * ND + dc
                    v.wait_ge(s_tt.h, tt0 + j + 1)
                    v.tensor_copy(bwT_sb[:, dc, oc * 128:(oc + 1) * 128],
                                  wt_ps[j % 2][:]).then_inc(s_te.h, 1)
        s_ab.v += ND
        s_al.v += ND
        s_bw.v += ND
        s_te.v += ND * ND

        @blk.scalar
        def _(sc):
            for oc in range(ND):
                sc.wait_ge(dwl.h(oc), dwl.tgt(oc))
                if oc >= 2:
                    sc.wait_ge(s_tt.h, tt0 + (oc - 1) * ND)
                sc.activation(sgn_sb[oc % 2][:], wtmp[oc % 2][:], AF.Abs,
                              accum_out=asum[:])
                sc.copy(asum2[:], asum[:]).then_inc(s_ab.h, 1)
                sc.sign(sgn_sb[oc % 2][:], wtmp[oc % 2][:]).then_inc(s_sg.h, 1)
        s_sg.v += ND

        @blk.tensor
        def _(t):
            for oc in range(ND):
                t.wait_ge(s_bw.h, bw0 + oc + 1)
                for dc in range(ND):
                    j = oc * ND + dc
                    if j >= 2:
                        t.wait_ge(s_te.h, te0 + j - 1)
                    t.transpose(wt_ps[j % 2][:], sgn_sb[oc % 2][:, dc * 128:(dc + 1) * 128],
                                ident_sb[:]).then_inc(s_tt.h, 1)
        s_tt.v += ND * ND

    _emit_proj_res_ln(p, x_fm, bwT_sb, (ones1_sb, brow_sb), x3, OUT, ps,
                      xr, xo, ssum, ssq, mean, var, sd, rstd)
    p.pop()
    return



NCORES = 8

INTERNAL_NAMES = {"Q_fm", "K_fm", "V_tm", "O_fm", "x1", "WupT", "WdnT", "x2",
                  "Q1_fm", "K1_fm", "V1_tm", "O1_fm", "x3"}


def build_fused(n_phases=9, out_name=None):
    p = Prog()
    p.internal = set(INTERNAL_NAMES)
    if out_name:
        p.internal.discard(out_name)
    phases = [
        (phase_qkv0, {}),
        (phase_attn, {"causal": False}),
        (phase_proj_ln, {"layer": 0, "bname": "bo_row",
                         "xres_name": "x_in", "xout_name": "x1"}),
        (phase_ffn_prep, {}),
        (phase_ffn_main, {}),
        (phase_qkv1, {}),
        (phase_attn, {"causal": True, "qn": "Q1_fm", "kn": "K1_fm",
                      "vn": "V1_tm", "on": "O1_fm"}),
        (phase_proj_ln, {"layer": 1, "on": "O1_fm", "wname": "L1_attn_W_o",
                         "bname": None, "xres_name": "x2", "xout_name": "x3",
                         "w_transposed": True}),
        (phase_bffn, {}),
    ]
    for fn, kw in phases[:n_phases]:
        p.push()
        p.phase_sems = []
        fn(p, **kw)
        used = list(p.phase_sems)
        p.pop()
        with p.nc.Block() as blk:
            @blk.gpsimd
            def _(g):
                for h in used:
                    g.sem_clear(h)
    p.close()
    return p


def host_maps(inputs, cores):
    consts = host_consts()
    x_full = np.ascontiguousarray(inputs["L0_attn_x"])
    shared = {**consts,
              "L0_attn_Wq": inputs["L0_attn_Wq"], "L0_attn_Wk": inputs["L0_attn_Wk"],
              "L0_attn_Wv": inputs["L0_attn_Wv"], "L0_attn_Wo": inputs["L0_attn_Wo"],
              "bq_cols": np.ascontiguousarray(inputs["L0_attn_bq"].reshape(ND, 128).T),
              "bk_cols": np.ascontiguousarray(inputs["L0_attn_bk"].reshape(ND, 128).T),
              "bv_row": np.ascontiguousarray(inputs["L0_attn_bv"].reshape(1, D)),
              "bo_row": np.ascontiguousarray(inputs["L0_attn_bo"].reshape(1, D)),
              "L0_ffn_w_up_q": inputs["L0_ffn_w_up_q"],
              "L0_ffn_w_up_scale": inputs["L0_ffn_w_up_scale"],
              "L0_ffn_w_down_q": inputs["L0_ffn_w_down_q"],
              "L0_ffn_w_down_scale": inputs["L0_ffn_w_down_scale"],
              "bup_cols": np.ascontiguousarray(inputs["L0_ffn_b_up"].reshape(NHC, 128).T),
              "bdnl_ext": np.ascontiguousarray(np.concatenate(
                  [inputs["L0_ffn_w_down_lora_b"],
                   inputs["L0_ffn_b_down"].reshape(1, D)], axis=0)),
              "L0_ffn_w_up_lora_a": inputs["L0_ffn_w_up_lora_a"],
              "L0_ffn_w_up_lora_b": inputs["L0_ffn_w_up_lora_b"],
              "L0_ffn_w_down_lora_a": inputs["L0_ffn_w_down_lora_a"],
              "L1_attn_W_qkv": inputs["L1_attn_W_qkv"],
              "L1_attn_W_o": inputs["L1_attn_W_o"],
              "L1_ffn_weight": inputs["L1_ffn_weight"],
              "bffn_row": np.ascontiguousarray(inputs["L1_ffn_bias"].reshape(1, D))}
    maps = []
    for b in cores:
        m = dict(shared)
        m["x_in"] = np.ascontiguousarray(x_full[b])
        maps.append(m)
    return maps


def kernel(**inputs):
    cores = list(range(NCORES))
    p = build_fused()
    maps = host_maps(inputs, cores)
    res = run_bass_kernel_spmd(p.nc, maps, core_ids=cores)
    return np.stack([res.results[b]["x4"] for b in cores], axis=0)


def _launch(phase_fn, kwargs, per_core_maps, core_ids):
    p = Prog()
    phase_fn(p, **kwargs)
    p.close()
    res = run_bass_kernel_spmd(p.nc, per_core_maps, core_ids=core_ids)
    return res.results


def kernel_multilaunch(**inputs):
    consts = host_consts()
    cores = list(range(NCORES))
    x_full = np.ascontiguousarray(inputs["L0_attn_x"])

    def rep(d):
        return [dict(d) for _ in cores]

    def percore(results, key):
        return [r[key] for r in results]

    bq = inputs["L0_attn_bq"]
    bk = inputs["L0_attn_bk"]
    bv = inputs["L0_attn_bv"]
    bo = inputs["L0_attn_bo"]

    maps = rep({**consts,
                "L0_attn_Wq": inputs["L0_attn_Wq"], "L0_attn_Wk": inputs["L0_attn_Wk"],
                "L0_attn_Wv": inputs["L0_attn_Wv"],
                "bq_cols": np.ascontiguousarray(bq.reshape(ND, 128).T),
                "bk_cols": np.ascontiguousarray(bk.reshape(ND, 128).T),
                "bv_row": np.ascontiguousarray(bv.reshape(1, D))})
    for b in cores:
        maps[b]["x_in"] = np.ascontiguousarray(x_full[b])
    rA = _launch(phase_qkv0, {}, maps, cores)

    maps = rep(consts)
    for b in cores:
        maps[b].update(Q_fm=rA[b]["Q_fm"], K_fm=rA[b]["K_fm"], V_tm=rA[b]["V_tm"])
    rB = _launch(phase_attn, {"causal": False}, maps, cores)

    maps = rep({**consts, "L0_attn_Wo": inputs["L0_attn_Wo"],
                "bo_row": np.ascontiguousarray(bo.reshape(1, D))})
    for b in cores:
        maps[b].update(O_fm=rB[b]["O_fm"], x_in=np.ascontiguousarray(x_full[b]))
    rC = _launch(phase_proj_ln, {"layer": 0, "bname": "bo_row",
                                 "xres_name": "x_in", "xout_name": "x1"}, maps, cores)

    maps = rep({**consts,
                "L0_ffn_w_up_q": inputs["L0_ffn_w_up_q"],
                "L0_ffn_w_up_scale": inputs["L0_ffn_w_up_scale"],
                "L0_ffn_w_down_q": inputs["L0_ffn_w_down_q"],
                "L0_ffn_w_down_scale": inputs["L0_ffn_w_down_scale"]})
    rD1 = _launch(phase_ffn_prep, {}, maps, cores)

    maps = rep({**consts,
                "bup_cols": np.ascontiguousarray(inputs["L0_ffn_b_up"].reshape(NHC, 128).T),
                "bdnl_ext": np.ascontiguousarray(np.concatenate(
                    [inputs["L0_ffn_w_down_lora_b"],
                     inputs["L0_ffn_b_down"].reshape(1, D)], axis=0)),
                "L0_ffn_w_up_lora_a": inputs["L0_ffn_w_up_lora_a"],
                "L0_ffn_w_up_lora_b": inputs["L0_ffn_w_up_lora_b"],
                "L0_ffn_w_down_lora_a": inputs["L0_ffn_w_down_lora_a"]})
    for b in cores:
        maps[b].update(x1=rC[b]["x1"], WupT=rD1[b]["WupT"], WdnT=rD1[b]["WdnT"])
    rD2 = _launch(phase_ffn_main, {}, maps, cores)

    maps = rep({**consts, "L1_attn_W_qkv": inputs["L1_attn_W_qkv"]})
    for b in cores:
        maps[b]["x2"] = rD2[b]["x2"]
    rE = _launch(phase_qkv1, {}, maps, cores)

    maps = rep(consts)
    for b in cores:
        maps[b].update(Q1_fm=rE[b]["Q1_fm"], K1_fm=rE[b]["K1_fm"], V1_tm=rE[b]["V1_tm"])
    rF = _launch(phase_attn, {"causal": True, "qn": "Q1_fm", "kn": "K1_fm",
                              "vn": "V1_tm", "on": "O1_fm"}, maps, cores)

    maps = rep({**consts, "L1_attn_W_o": inputs["L1_attn_W_o"]})
    for b in cores:
        maps[b].update(O1_fm=rF[b]["O1_fm"], x2=rD2[b]["x2"])
    rG = _launch(phase_proj_ln, {"layer": 1, "on": "O1_fm", "wname": "L1_attn_W_o",
                                 "bname": None, "xres_name": "x2", "xout_name": "x3",
                                 "w_transposed": True}, maps, cores)

    maps = rep({**consts, "L1_ffn_weight": inputs["L1_ffn_weight"],
                "bffn_row": np.ascontiguousarray(inputs["L1_ffn_bias"].reshape(1, D))})
    for b in cores:
        maps[b]["x3"] = rG[b]["x3"]
    rH = _launch(phase_bffn, {}, maps, cores)

    out = np.stack([rH[b]["x4"] for b in cores], axis=0)
    return out


# version 07 (final)
# speedup vs baseline: 1.5606x; 1.2578x over previous
import math
import numpy as np
import ml_dtypes
from contextlib import ExitStack

import concourse.bass as bass
import concourse.mybir as mybir
from concourse.bass_utils import run_bass_kernel_spmd

F32 = mybir.dt.float32
BF16 = mybir.dt.bfloat16
I32 = mybir.dt.int32
AF = mybir.ActivationFunctionType
ALU = mybir.AluOpType
AX = mybir.AxisListType

T = 2048
D = 2048
NH = 16
HS = 128
HF = 8192
RK = 16
NT = T // 128
ND = D // 128
NHC = HF // 128
SM_SCALE = 1.0 / math.sqrt(HS)
LN_EPS = 1e-5
INV_D = 1.0 / D

BF = ml_dtypes.bfloat16


class Sem:
    __slots__ = ("h", "v")

    def __init__(self, h):
        self.h = h
        self.v = 0


class DSem:

    def __init__(self, p, name, n):
        self.n = n
        self.s = [p.sem(f"{name}_sl{i}", global_=True) for i in range(n)]
        self.base = [x.v for x in self.s]

    def rebase(self):
        self.base = [x.v for x in self.s]

    def inc(self, k, amt=16):
        ss = self.s[k % self.n]
        ss.v += amt
        return ss.h

    def h(self, k):
        return self.s[k % self.n].h

    def tgt(self, k, amt=16):
        return self.base[k % self.n] + amt * (k // self.n + 1)


class Prog:
    def __init__(self, name="prog"):
        self.nc = bass.Bass("TRN2", target_bir_lowering=False,
                            detect_race_conditions=False)
        self.sem_st = ExitStack()
        self.stacks = [ExitStack()]
        self._n = 0

    def uid(self):
        self._n += 1
        return self._n

    def sem(self, name, global_=False):
        if global_:
            if not hasattr(self, "_gsems"):
                self._gsems = {}
            if name not in self._gsems:
                h = self.stacks[0].enter_context(self.nc.semaphore(name))
                self._gsems[name] = Sem(h)
            return self._gsems[name]
        h = self.stacks[-1].enter_context(self.nc.semaphore(f"{name}{self.uid()}"))
        if not hasattr(self, "phase_sems"):
            self.phase_sems = []
        self.phase_sems.append(h)
        return Sem(h)

    def gdsem(self, name, n):
        if not hasattr(self, "_gds"):
            self._gds = {}
        if name not in self._gds:
            self._gds[name] = DSem(self, name, n)
        d = self._gds[name]
        assert d.n == n
        d.rebase()
        return d

    def sb(self, name, shape, dt):
        return self.stacks[-1].enter_context(
            self.nc.sbuf_tensor(f"{name}{self.uid()}", shape, dt))

    def ps(self, name, shape, dt):
        return self.stacks[-1].enter_context(
            self.nc.psum_tensor(f"{name}{self.uid()}", shape, dt))

    def push(self):
        self.stacks.append(ExitStack())

    def pop(self):
        self.stacks.pop().close()

    def inp(self, name, shape, dt):
        return self._dram(name, shape, dt, "ExternalInput")

    def outp(self, name, shape, dt):
        return self._dram(name, shape, dt, "ExternalOutput")

    def _dram(self, name, shape, dt, kind):
        if not hasattr(self, "_dram_cache"):
            self._dram_cache = {}
        if name in self._dram_cache:
            return self._dram_cache[name]
        if getattr(self, "internal", None) and name in self.internal:
            kind = "Internal"
        ap = self.nc.dram_tensor(name, shape, dt, kind=kind).ap()
        self._dram_cache[name] = ap
        return ap

    def close(self):
        while self.stacks:
            self.stacks.pop().close()
        self.sem_st.close()



def host_consts():
    c = {}
    c["c_ident"] = np.eye(128, dtype=BF)
    c["c_tril"] = np.tril(np.ones((128, 128), dtype=np.float32)).astype(BF)
    triu_neg = np.triu(np.full((128, 128), -1e30, dtype=np.float32), k=1)
    c["c_triuneg"] = triu_neg
    c["c_ones1"] = np.ones((1, 128), dtype=BF)
    c["c_onesrow"] = np.ones((1, 512), dtype=BF)
    t = np.arange(T, dtype=np.float32)
    inv_freq = (1.0 / (10000.0 ** (np.arange(0, HS, 2, dtype=np.float32) / HS))).astype(np.float32)
    ang = t[None, :] * inv_freq[:, None]
    cos = np.cos(ang).astype(np.float32)
    sin = np.sin(ang).astype(np.float32)
    C = np.empty((128, T), dtype=np.float32)
    S = np.empty((128, T), dtype=np.float32)
    C[0::2] = cos
    C[1::2] = cos
    S[0::2] = -sin
    S[1::2] = sin
    c["c_ropeC"] = C
    c["c_ropeS"] = S
    return c


SWAP_MASK = []
for _i in range(16):
    SWAP_MASK += [2 * _i + 1, 2 * _i]



def emit_load_xfm(p, x_dram, x_fm, ident_sb, bufs=None):
    nc = p.nc
    p.push()
    xbf = bufs if bufs is not None else [p.sb("xbf", [128, D], BF16) for _ in range(2)]
    pt = [p.ps("xtp", [128, 128], BF16) for _ in range(2)]
    dld = p.gdsem("xld", 2)
    s_tp, s_ev = p.sem("xtp"), p.sem("xev")
    tp0, ev0 = s_tp.v, s_ev.v
    with nc.Block() as blk:
        @blk.gpsimd
        def _(g):
            for tc in range(NT):
                if tc >= 2:
                    g.wait_ge(s_ev.h, ev0 + (tc - 1) * ND)
                g.dma_start(xbf[tc % 2][:], x_dram[tc * 128:(tc + 1) * 128, :]).then_inc(dld.inc(tc), 16)

        @blk.tensor
        def _(t):
            for tc in range(NT):
                t.wait_ge(dld.h(tc), dld.tgt(tc))
                for dc in range(ND):
                    i = tc * ND + dc
                    if i >= 2:
                        t.wait_ge(s_ev.h, ev0 + i - 1)
                    t.transpose(pt[i % 2][:], xbf[tc % 2][:, dc * 128:(dc + 1) * 128],
                                ident_sb[:]).then_inc(s_tp.h, 1)
        s_tp.v += NT * ND

        @blk.vector
        def _(v):
            for tc in range(NT):
                for dc in range(ND):
                    i = tc * ND + dc
                    v.wait_ge(s_tp.h, tp0 + i + 1)
                    v.tensor_copy(x_fm[:, dc, tc * 128:(tc + 1) * 128], pt[i % 2][:]).then_inc(s_ev.h, 1)
        s_ev.v += NT * ND
    p.pop()


def emit_res_ln(p, blk_engines, delta_ps, xres_sb, xout_sb, tmp):
    raise NotImplementedError



def phase_qkv0(p):
    nc = p.nc
    x = p.inp("x_in", [T, D], F32)
    Wq = p.inp("L0_attn_Wq", [D, D], F32)
    Wk = p.inp("L0_attn_Wk", [D, D], F32)
    Wv = p.inp("L0_attn_Wv", [D, D], F32)
    bq = p.inp("bq_cols", [128, ND], F32)
    bk = p.inp("bk_cols", [128, ND], F32)
    bv = p.inp("bv_row", [1, D], F32)
    ident = p.inp("c_ident", [128, 128], BF16)
    ropeC = p.inp("c_ropeC", [128, T], F32)
    ropeS = p.inp("c_ropeS", [128, T], F32)
    ones1 = p.inp("c_ones1", [1, 128], BF16)
    Qo = p.outp("Q_fm", [D, T], BF16)
    Ko = p.outp("K_fm", [D, T], BF16)
    Vo = p.outp("V_tm", [T, D], BF16)

    ident_sb = p.sb("ident", [128, 128], BF16)
    ropeC_sb = p.sb("ropeC", [128, T], F32)
    ropeS_sb = p.sb("ropeS", [128, T], F32)
    ones1_sb = p.sb("ones1", [1, 128], BF16)
    bq_sb = p.sb("bq", [128, ND], F32)
    bk_sb = p.sb("bk", [128, ND], F32)
    bv_sb = p.sb("bv", [1, D], BF16)
    x_fm = p.sb("xfm", [128, ND, T], BF16)
    w_sb = p.sb("wsb", [128, ND, D], BF16)

    s_c = p.sem("qk0c", global_=True)
    s_cg = p.sem("qk0cg", global_=True)
    c0 = s_c.v
    with nc.Block() as blk:
        @blk.sync
        def _(s):
            s.dma_start(ident_sb[:], ident).then_inc(s_c.h, 16)
            s.dma_start(ropeC_sb[:], ropeC).then_inc(s_c.h, 16)
            s.dma_start(ropeS_sb[:], ropeS).then_inc(s_c.h, 16)
            s.dma_start(ones1_sb[:], ones1).then_inc(s_c.h, 16)
            s.dma_start(bq_sb[:], bq).then_inc(s_c.h, 16)
            s.dma_start(bk_sb[:], bk).then_inc(s_c.h, 16)
            s.wait_ge(s_c.h, c0 + 96)
        s_c.v += 96

        @blk.gpsimd
        def _(g):
            g.dma_start(bv_sb[:], bv).then_inc(s_cg.h, 16)
            g.wait_ge(s_cg.h, s_cg.v + 16)
        s_cg.v += 16

    emit_load_xfm(p, x, x_fm, ident_sb)

    proj_ps = p.ps("pj", [128, T], F32)
    for W, bias_cols, OUT, roped in ((Wq, bq_sb, Qo, True), (Wk, bk_sb, Ko, True)):
        _emit_proj_fm(p, proj_ps, W, w_sb, x_fm, bias_cols, OUT,
                      rope=(ropeC_sb, ropeS_sb) if roped else None)

    _emit_proj_tm_v(p, proj_ps, Wv, w_sb, x_fm, bv_sb, ones1_sb, Vo)
    return ["x_in", "L0_attn_Wq", "L0_attn_Wk", "L0_attn_Wv", "bq_cols",
            "bk_cols", "bv_row", "c_ident", "c_ropeC", "c_ropeS", "c_ones1"]


def _emit_proj_fm(p, ps, W, w_sb, x_fm, bias_cols, OUT, rope=None):
    nc = p.nc
    p.push()
    raw = [p.sb("qraw", [128, T], BF16) for _ in range(2)]
    sw = p.sb("qsw", [128, T], BF16)
    t1 = p.sb("qt1", [128, T], BF16)
    out_sb = [p.sb("qout", [128, T], BF16) for _ in range(2)]
    s_w = p.sem("pfw", global_=True)
    s_mm, s_ev, s_rp = (p.sem(x) for x in ("mm", "ev", "rp"))
    dst = p.gdsem("pfst", 2)
    w0, mm0, ev0, rp0 = s_w.v, s_mm.v, s_ev.v, s_rp.v
    with nc.Block() as blk:
        @blk.gpsimd
        def _(g):
            for dc in range(ND):
                g.dma_start(w_sb[:, dc, :], W[dc * 128:(dc + 1) * 128, :]).then_inc(s_w.h, 16)
            for ec in range(ND):
                g.wait_ge(s_rp.h, rp0 + ec + 1)
                g.dma_start(OUT[ec * 128:(ec + 1) * 128, :], out_sb[ec % 2][:]).then_inc(dst.inc(ec), 16)
            g.wait_ge(dst.h(ND - 2), dst.tgt(ND - 2))
            g.wait_ge(dst.h(ND - 1), dst.tgt(ND - 1))
        s_w.v += 16 * ND

        @blk.tensor
        def _(t):
            t.wait_ge(s_w.h, w0 + 16 * ND)
            for ec in range(ND):
                if ec > 0:
                    t.wait_ge(s_ev.h, ev0 + ec)
                for dc in range(ND):
                    for n in range(4):
                        ins = t.matmul(ps[:, n * 512:(n + 1) * 512],
                                       w_sb[:, dc, ec * 128:(ec + 1) * 128],
                                       x_fm[:, dc, n * 512:(n + 1) * 512],
                                       start=(dc == 0), stop=(dc == ND - 1))
                        if dc == ND - 1 and n == 3:
                            ins.then_inc(s_mm.h, 1)
        s_mm.v += ND

        @blk.scalar
        def _(sc):
            for ec in range(ND):
                sc.wait_ge(s_mm.h, mm0 + ec + 1)
                if ec >= 2:
                    sc.wait_ge(s_rp.h, rp0 + ec - 1)
                sc.activation(raw[ec % 2][:], ps[:], AF.Identity,
                              bias=bias_cols[:, ec:ec + 1], scale=1.0).then_inc(s_ev.h, 1)
        s_ev.v += ND

        @blk.vector
        def _(v):
            C, S = rope
            for ec in range(ND):
                v.wait_ge(s_ev.h, ev0 + ec + 1)
                if ec >= 2:
                    v.wait_ge(dst.h(ec), dst.tgt(ec - 2))
                r = raw[ec % 2]
                v.stream_shuffle(sw[:], r[:], SWAP_MASK)
                v.tensor_mul(t1[:], r[:], C[:])
                v.tensor_mul(sw[:], sw[:], S[:])
                v.tensor_add(out_sb[ec % 2][:], t1[:], sw[:]).then_inc(s_rp.h, 1)
        s_rp.v += ND
    p.pop()


def _emit_proj_tm_v(p, ps, W, w_sb, x_fm, bv_sb, ones1_sb, OUT):
    nc = p.nc
    p.push()
    out_sb = [p.sb("vout", [128, D], BF16) for _ in range(2)]
    s_w = p.sem("tvw", global_=True)
    s_mm, s_ev = (p.sem(x) for x in ("mm", "ev"))
    dst = p.gdsem("tvst", 2)
    w0, mm0, ev0 = s_w.v, s_mm.v, s_ev.v
    with nc.Block() as blk:
        @blk.gpsimd
        def _(g):
            for dc in range(ND):
                g.dma_start(w_sb[:, dc, :], W[dc * 128:(dc + 1) * 128, :]).then_inc(s_w.h, 16)
            for tc in range(NT):
                g.wait_ge(s_ev.h, ev0 + tc + 1)
                g.dma_start(OUT[tc * 128:(tc + 1) * 128, :], out_sb[tc % 2][:]).then_inc(dst.inc(tc), 16)
            g.wait_ge(dst.h(NT - 2), dst.tgt(NT - 2))
            g.wait_ge(dst.h(NT - 1), dst.tgt(NT - 1))
        s_w.v += 16 * ND

        @blk.tensor
        def _(t):
            t.wait_ge(s_w.h, w0 + 16 * ND)
            for tc in range(NT):
                if tc > 0:
                    t.wait_ge(s_ev.h, ev0 + tc)
                for dc in range(ND):
                    for n in range(4):
                        t.matmul(ps[:, n * 512:(n + 1) * 512],
                                 x_fm[:, dc, tc * 128:(tc + 1) * 128],
                                 w_sb[:, dc, n * 512:(n + 1) * 512],
                                 start=(dc == 0), stop=False)
                for n in range(4):
                    ins = t.matmul(ps[:, n * 512:(n + 1) * 512],
                                   ones1_sb[:],
                                   bv_sb[:, n * 512:(n + 1) * 512],
                                   start=False, stop=True)
                    if n == 3:
                        ins.then_inc(s_mm.h, 1)
        s_mm.v += NT

        @blk.vector
        def _(v):
            for tc in range(NT):
                v.wait_ge(s_mm.h, mm0 + tc + 1)
                if tc >= 2:
                    v.wait_ge(dst.h(tc), dst.tgt(tc - 2))
                v.tensor_copy(out_sb[tc % 2][:], ps[:]).then_inc(s_ev.h, 1)
        s_ev.v += NT
    p.pop()



def phase_attn(p, causal, qn="Q_fm", kn="K_fm", vn="V_tm", on="O_fm"):
    nc = p.nc
    Q = p.inp(qn, [D, T], BF16)
    K = p.inp(kn, [D, T], BF16)
    V = p.inp(vn, [T, D], BF16)
    O = p.outp(on, [D, T], BF16)
    ident = p.inp("c_ident", [128, 128], BF16)
    if causal:
        mask_in = p.inp("c_triuneg", [128, 128], F32)
    else:
        mask_in = p.inp("c_tril", [128, 128], BF16)

    p.push()
    V_sb = p.sb("v", [128, NT, D], BF16)
    K_sb = [p.sb("k", [128, T], BF16) for _ in range(2)]
    Q_sb = [p.sb("q", [128, T], BF16) for _ in range(2)]
    A_sb = [p.sb("a", [128, T], BF16) for _ in range(3)]
    AT_sb = [p.sb("at", [128, NT, 512], BF16) for _ in range(2)]
    O_sb = [p.sb("o", [128, T], BF16) for _ in range(2)]
    ident_sb = p.sb("id", [128, 128], BF16)
    mask_sb = p.sb("mk", [128, 128], F32 if causal else BF16)
    ncol = [p.sb("ncol", [128, 2], F32) for _ in range(2)]
    nflt = [p.sb("nflt", [128, 2], F32) for _ in range(2)]
    ntot = p.sb("ntot", [128, 1], F32)
    r_col = p.sb("rcol", [128, 1], F32)
    s_ps = [p.ps("sps", [128, 1024], F32) for _ in range(2)]
    at_ps = [p.ps("atps", [128, 128], BF16) for _ in range(3)]
    o_ps = p.ps("ops", [128, 512], F32)

    s_v = p.sem("attv", global_=True)
    s_Sh, s_msk, s_exph, s_anorm = (p.sem(x) for x in ("Sh", "msk", "exph", "an"))
    s_ncp = p.sem("ncp")
    s_na = p.sem("na")
    s_tp, s_tpe, s_zm, s_av, s_oev = (p.sem(x) for x in ("tp", "tpe", "zm", "av", "oev"))
    s_rr = p.sem("rr")
    dkq = p.gdsem("attkq", 2)
    dost = p.gdsem("attost", 2)
    v0, Sh0, msk0, exph0, an0 = s_v.v, s_Sh.v, s_msk.v, s_exph.v, s_anorm.v
    ncp0 = s_ncp.v
    na0 = s_na.v
    tp0, tpe0, zm0, av0, oev0, rr0 = s_tp.v, s_tpe.v, s_zm.v, s_av.v, s_oev.v, s_rr.v

    def halves(qt):
        W = (qt + 1) * 128 if causal else T
        if W <= 1024:
            return [(0, W)]
        return [(0, 1024), (1024, W)]

    def nhalves(qt):
        return len(halves(qt))

    def halfcum(g):
        h, qt = divmod(g, NT)
        per_head = sum(nhalves(q) for q in range(NT))
        return h * per_head + sum(nhalves(q) for q in range(qt))

    def cum_tp(g):
        h, qt = divmod(g, NT)
        return h * (NT * (NT + 1) // 2) + qt * (qt + 1) // 2

    def nmask(g):
        return g + 1

    GMAX = NH * NT

    def emit_S(t, g):
        h, qt = divmod(g, NT)
        if qt == 0:
            t.wait_ge(dkq.h(h), dkq.tgt(h, 32))
        for hi, (lo, hiW) in enumerate(halves(qt)):
            hc = halfcum(g) + hi
            if hc >= 2:
                t.wait_ge(s_exph.h, exph0 + hc - 1)
            width = hiW - lo
            nfull = (width + 511) // 512
            for n in range(nfull):
                a, b = lo + n * 512, min(hiW, lo + (n + 1) * 512)
                ins = t.matmul(s_ps[hc % 2][:, a - lo:b - lo],
                               Q_sb[h % 2][:, qt * 128:(qt + 1) * 128],
                               K_sb[h % 2][:, a:b],
                               start=True, stop=True)
            ins.then_inc(s_Sh.h, 1)

    def emit_work(t, g):
        h, qt = divmod(g, NT)
        qg, qi = divmod(qt, 4)
        t.wait_ge(s_anorm.h, an0 + g + 1)
        for kc in range(qt + 1):
            tau = cum_tp(g) + kc
            if tau >= 3:
                t.wait_ge(s_tpe.h, tpe0 + tau - 2)
            t.transpose(at_ps[tau % 3][:], A_sb[g % 3][:, kc * 128:(kc + 1) * 128],
                        ident_sb[:]).then_inc(s_tp.h, 1)
        if qi == 3:
            t.wait_ge(s_tpe.h, tpe0 + cum_tp(g) + qt + 1)
            t.wait_ge(s_zm.h, zm0 + 6 * (4 * h + qg + 1))
            t.wait_ge(s_oev.h, oev0 + 4 * h + qg)
            kmax = 4 * qg + 4
            for kc in range(kmax):
                ins = t.matmul(o_ps[:], V_sb[:, kc, h * 128:(h + 1) * 128],
                               AT_sb[qg % 2][:, kc, :],
                               start=(kc == 0), stop=(kc == kmax - 1))
            ins.then_inc(s_av.h, 1)

    with nc.Block() as blk:
        @blk.sync
        def _(s):
            s.dma_start(ident_sb[:], ident).then_inc(s_v.h, 16)
            s.dma_start(mask_sb[:], mask_in).then_inc(s_v.h, 16)
            for tc in range(NT):
                s.dma_start(V_sb[:, tc, :], V[tc * 128:(tc + 1) * 128, :]).then_inc(s_v.h, 16)
            for h in range(NH):
                if h >= 2:
                    s.wait_ge(s_Sh.h, Sh0 + halfcum(NT * (h - 1)))
                s.dma_start(K_sb[h % 2][:], K[h * 128:(h + 1) * 128, :]).then_inc(dkq.inc(h), 16)
                s.dma_start(Q_sb[h % 2][:], Q[h * 128:(h + 1) * 128, :]).then_inc(dkq.inc(h), 16)
        s_v.v += 16 * (NT + 2)

        @blk.tensor
        def _(t):
            t.wait_ge(s_v.h, v0 + 16 * (NT + 2))
            emit_S(t, 0)
            emit_S(t, 1)
            for g in range(GMAX):
                if g + 2 < GMAX:
                    emit_S(t, g + 2)
                emit_work(t, g)
        s_Sh.v += halfcum(GMAX - 1) + nhalves(NT - 1)
        s_tp.v += NH * (NT * (NT + 1) // 2)
        s_av.v += NH * 4

        @blk.scalar
        def _(sc):
            for g in range(GMAX):
                h, qt = divmod(g, NT)
                if g >= 3:
                    sc.wait_ge(s_tp.h, tp0 + cum_tp(g - 2))
                for hi, (lo, hiW) in enumerate(halves(qt)):
                    hc = halfcum(g) + hi
                    sc.wait_ge(s_Sh.h, Sh0 + hc + 1)
                    if causal and hi == len(halves(qt)) - 1:
                        sc.wait_ge(s_msk.h, msk0 + nmask(g))
                    sc.activation(A_sb[g % 3][:, lo:hiW], s_ps[hc % 2][:, 0:hiW - lo],
                                  AF.Exp, scale=SM_SCALE,
                                  accum_out=ncol[g % 2][:, hi:hi + 1]).then_inc(s_exph.h, 1)
                if g >= 2:
                    sc.wait_ge(s_na.h, na0 + g - 1)
                nhq = nhalves(qt)
                sc.copy(nflt[g % 2][:, 0:nhq], ncol[g % 2][:, 0:nhq]).then_inc(s_ncp.h, 1)
        s_exph.v += halfcum(GMAX - 1) + nhalves(NT - 1)
        s_ncp.v += GMAX

        @blk.vector
        def _(v):
            def emit_evacs(v, g):
                h, qt = divmod(g, NT)
                qg, qi = divmod(qt, 4)
                for kc in range(qt + 1):
                    tau = cum_tp(g) + kc
                    if qi == 0 and kc == 0 and qg >= 2:
                        v.wait_ge(s_av.h, av0 + 4 * h + qg - 1)
                    if qi == 0 and kc == 0 and qg < 2 and h >= 1:
                        v.wait_ge(s_av.h, av0 + 4 * (h - 1) + qg + 3)
                    v.wait_ge(s_tp.h, tp0 + tau + 1)
                    v.tensor_copy(AT_sb[qg % 2][:, kc, qi * 128:(qi + 1) * 128],
                                  at_ps[tau % 3][:]).then_inc(s_tpe.h, 1)
                if qi == 3:
                    for qt2 in range(4 * qg, 4 * qg + 4):
                        for kc in range(qt2 + 1, 4 * qg + 4):
                            v.memset(AT_sb[qg % 2][:, kc, (qt2 % 4) * 128:(qt2 % 4 + 1) * 128],
                                     0.0).then_inc(s_zm.h, 1)
                    v.wait_ge(s_av.h, av0 + 4 * h + qg + 1)
                    if qg == 0 and h >= 2:
                        v.wait_ge(dost.h(h), dost.tgt(h - 2))
                    v.tensor_copy(O_sb[h % 2][:, qg * 512:(qg + 1) * 512],
                                  o_ps[:]).then_inc(s_oev.h, 1)
            for g in range(GMAX):
                h, qt = divmod(g, NT)
                Wn = (qt + 1) * 128
                nh = nhalves(qt)
                if causal:
                    hcl = halfcum(g) + nh - 1
                    lo = halves(qt)[-1][0]
                    v.wait_ge(s_Sh.h, Sh0 + hcl + 1)
                    v.tensor_add(s_ps[hcl % 2][:, qt * 128 - lo:(qt + 1) * 128 - lo],
                                 s_ps[hcl % 2][:, qt * 128 - lo:(qt + 1) * 128 - lo],
                                 mask_sb[:]).then_inc(s_msk.h, 1)
                v.wait_ge(s_ncp.h, ncp0 + g + 1)
                if nh == 2:
                    v.tensor_add(ntot[:], nflt[g % 2][:, 0:1],
                                 nflt[g % 2][:, 1:2]).then_inc(s_na.h, 1)
                else:
                    v.tensor_copy(ntot[:], nflt[g % 2][:, 0:1]).then_inc(s_na.h, 1)
                v.wait_ge(s_na.h, na0 + g + 1)
                v.reciprocal(r_col[:], ntot[:]).then_inc(s_rr.h, 1)
                v.wait_ge(s_rr.h, rr0 + g + 1)
                ins = v.tensor_scalar_mul(A_sb[g % 3][:, 0:Wn], A_sb[g % 3][:, 0:Wn], r_col[:])
                if not causal:
                    ins = v.tensor_mul(A_sb[g % 3][:, qt * 128:(qt + 1) * 128],
                                       A_sb[g % 3][:, qt * 128:(qt + 1) * 128],
                                       mask_sb[:])
                ins.then_inc(s_anorm.h, 1)
                if g >= 1:
                    emit_evacs(v, g - 1)
            emit_evacs(v, GMAX - 1)
        if causal:
            s_msk.v += GMAX
        s_na.v += GMAX
        s_rr.v += GMAX
        s_anorm.v += GMAX
        s_tpe.v += NH * (NT * (NT + 1) // 2)
        s_zm.v += NH * 24
        s_oev.v += NH * 4

        @blk.gpsimd
        def _(g):
            for h in range(NH):
                g.wait_ge(s_oev.h, oev0 + 4 * (h + 1))
                g.dma_start(O[h * 128:(h + 1) * 128, :], O_sb[h % 2][:]).then_inc(dost.inc(h), 16)
            g.wait_ge(dost.h(NH - 2), dost.tgt(NH - 2))
            g.wait_ge(dost.h(NH - 1), dost.tgt(NH - 1))
    p.pop()



def phase_proj_ln(p, layer, on="O_fm", wname="L0_attn_Wo", bname=None,
                  xres_name="x_in", xout_name="x1", w_transposed=False):
    nc = p.nc
    O = p.inp(on, [D, T], BF16)
    Wo = p.inp(wname, [D, D], F32)
    bo = p.inp(bname, [1, D], F32) if bname else None
    xres = p.inp(xres_name, [T, D], F32)
    xout = p.outp(xout_name, [T, D], F32)
    ident = p.inp("c_ident", [128, 128], BF16)
    ones1 = p.inp("c_ones1", [1, 128], BF16)

    p.push()
    O_sb = p.sb("osb", [128, ND, T], BF16)
    W_sb = p.sb("wsb", [128, ND, D], BF16)
    ident_sb = p.sb("id", [128, 128], BF16)
    ones1_sb = p.sb("on", [1, 128], BF16)
    bo_sb = p.sb("bo", [1, D], BF16) if bname else None
    xr = [p.sb("xr", [128, D], F32) for _ in range(2)]
    xo = [p.sb("xo", [128, D], F32) for _ in range(2)]
    ssum = p.sb("ssum", [128, 1], F32)
    ssq = p.sb("ssq", [128, 1], F32)
    mean = p.sb("mean", [128, 1], F32)
    var = p.sb("var", [128, 1], F32)
    sd = p.sb("sd", [128, 1], F32)
    rstd = p.sb("rstd", [128, 1], F32)
    ps = p.ps("pj", [128, D], F32)

    s_c = p.sem("plc", global_=True)
    s_w = p.sem("plw", global_=True)
    c0, w0 = s_c.v, s_w.v

    with nc.Block() as blk:
        @blk.sync
        def _(s):
            s.dma_start(ident_sb[:], ident).then_inc(s_c.h, 16)
            s.dma_start(ones1_sb[:], ones1).then_inc(s_c.h, 16)
            for dc in range(ND):
                s.dma_start(O_sb[:, dc, :], O[dc * 128:(dc + 1) * 128, :]).then_inc(s_c.h, 16)
            s.wait_ge(s_c.h, c0 + 16 * (ND + 2))
        s_c.v += 16 * (ND + 2)

        @blk.gpsimd
        def _(g):
            if bname:
                g.dma_start(bo_sb[:], bo).then_inc(s_w.h, 16)
                g.wait_ge(s_w.h, w0 + 16)
        if bname:
            s_w.v += 16

    if not w_transposed:
        w0b = s_w.v
        with nc.Block() as blk:
            @blk.gpsimd
            def _(g):
                for dc in range(ND):
                    g.dma_start(W_sb[:, dc, :], Wo[dc * 128:(dc + 1) * 128, :]).then_inc(s_w.h, 16)
                g.wait_ge(s_w.h, w0b + 16 * ND)
            s_w.v += 16 * ND
    else:
        wtmp = [p.sb("wt", [128, D], BF16) for _ in range(2)]
        wt_ps = [p.ps("wtp", [128, 128], BF16) for _ in range(2)]
        s_tt, s_te = p.sem("tt"), p.sem("te")
        dwl = p.gdsem("plwl", 2)
        tt0, te0 = s_tt.v, s_te.v
        with nc.Block() as blk:
            @blk.gpsimd
            def _(g):
                for ec in range(ND):
                    if ec >= 2:
                        g.wait_ge(s_te.h, te0 + (ec - 1) * ND)
                    g.dma_start(wtmp[ec % 2][:], Wo[ec * 128:(ec + 1) * 128, :]).then_inc(dwl.inc(ec), 16)

            @blk.tensor
            def _(t):
                for ec in range(ND):
                    t.wait_ge(dwl.h(ec), dwl.tgt(ec))
                    for dc in range(ND):
                        i = ec * ND + dc
                        if i >= 2:
                            t.wait_ge(s_te.h, te0 + i - 1)
                        t.transpose(wt_ps[i % 2][:], wtmp[ec % 2][:, dc * 128:(dc + 1) * 128],
                                    ident_sb[:]).then_inc(s_tt.h, 1)
            s_tt.v += ND * ND

            @blk.vector
            def _(v):
                for ec in range(ND):
                    for dc in range(ND):
                        i = ec * ND + dc
                        v.wait_ge(s_tt.h, tt0 + i + 1)
                        v.tensor_copy(W_sb[:, dc, ec * 128:(ec + 1) * 128],
                                      wt_ps[i % 2][:]).then_inc(s_te.h, 1)
            s_te.v += ND * ND

    _emit_proj_res_ln(p, O_sb, W_sb,
                      (ones1_sb, bo_sb) if bname else None, xres, xout, ps,
                      xr, xo, ssum, ssq, mean, var, sd, rstd)
    p.pop()
    return


def _emit_proj_res_ln(p, lhsT_sb, W_sb, bias_pair, xres, xout, ps,
                      xr, xo, ssum, ssq, mean, var, sd, rstd):
    nc = p.nc
    s_mm, s_r1, s_sq, s_nm = (p.sem(x) for x in ("mm", "r1", "sq", "nm"))
    s_v2, s_sd, s_rc = p.sem("v2"), p.sem("sd"), p.sem("rc")
    dx = p.gdsem("lnx", 2)
    dst = p.gdsem("lnst", 2)
    mm0, r10, sq0, nm0 = (s.v for s in (s_mm, s_r1, s_sq, s_nm))
    v20, sd0, rc0 = s_v2.v, s_sd.v, s_rc.v
    with nc.Block() as blk:
        @blk.sync
        def _(s):
            for qc in range(NT):
                if qc >= 2:
                    s.wait_ge(s_nm.h, nm0 + qc - 1)
                s.dma_start(xr[qc % 2][:], xres[qc * 128:(qc + 1) * 128, :]).then_inc(dx.inc(qc), 16)

        @blk.tensor
        def _(t):
            for qc in range(NT):
                if qc > 0:
                    t.wait_ge(s_sq.h, sq0 + qc)
                last = None
                for dc in range(ND):
                    for n in range(4):
                        last = t.matmul(ps[:, n * 512:(n + 1) * 512],
                                        lhsT_sb[:, dc, qc * 128:(qc + 1) * 128],
                                        W_sb[:, dc, n * 512:(n + 1) * 512],
                                        start=(dc == 0),
                                        stop=(False if bias_pair else dc == ND - 1))
                if bias_pair:
                    ones1_sb, bo_sb = bias_pair
                    for n in range(4):
                        last = t.matmul(ps[:, n * 512:(n + 1) * 512], ones1_sb[:],
                                        bo_sb[:, n * 512:(n + 1) * 512],
                                        start=False, stop=True)
                last.then_inc(s_mm.h, 1)
        s_mm.v += NT

        @blk.vector
        def _(v):
            for qc in range(NT):
                v.wait_ge(s_mm.h, mm0 + qc + 1)
                v.wait_ge(dx.h(qc), dx.tgt(qc))
                v.scalar_tensor_tensor(xr[qc % 2][:], ps[:], 1.0, xr[qc % 2][:],
                                       ALU.mult, ALU.add,
                                       accum_out=ssum[:]).then_inc(s_r1.h, 1)
                v.wait_ge(s_sq.h, sq0 + qc + 1)
                v.tensor_scalar_mul(mean[:], ssum[:], INV_D)
                v.tensor_scalar_mul(var[:], ssq[:], INV_D)
                v.tensor_mul(rstd[:], mean[:], mean[:])
                v.tensor_sub(var[:], var[:], rstd[:])
                v.tensor_scalar_add(var[:], var[:], LN_EPS).then_inc(s_v2.h, 1)
                v.wait_ge(s_sd.h, sd0 + qc + 1)
                if qc >= 2:
                    v.wait_ge(dst.h(qc), dst.tgt(qc - 2))
                v.reciprocal(rstd[:], sd[:]).then_inc(s_rc.h, 1)
                v.wait_ge(s_rc.h, rc0 + qc + 1)
                v.tensor_scalar(xo[qc % 2][:], xr[qc % 2][:], mean[:], rstd[:],
                                ALU.subtract, ALU.mult).then_inc(s_nm.h, 1)
        s_r1.v += NT
        s_v2.v += NT
        s_rc.v += NT
        s_nm.v += NT

        @blk.scalar
        def _(sc):
            for qc in range(NT):
                sc.wait_ge(s_r1.h, r10 + qc + 1)
                sc.activation(ps[:], xr[qc % 2][:], AF.Square,
                              accum_out=ssq[:]).then_inc(s_sq.h, 1)
                sc.wait_ge(s_v2.h, v20 + qc + 1)
                sc.activation(sd[:], var[:], AF.Sqrt).then_inc(s_sd.h, 1)
        s_sq.v += NT
        s_sd.v += NT

        @blk.gpsimd
        def _(g):
            for qc in range(NT):
                g.wait_ge(s_nm.h, nm0 + qc + 1)
                g.dma_start(xout[qc * 128:(qc + 1) * 128, :], xo[qc % 2][:]).then_inc(dst.inc(qc), 16)
            g.wait_ge(dst.h(NT - 2), dst.tgt(NT - 2))
            g.wait_ge(dst.h(NT - 1), dst.tgt(NT - 1))



def phase_ffn_prep(p):
    nc = p.nc
    wup_q = p.inp("L0_ffn_w_up_q", [HF, D], I32)
    wup_s = p.inp("L0_ffn_w_up_scale", [HF, 32], F32)
    wdn_q = p.inp("L0_ffn_w_down_q", [D, HF], I32)
    wdn_s = p.inp("L0_ffn_w_down_scale", [D, HF // 64], F32)
    ident = p.inp("c_ident", [128, 128], BF16)
    WupT = p.outp("WupT", [NHC, 128, D], BF16)
    WdnT = p.outp("WdnT", [NHC, 128, D], BF16)

    p.push()
    ident_sb = p.sb("id", [128, 128], BF16)
    qrow = [p.sb("qr", [128, D], I32) for _ in range(4)]
    dqr = [p.sb("dqr", [128, D], BF16) for _ in range(4)]
    wus_sb = [p.sb("wus", [128, 32], F32) for _ in range(2)]
    wds_sb = p.sb("wds", [128, ND, HF // 64], F32)
    upstage = [p.sb("ups", [128, D], BF16) for _ in range(2)]
    dnstage = [p.sb("dns", [128, D], BF16) for _ in range(2)]
    tp_ps = [p.ps("tp", [128, 128], BF16) for _ in range(4)]

    s_c = p.sem("d1c", global_=True)
    s_dq, s_tp, s_ev = (p.sem(x) for x in ("dq", "tp", "ev"))
    dld = p.gdsem("d1ld", 4)
    dsc = p.gdsem("d1sc", 2)
    dstg = p.gdsem("d1stg", 2)
    c0, dq0, tp0, ev0 = (x.v for x in (s_c, s_dq, s_tp, s_ev))

    wdn_qv = wdn_q.rearrange("(dc p) h -> p dc h", p=128)

    with nc.Block() as blk:
        @blk.sync
        def _(s):
            s.dma_start(ident_sb[:], ident).then_inc(s_c.h, 16)
            for dc in range(ND):
                s.dma_start(wds_sb[:, dc, :],
                            wdn_s[dc * 128:(dc + 1) * 128, :]).then_inc(s_c.h, 16)
            s.wait_ge(s_c.h, c0 + 16 * (ND + 1))
        s_c.v += 16 * (ND + 1)

    with nc.Block() as blk:
        @blk.sync
        def _(s):
            for hc in range(NHC):
                for j in range(2):
                    r = hc * 2 + j
                    if r >= 4:
                        s.wait_ge(s_dq.h, dq0 + r - 3)
                    if j == 0:
                        src = wup_q[hc * 128:(hc + 1) * 128, :]
                    else:
                        src = wdn_qv[:, :, hc * 128:(hc + 1) * 128]
                    s.dma_start(qrow[r % 4][:], src).then_inc(dld.inc(r), 16)

        @blk.gpsimd
        def _(g):
            def stores(hcs):
                g.wait_ge(s_ev.h, ev0 + hcs * 32 + ND)
                g.dma_start(WupT[hcs], upstage[hcs % 2][:]).then_inc(dstg.inc(hcs), 16)
                g.wait_ge(s_ev.h, ev0 + hcs * 32 + 32)
                g.dma_start(WdnT[hcs], dnstage[hcs % 2][:]).then_inc(dstg.inc(hcs), 16)
            for hc in range(NHC):
                if hc >= 2:
                    g.wait_ge(s_dq.h, dq0 + 2 * hc - 3)
                g.dma_start(wus_sb[hc % 2][:],
                            wup_s[hc * 128:(hc + 1) * 128, :]).then_inc(dsc.inc(hc), 16)
                if hc >= 1:
                    stores(hc - 1)
            stores(NHC - 1)
            g.wait_ge(dstg.h(NHC - 2), dstg.tgt(NHC - 2, 32))
            g.wait_ge(dstg.h(NHC - 1), dstg.tgt(NHC - 1, 32))

        @blk.vector
        def _(v):
            for hc in range(NHC):
                for j in range(2):
                    r = hc * 2 + j
                    v.wait_ge(dld.h(r), dld.tgt(r))
                    if r >= 4:
                        v.wait_ge(s_tp.h, tp0 + (r - 3) * ND)
                    q = qrow[r % 4]
                    d = dqr[r % 4]
                    if j == 0:
                        v.wait_ge(dsc.h(hc), dsc.tgt(hc))
                        for b in range(32):
                            ins = v.tensor_scalar(d[:, b * 64:(b + 1) * 64],
                                                  q[:, b * 64:(b + 1) * 64], -7.5,
                                                  wus_sb[hc % 2][:, b:b + 1],
                                                  ALU.add, ALU.mult)
                        ins.then_inc(s_dq.h, 1)
                    else:
                        for dc in range(ND):
                            for hh in range(2):
                                ins = v.tensor_scalar(
                                    d[:, dc * 128 + hh * 64:dc * 128 + (hh + 1) * 64],
                                    q[:, dc * 128 + hh * 64:dc * 128 + (hh + 1) * 64],
                                    -7.5, wds_sb[:, dc, 2 * hc + hh:2 * hc + hh + 1],
                                    ALU.add, ALU.mult)
                        ins.then_inc(s_dq.h, 1)
                    if r >= 1:
                        _d1_evacs(v, r - 1, s_tp, tp0, s_ev, tp_ps, upstage, dnstage)
            _d1_evacs(v, 2 * NHC - 1, s_tp, tp0, s_ev, tp_ps, upstage, dnstage)
        s_dq.v += 2 * NHC
        s_ev.v += 2 * NHC * ND

        @blk.tensor
        def _(t):
            for hc in range(NHC):
                if hc >= 2:
                    t.wait_ge(dstg.h(hc), dstg.tgt(hc - 2, 32))
                for j in range(2):
                    r = hc * 2 + j
                    t.wait_ge(s_dq.h, dq0 + r + 1)
                    for dc in range(ND):
                        i = r * ND + dc
                        if i >= 4:
                            t.wait_ge(s_ev.h, ev0 + i - 3)
                        t.transpose(tp_ps[i % 4][:],
                                    dqr[r % 4][:, dc * 128:(dc + 1) * 128],
                                    ident_sb[:]).then_inc(s_tp.h, 1)
        s_tp.v += 2 * NHC * ND
    p.pop()
    return


def _d1_evacs(v, r, s_tp, tp0, s_ev, tp_ps, upstage, dnstage):
    hc, j = divmod(r, 2)
    stage = upstage[hc % 2] if j == 0 else dnstage[hc % 2]
    for dc in range(ND):
        i = r * ND + dc
        v.wait_ge(s_tp.h, tp0 + i + 1)
        v.tensor_copy(stage[:, dc * 128:(dc + 1) * 128],
                      tp_ps[i % 4][:]).then_inc(s_ev.h, 1)





def phase_ffn_main(p):
    nc = p.nc
    x1 = p.inp("x1", [T, D], F32)
    WupT = p.inp("WupT", [NHC, 128, D], BF16)
    WdnT = p.inp("WdnT", [NHC, 128, D], BF16)
    bup = p.inp("bup_cols", [128, NHC], F32)
    Aup = p.inp("L0_ffn_w_up_lora_a", [D, RK], F32)
    Bup = p.inp("L0_ffn_w_up_lora_b", [RK, HF], F32)
    Adn = p.inp("L0_ffn_w_down_lora_a", [HF, RK], F32)
    Bdn = p.inp("bdnl_ext", [RK + 1, D], F32)
    onesrow = p.inp("c_onesrow", [1, 512], BF16)
    ident = p.inp("c_ident", [128, 128], BF16)
    ones1 = p.inp("c_ones1", [1, 128], BF16)
    x2out = p.outp("x2", [T, D], F32)

    TB = 512
    NTB = T // TB
    NTC = TB // 128

    p.push()
    ident_sb = p.sb("id", [128, 128], BF16)
    ones1_sb = p.sb("on", [1, 128], BF16)
    x_fm = p.sb("xfm", [128, ND, T], BF16)
    x2_fm = p.sb("x2", [128, NHC, TB], BF16)
    wup_sb = [p.sb("wu", [128, D], BF16) for _ in range(2)]
    wdn_sb = [p.sb("wd", [128, 512], BF16) for _ in range(4)]
    aup_sb = p.sb("aup", [128, ND * RK], BF16)
    bupl_sb = p.sb("bupl", [RK, HF], BF16)
    adn_sb = p.sb("adn", [128, NHC * RK], BF16)
    bdnl_sb = p.sb("bdnl", [RK + 1, D], BF16)
    bup_sb = p.sb("bup", [128, NHC], F32)
    xaup_sb = p.sb("xaup", [RK, T], BF16)
    xadn_sb = p.sb("xadn", [RK + 1, TB], BF16)
    xr_sb = [p.sb("xr", [128, D], F32) for _ in range(4)]
    ssum4 = p.sb("ssum4", [128, 4], F32)
    ssq = p.sb("ssq", [128, 1], F32)
    ssum = p.sb("ssum", [128, 1], F32)
    mean = p.sb("mean", [128, 1], F32)
    var = p.sb("var", [128, 1], F32)
    sd = p.sb("sd", [128, 1], F32)
    rstd = p.sb("rstd", [128, 1], F32)

    s_c = p.sem("d2c", global_=True)
    s_cg = p.sem("d2cg", global_=True)
    with nc.Block() as blk:
        @blk.sync
        def _(s):
            s.dma_start(ident_sb[:], ident).then_inc(s_c.h, 16)
            s.dma_start(ones1_sb[:], ones1).then_inc(s_c.h, 16)
            s.dma_start(bup_sb[:], bup).then_inc(s_c.h, 16)
            s.wait_ge(s_c.h, s_c.v + 48)
        s_c.v += 48

        @blk.gpsimd
        def _(g):
            g.dma_start(xadn_sb[RK:RK + 1, :], onesrow[0:1, 0:TB]).then_inc(s_cg.h, 16)
            g.dma_start(bupl_sb[:], Bup).then_inc(s_cg.h, 16)
            g.dma_start(bdnl_sb[:], Bdn).then_inc(s_cg.h, 16)
            for dc in range(ND):
                g.dma_start(aup_sb[:, dc * RK:(dc + 1) * RK],
                            Aup[dc * 128:(dc + 1) * 128, :]).then_inc(s_cg.h, 16)
            for hc in range(NHC):
                g.dma_start(adn_sb[:, hc * RK:(hc + 1) * RK],
                            Adn[hc * 128:(hc + 1) * 128, :]).then_inc(s_cg.h, 16)
            g.wait_ge(s_cg.h, s_cg.v + 16 * (3 + ND + NHC))
        s_cg.v += 16 * (3 + ND + NHC)

    emit_load_xfm(p, x1, x_fm, ident_sb, bufs=wup_sb)

    p.push()
    xaup_ps = p.ps("xaupp", [RK, T], F32)
    s_u, s_ue = p.sem("u"), p.sem("ue")
    with nc.Block() as blk:
        @blk.tensor
        def _(t):
            last = None
            for dc in range(ND):
                for n in range(4):
                    last = t.matmul(xaup_ps[:, n * 512:(n + 1) * 512],
                                    aup_sb[:, dc * RK:(dc + 1) * RK],
                                    x_fm[:, dc, n * 512:(n + 1) * 512],
                                    start=(dc == 0), stop=(dc == ND - 1))
            last.then_inc(s_u.h, 1)
        s_u.v += 1

        @blk.vector
        def _(v):
            v.wait_ge(s_u.h, s_u.v)
            v.tensor_copy(xaup_sb[:], xaup_ps[:]).then_inc(s_ue.h, 1)
        s_ue.v += 1
    p.pop()

    up_ps = [p.ps("up", [128, TB], F32) for _ in range(2)]
    xa_ps = p.ps("xa", [RK, TB], F32)
    dn_ps = [p.ps("dn", [128, 512], F32) for _ in range(4)]

    s_up, s_rel, s_xa, s_xae, s_dn, s_dnw = (p.sem(x) for x in
        ("up", "rel", "xa", "xae", "dn", "dnw"))
    dwu = p.gdsem("d2wu", 2)
    dwd = p.gdsem("d2wd", 4)
    dxl = p.gdsem("d2xl", 4)
    dxs = p.gdsem("d2xs", 4)
    s_r1, s_sq2, s_v2, s_sd2, s_rc, s_nm = (p.sem(x) for x in
        ("r1", "sq2", "v2", "sd2", "rc", "nm"))

    for tb in range(NTB):
        up0, rel0, xa0 = s_up.v, s_rel.v, s_xa.v
        with nc.Block() as blk:
            @blk.sync
            def _(s):
                for hc in range(NHC):
                    k = tb * NHC + hc
                    if k >= 2:
                        s.wait_ge(s_up.h, k - 1)
                    s.dma_start(wup_sb[hc % 2][:], WupT[hc]).then_inc(dwu.inc(k), 16)

            @blk.tensor
            def _(t):
                t.wait_ge(s_ue.h, s_ue.v)
                for hc in range(NHC):
                    k = tb * NHC + hc
                    t.wait_ge(dwu.h(k), dwu.tgt(k))
                    if hc >= 2:
                        t.wait_ge(s_rel.h, rel0 + hc - 1)
                    last = None
                    for dc in range(ND):
                        last = t.matmul(up_ps[hc % 2][:],
                                        wup_sb[hc % 2][:, dc * 128:(dc + 1) * 128],
                                        x_fm[:, dc, tb * TB:(tb + 1) * TB],
                                        start=(dc == 0), stop=False)
                    last = t.matmul(up_ps[hc % 2][:],
                                    bupl_sb[:, hc * 128:(hc + 1) * 128],
                                    xaup_sb[:, tb * TB:(tb + 1) * TB],
                                    start=False, stop=True)
                    last.then_inc(s_up.h, 1)
                    if hc >= 1:
                        t.wait_ge(s_rel.h, rel0 + hc)
                        t.matmul(xa_ps[:], adn_sb[:, (hc - 1) * RK:hc * RK],
                                 x2_fm[:, hc - 1, :],
                                 start=(hc == 1), stop=False)
                t.wait_ge(s_rel.h, rel0 + NHC)
                t.matmul(xa_ps[:], adn_sb[:, (NHC - 1) * RK:NHC * RK],
                         x2_fm[:, NHC - 1, :],
                         start=False, stop=True).then_inc(s_xa.h, 1)
            s_up.v += NHC
            s_xa.v += 1

            @blk.scalar
            def _(sc):
                for hc in range(NHC):
                    sc.wait_ge(s_up.h, up0 + hc + 1)
                    sc.activation(x2_fm[:, hc, :], up_ps[hc % 2][:], AF.Relu,
                                  bias=bup_sb[:, hc:hc + 1]).then_inc(s_rel.h, 1)
            s_rel.v += NHC

            @blk.vector
            def _(v):
                v.wait_ge(s_xa.h, xa0 + 1)
                v.tensor_copy(xadn_sb[0:RK, :], xa_ps[:]).then_inc(s_xae.h, 1)
            s_xae.v += 1

        dn0, dnw0 = s_dn.v, s_dnw.v
        r10, sq20, v20, sd20, rc0, nm0 = (s.v for s in
            (s_r1, s_sq2, s_v2, s_sd2, s_rc, s_nm))
        with nc.Block() as blk:
            @blk.sync
            def _(s):
                for tc in range(NTC):
                    qcg = tb * NTC + tc
                    if tb > 0:
                        s.wait_ge(dxs.h(qcg), dxs.tgt(qcg - NTC))
                    s.dma_start(xr_sb[tc][:],
                                x1[qcg * 128:(qcg + 1) * 128, :]
                                ).then_inc(dxl.inc(qcg), 16)
                for dh in range(4):
                    for hc in range(NHC):
                        i = dh * NHC + hc
                        k = tb * 4 * NHC + i
                        if k % 2 == 1:
                            continue
                        if k >= 4:
                            s.wait_ge(s_dnw.h, k - 3)
                        s.dma_start(wdn_sb[k % 4][:],
                                    WdnT[hc][:, dh * 512:(dh + 1) * 512]).then_inc(dwd.inc(k), 16)

            @blk.scalar
            def _(sc_e):
                for dh in range(4):
                    for hc in range(NHC):
                        i = dh * NHC + hc
                        k = tb * 4 * NHC + i
                        if k % 2 == 0:
                            continue
                        if k >= 4:
                            sc_e.wait_ge(s_dnw.h, k - 3)
                        sc_e.dma_start(wdn_sb[k % 4][:],
                                       WdnT[hc][:, dh * 512:(dh + 1) * 512]).then_inc(dwd.inc(k), 16)

            @blk.tensor
            def _(t):
                for dh in range(4):
                    if dh > 0:
                        t.wait_ge(s_r1.h, r10 + dh * NTC)
                    for hc in range(NHC):
                        i = dh * NHC + hc
                        k = tb * 4 * NHC + i
                        t.wait_ge(dwd.h(k), dwd.tgt(k))
                        for tc in range(NTC):
                            ins = t.matmul(dn_ps[tc][:],
                                           x2_fm[:, hc, tc * 128:(tc + 1) * 128],
                                           wdn_sb[k % 4][:],
                                           start=(hc == 0), stop=False)
                            if tc == NTC - 1:
                                ins.then_inc(s_dnw.h, 1)
                    t.wait_ge(s_xae.h, s_xae.v)
                    for tc in range(NTC):
                        t.matmul(dn_ps[tc][:], xadn_sb[:, tc * 128:(tc + 1) * 128],
                                 bdnl_sb[:, dh * 512:(dh + 1) * 512],
                                 start=False, stop=True).then_inc(s_dn.h, 1)
            s_dn.v += 4 * NTC
            s_dnw.v += 4 * NHC

            @blk.vector
            def _(v):
                for dh in range(4):
                    for tc in range(NTC):
                        v.wait_ge(s_dn.h, dn0 + dh * NTC + tc + 1)
                        v.wait_ge(dxl.h(tb * NTC + tc), dxl.tgt(tb * NTC + tc))
                        v.scalar_tensor_tensor(
                            xr_sb[tc][:, dh * 512:(dh + 1) * 512],
                            dn_ps[tc][:], 1.0,
                            xr_sb[tc][:, dh * 512:(dh + 1) * 512],
                            ALU.mult, ALU.add,
                            accum_out=ssum4[:, dh:dh + 1]).then_inc(s_r1.h, 1)
            s_r1.v += 16

        with nc.Block() as blk:
            @blk.vector
            def _(v):
                for tc in range(NTC):
                    v.reduce_sum(ssum[:], xr_sb[tc][:], axis=AX.X).then_inc(s_r1.h, 1)
                    v.wait_ge(s_sq2.h, sq20 + tc + 1)
                    v.tensor_scalar_mul(mean[:], ssum[:], INV_D)
                    v.tensor_scalar_mul(var[:], ssq[:], INV_D)
                    v.tensor_mul(rstd[:], mean[:], mean[:])
                    v.tensor_sub(var[:], var[:], rstd[:])
                    v.tensor_scalar_add(var[:], var[:], LN_EPS).then_inc(s_v2.h, 1)
                    v.wait_ge(s_sd2.h, sd20 + tc + 1)
                    v.reciprocal(rstd[:], sd[:]).then_inc(s_rc.h, 1)
                    v.wait_ge(s_rc.h, rc0 + tc + 1)
                    v.tensor_scalar(xr_sb[tc][:], xr_sb[tc][:], mean[:], rstd[:],
                                    ALU.subtract, ALU.mult).then_inc(s_nm.h, 1)
            s_r1.v += NTC
            s_v2.v += NTC
            s_rc.v += NTC
            s_nm.v += NTC

            @blk.scalar
            def _(sc):
                for tc in range(NTC):
                    sc.wait_ge(s_r1.h, r10 + 16 + tc + 1)
                    sc.activation(x2_fm[:, 0:4, :], xr_sb[tc][:], AF.Square,
                                  accum_out=ssq[:]).then_inc(s_sq2.h, 1)
                    sc.wait_ge(s_v2.h, v20 + tc + 1)
                    sc.activation(sd[:], var[:], AF.Sqrt).then_inc(s_sd2.h, 1)
            s_sq2.v += NTC
            s_sd2.v += NTC

            @blk.gpsimd
            def _(g):
                for tc in range(NTC):
                    g.wait_ge(s_nm.h, nm0 + tc + 1)
                    qc = tb * NTC + tc
                    g.dma_start(x2out[qc * 128:(qc + 1) * 128, :],
                                xr_sb[tc][:]).then_inc(dxs.inc(qc), 16)
                if tb == NTB - 1:
                    for tc in range(NTC):
                        g.wait_ge(dxs.h(tc), dxs.tgt((NTB - 1) * NTC + tc))
    p.pop()
    return



def phase_qkv1(p):
    nc = p.nc
    x2 = p.inp("x2", [T, D], F32)
    Wqkv = p.inp("L1_attn_W_qkv", [3 * D, D], F32)
    ident = p.inp("c_ident", [128, 128], BF16)
    Qo = p.outp("Q1_fm", [D, T], BF16)
    Ko = p.outp("K1_fm", [D, T], BF16)
    Vo = p.outp("V1_tm", [T, D], BF16)

    p.push()
    ident_sb = p.sb("id", [128, 128], BF16)
    x_fm = p.sb("xfm", [128, ND, T], BF16)
    wtmp = [p.sb("wt", [128, D], BF16) for _ in range(2)]
    wt_sb = p.sb("wts", [128, ND, 128], BF16)
    WTv_sb = p.sb("wtv", [128, ND, D], BF16)
    ev_sb = [p.sb("ev", [128, T], BF16) for _ in range(2)]
    wt_ps = [p.ps("wtp", [128, 128], BF16) for _ in range(2)]
    pj_ps = p.ps("pj", [128, T], F32)

    s_c = p.sem("ec", global_=True)
    with nc.Block() as blk:
        @blk.sync
        def _(s):
            s.dma_start(ident_sb[:], ident).then_inc(s_c.h, 16)
            s.wait_ge(s_c.h, s_c.v + 16)
        s_c.v += 16

    emit_load_xfm(p, x2, x_fm, ident_sb)

    s_tt, s_te = p.sem("tt"), p.sem("te")
    dwv = p.gdsem("ewv", 2)
    tt0, te0 = s_tt.v, s_te.v
    with nc.Block() as blk:
        @blk.gpsimd
        def _(g):
            for i, ec in enumerate(range(2 * ND, 3 * ND)):
                if i >= 2:
                    g.wait_ge(s_te.h, te0 + (i - 1) * ND)
                g.dma_start(wtmp[i % 2][:], Wqkv[ec * 128:(ec + 1) * 128, :]).then_inc(dwv.inc(i), 16)

        @blk.tensor
        def _(t):
            for i in range(ND):
                t.wait_ge(dwv.h(i), dwv.tgt(i))
                for dc in range(ND):
                    j = i * ND + dc
                    if j >= 2:
                        t.wait_ge(s_te.h, te0 + j - 1)
                    t.transpose(wt_ps[j % 2][:], wtmp[i % 2][:, dc * 128:(dc + 1) * 128],
                                ident_sb[:]).then_inc(s_tt.h, 1)
        s_tt.v += ND * ND

        @blk.vector
        def _(v):
            for i in range(ND):
                for dc in range(ND):
                    j = i * ND + dc
                    v.wait_ge(s_tt.h, tt0 + j + 1)
                    v.tensor_copy(WTv_sb[:, dc, i * 128:(i + 1) * 128],
                                  wt_ps[j % 2][:]).then_inc(s_te.h, 1)
        s_te.v += ND * ND

    s_vm, s_ve = p.sem("vm"), p.sem("ve")
    dvs = p.gdsem("evs", 2)
    vm0, ve0 = s_vm.v, s_ve.v
    with nc.Block() as blk:
        @blk.tensor
        def _(t):
            for tc in range(NT):
                if tc > 0:
                    t.wait_ge(s_ve.h, ve0 + tc)
                last = None
                for dc in range(ND):
                    for n in range(4):
                        last = t.matmul(pj_ps[:, n * 512:(n + 1) * 512],
                                        x_fm[:, dc, tc * 128:(tc + 1) * 128],
                                        WTv_sb[:, dc, n * 512:(n + 1) * 512],
                                        start=(dc == 0), stop=(dc == ND - 1))
                last.then_inc(s_vm.h, 1)
        s_vm.v += NT

        @blk.vector
        def _(v):
            for tc in range(NT):
                v.wait_ge(s_vm.h, vm0 + tc + 1)
                if tc >= 2:
                    v.wait_ge(dvs.h(tc), dvs.tgt(tc - 2))
                v.tensor_copy(ev_sb[tc % 2][:], pj_ps[:]).then_inc(s_ve.h, 1)
        s_ve.v += NT

        @blk.gpsimd
        def _(g):
            for tc in range(NT):
                g.wait_ge(s_ve.h, ve0 + tc + 1)
                g.dma_start(Vo[tc * 128:(tc + 1) * 128, :], ev_sb[tc % 2][:]).then_inc(dvs.inc(tc), 16)
            g.wait_ge(dvs.h(NT - 2), dvs.tgt(NT - 2))
            g.wait_ge(dvs.h(NT - 1), dvs.tgt(NT - 1))

    s_t2, s_e2, s_qm, s_qe = (p.sem(x) for x in ("t2", "e2", "qm", "qe"))
    dw2 = p.gdsem("ew2", 2)
    dqs = p.gdsem("eqs", 2)
    t20, e20, qm0, qe0 = (s.v for s in (s_t2, s_e2, s_qm, s_qe))
    with nc.Block() as blk:
        @blk.gpsimd
        def _(g):
            def qstore(e):
                g.wait_ge(s_qe.h, qe0 + e + 1)
                dst = Qo if e < ND else Ko
                r = e % ND
                g.dma_start(dst[r * 128:(r + 1) * 128, :], ev_sb[e % 2][:]).then_inc(dqs.inc(e), 16)
            for ec in range(2 * ND):
                if ec >= 2:
                    g.wait_ge(s_qm.h, qm0 + ec - 1)
                g.dma_start(wtmp[ec % 2][:], Wqkv[ec * 128:(ec + 1) * 128, :]).then_inc(dw2.inc(ec), 16)
                if ec >= 2:
                    qstore(ec - 2)
            qstore(2 * ND - 2)
            qstore(2 * ND - 1)
            g.wait_ge(dqs.h(2 * ND - 2), dqs.tgt(2 * ND - 2))
            g.wait_ge(dqs.h(2 * ND - 1), dqs.tgt(2 * ND - 1))

        @blk.tensor
        def _(t):
            for ec in range(2 * ND):
                t.wait_ge(dw2.h(ec), dw2.tgt(ec))
                for dc in range(ND):
                    j = ec * ND + dc
                    if j >= 2:
                        t.wait_ge(s_e2.h, e20 + j - 1)
                    t.transpose(wt_ps[j % 2][:], wtmp[ec % 2][:, dc * 128:(dc + 1) * 128],
                                ident_sb[:]).then_inc(s_t2.h, 1)
                t.wait_ge(s_e2.h, e20 + (ec + 1) * ND)
                if ec > 0:
                    t.wait_ge(s_qe.h, qe0 + ec)
                last = None
                for dc in range(ND):
                    for n in range(4):
                        last = t.matmul(pj_ps[:, n * 512:(n + 1) * 512],
                                        wt_sb[:, dc, :],
                                        x_fm[:, dc, n * 512:(n + 1) * 512],
                                        start=(dc == 0), stop=(dc == ND - 1))
                last.then_inc(s_qm.h, 1)
        s_t2.v += 2 * ND * ND
        s_qm.v += 2 * ND

        @blk.vector
        def _(v):
            for ec in range(2 * ND):
                for dc in range(ND):
                    j = ec * ND + dc
                    v.wait_ge(s_t2.h, t20 + j + 1)
                    v.tensor_copy(wt_sb[:, dc, :], wt_ps[j % 2][:]).then_inc(s_e2.h, 1)
                v.wait_ge(s_qm.h, qm0 + ec + 1)
                if ec >= 2:
                    v.wait_ge(dqs.h(ec), dqs.tgt(ec - 2))
                v.tensor_copy(ev_sb[ec % 2][:], pj_ps[:]).then_inc(s_qe.h, 1)
        s_e2.v += 2 * ND * ND
        s_qe.v += 2 * ND
    p.pop()
    return



def phase_bffn(p, xres_name="x3", out_name="x4"):
    nc = p.nc
    x3 = p.inp(xres_name, [T, D], F32)
    Wt = p.inp("L1_ffn_weight", [D, D], F32)
    brow = p.inp("bffn_row", [1, D], F32)
    ident = p.inp("c_ident", [128, 128], BF16)
    ones1 = p.inp("c_ones1", [1, 128], BF16)
    OUT = p.outp(out_name, [T, D], F32)

    p.push()
    ident_sb = p.sb("id", [128, 128], BF16)
    ones1_sb = p.sb("on", [1, 128], BF16)
    brow_sb = p.sb("br", [1, D], BF16)
    x_fm = p.sb("xfm", [128, ND, T], BF16)
    bwT_sb = p.sb("bwt", [128, ND, D], BF16)
    wtmp = [p.sb("wt", [128, D], BF16) for _ in range(2)]
    sgn_sb = [p.sb("sg", [128, D], BF16) for _ in range(2)]
    asum = p.sb("as", [128, 1], F32)
    asum2 = p.sb("as2", [128, 1], F32)
    alpha = p.sb("al", [128, 1], F32)
    wt_ps = [p.ps("wtp", [128, 128], BF16) for _ in range(2)]
    ps = p.ps("pj", [128, D], F32)
    xr = [p.sb("xr", [128, D], F32) for _ in range(2)]
    xo = [p.sb("xo", [128, D], F32) for _ in range(2)]
    ssum = p.sb("ssum", [128, 1], F32)
    ssq = p.sb("ssq", [128, 1], F32)
    mean = p.sb("mean", [128, 1], F32)
    var = p.sb("var", [128, 1], F32)
    sd = p.sb("sd", [128, 1], F32)
    rstd = p.sb("rstd", [128, 1], F32)

    s_c = p.sem("hc2", global_=True)
    s_cg = p.sem("hcg", global_=True)
    with nc.Block() as blk:
        @blk.sync
        def _(s):
            s.dma_start(ident_sb[:], ident).then_inc(s_c.h, 16)
            s.dma_start(ones1_sb[:], ones1).then_inc(s_c.h, 16)
            s.wait_ge(s_c.h, s_c.v + 32)
        s_c.v += 32

        @blk.gpsimd
        def _(g):
            g.dma_start(brow_sb[:], brow).then_inc(s_cg.h, 16)
            g.wait_ge(s_cg.h, s_cg.v + 16)
        s_cg.v += 16

    emit_load_xfm(p, x3, x_fm, ident_sb)

    s_al, s_sg, s_bw, s_tt, s_te, s_ab = (p.sem(x) for x in
        ("al", "sg", "bw", "tt", "te", "ab"))
    dwl = p.gdsem("hwl", 2)
    al0, sg0, bw0, tt0, te0, ab0 = (s.v for s in (s_al, s_sg, s_bw, s_tt, s_te, s_ab))
    with nc.Block() as blk:
        @blk.gpsimd
        def _(g):
            for oc in range(ND):
                if oc >= 2:
                    g.wait_ge(s_bw.h, bw0 + oc - 1)
                g.dma_start(wtmp[oc % 2][:], Wt[oc * 128:(oc + 1) * 128, :]).then_inc(dwl.inc(oc), 16)

        @blk.vector
        def _(v):
            for oc in range(ND):
                v.wait_ge(s_ab.h, ab0 + oc + 1)
                v.tensor_scalar_mul(alpha[:], asum2[:], INV_D).then_inc(s_al.h, 1)
                v.wait_ge(s_sg.h, sg0 + oc + 1)
                v.tensor_scalar_mul(sgn_sb[oc % 2][:], sgn_sb[oc % 2][:],
                                    alpha[:]).then_inc(s_bw.h, 1)
                for dc in range(ND):
                    j = oc * ND + dc
                    v.wait_ge(s_tt.h, tt0 + j + 1)
                    v.tensor_copy(bwT_sb[:, dc, oc * 128:(oc + 1) * 128],
                                  wt_ps[j % 2][:]).then_inc(s_te.h, 1)
        s_ab.v += ND
        s_al.v += ND
        s_bw.v += ND
        s_te.v += ND * ND

        @blk.scalar
        def _(sc):
            for oc in range(ND):
                sc.wait_ge(dwl.h(oc), dwl.tgt(oc))
                if oc >= 2:
                    sc.wait_ge(s_tt.h, tt0 + (oc - 1) * ND)
                sc.activation(sgn_sb[oc % 2][:], wtmp[oc % 2][:], AF.Abs,
                              accum_out=asum[:])
                sc.copy(asum2[:], asum[:]).then_inc(s_ab.h, 1)
                sc.sign(sgn_sb[oc % 2][:], wtmp[oc % 2][:]).then_inc(s_sg.h, 1)
        s_sg.v += ND

        @blk.tensor
        def _(t):
            for oc in range(ND):
                t.wait_ge(s_bw.h, bw0 + oc + 1)
                for dc in range(ND):
                    j = oc * ND + dc
                    if j >= 2:
                        t.wait_ge(s_te.h, te0 + j - 1)
                    t.transpose(wt_ps[j % 2][:], sgn_sb[oc % 2][:, dc * 128:(dc + 1) * 128],
                                ident_sb[:]).then_inc(s_tt.h, 1)
        s_tt.v += ND * ND

    _emit_proj_res_ln(p, x_fm, bwT_sb, (ones1_sb, brow_sb), x3, OUT, ps,
                      xr, xo, ssum, ssq, mean, var, sd, rstd)
    p.pop()
    return



NCORES = 8

INTERNAL_NAMES = {"Q_fm", "K_fm", "V_tm", "O_fm", "x1", "WupT", "WdnT", "x2",
                  "Q1_fm", "K1_fm", "V1_tm", "O1_fm", "x3"}


def build_fused(n_phases=9, out_name=None):
    p = Prog()
    p.internal = set(INTERNAL_NAMES)
    if out_name:
        p.internal.discard(out_name)
    phases = [
        (phase_qkv0, {}),
        (phase_attn, {"causal": False}),
        (phase_proj_ln, {"layer": 0, "bname": "bo_row",
                         "xres_name": "x_in", "xout_name": "x1"}),
        (phase_ffn_prep, {}),
        (phase_ffn_main, {}),
        (phase_qkv1, {}),
        (phase_attn, {"causal": True, "qn": "Q1_fm", "kn": "K1_fm",
                      "vn": "V1_tm", "on": "O1_fm"}),
        (phase_proj_ln, {"layer": 1, "on": "O1_fm", "wname": "L1_attn_W_o",
                         "bname": None, "xres_name": "x2", "xout_name": "x3",
                         "w_transposed": True}),
        (phase_bffn, {}),
    ]
    for fn, kw in phases[:n_phases]:
        p.push()
        p.phase_sems = []
        fn(p, **kw)
        used = list(p.phase_sems)
        p.pop()
        with p.nc.Block() as blk:
            @blk.gpsimd
            def _(g):
                for h in used:
                    g.sem_clear(h)
    p.close()
    return p


def host_maps(inputs, cores):
    consts = host_consts()
    x_full = np.ascontiguousarray(inputs["L0_attn_x"])
    shared = {**consts,
              "L0_attn_Wq": inputs["L0_attn_Wq"], "L0_attn_Wk": inputs["L0_attn_Wk"],
              "L0_attn_Wv": inputs["L0_attn_Wv"], "L0_attn_Wo": inputs["L0_attn_Wo"],
              "bq_cols": np.ascontiguousarray(inputs["L0_attn_bq"].reshape(ND, 128).T),
              "bk_cols": np.ascontiguousarray(inputs["L0_attn_bk"].reshape(ND, 128).T),
              "bv_row": np.ascontiguousarray(inputs["L0_attn_bv"].reshape(1, D)),
              "bo_row": np.ascontiguousarray(inputs["L0_attn_bo"].reshape(1, D)),
              "L0_ffn_w_up_q": inputs["L0_ffn_w_up_q"],
              "L0_ffn_w_up_scale": inputs["L0_ffn_w_up_scale"],
              "L0_ffn_w_down_q": inputs["L0_ffn_w_down_q"],
              "L0_ffn_w_down_scale": inputs["L0_ffn_w_down_scale"],
              "bup_cols": np.ascontiguousarray(inputs["L0_ffn_b_up"].reshape(NHC, 128).T),
              "bdnl_ext": np.ascontiguousarray(np.concatenate(
                  [inputs["L0_ffn_w_down_lora_b"],
                   inputs["L0_ffn_b_down"].reshape(1, D)], axis=0)),
              "L0_ffn_w_up_lora_a": inputs["L0_ffn_w_up_lora_a"],
              "L0_ffn_w_up_lora_b": inputs["L0_ffn_w_up_lora_b"],
              "L0_ffn_w_down_lora_a": inputs["L0_ffn_w_down_lora_a"],
              "L1_attn_W_qkv": inputs["L1_attn_W_qkv"],
              "L1_attn_W_o": inputs["L1_attn_W_o"],
              "L1_ffn_weight": inputs["L1_ffn_weight"],
              "bffn_row": np.ascontiguousarray(inputs["L1_ffn_bias"].reshape(1, D))}
    maps = []
    for b in cores:
        m = dict(shared)
        m["x_in"] = np.ascontiguousarray(x_full[b])
        maps.append(m)
    return maps


def kernel(**inputs):
    cores = list(range(NCORES))
    p = build_fused()
    maps = host_maps(inputs, cores)
    res = run_bass_kernel_spmd(p.nc, maps, core_ids=cores)
    return np.stack([res.results[b]["x4"] for b in cores], axis=0)


def _launch(phase_fn, kwargs, per_core_maps, core_ids):
    p = Prog()
    phase_fn(p, **kwargs)
    p.close()
    res = run_bass_kernel_spmd(p.nc, per_core_maps, core_ids=core_ids)
    return res.results


def kernel_multilaunch(**inputs):
    consts = host_consts()
    cores = list(range(NCORES))
    x_full = np.ascontiguousarray(inputs["L0_attn_x"])

    def rep(d):
        return [dict(d) for _ in cores]

    def percore(results, key):
        return [r[key] for r in results]

    bq = inputs["L0_attn_bq"]
    bk = inputs["L0_attn_bk"]
    bv = inputs["L0_attn_bv"]
    bo = inputs["L0_attn_bo"]

    maps = rep({**consts,
                "L0_attn_Wq": inputs["L0_attn_Wq"], "L0_attn_Wk": inputs["L0_attn_Wk"],
                "L0_attn_Wv": inputs["L0_attn_Wv"],
                "bq_cols": np.ascontiguousarray(bq.reshape(ND, 128).T),
                "bk_cols": np.ascontiguousarray(bk.reshape(ND, 128).T),
                "bv_row": np.ascontiguousarray(bv.reshape(1, D))})
    for b in cores:
        maps[b]["x_in"] = np.ascontiguousarray(x_full[b])
    rA = _launch(phase_qkv0, {}, maps, cores)

    maps = rep(consts)
    for b in cores:
        maps[b].update(Q_fm=rA[b]["Q_fm"], K_fm=rA[b]["K_fm"], V_tm=rA[b]["V_tm"])
    rB = _launch(phase_attn, {"causal": False}, maps, cores)

    maps = rep({**consts, "L0_attn_Wo": inputs["L0_attn_Wo"],
                "bo_row": np.ascontiguousarray(bo.reshape(1, D))})
    for b in cores:
        maps[b].update(O_fm=rB[b]["O_fm"], x_in=np.ascontiguousarray(x_full[b]))
    rC = _launch(phase_proj_ln, {"layer": 0, "bname": "bo_row",
                                 "xres_name": "x_in", "xout_name": "x1"}, maps, cores)

    maps = rep({**consts,
                "L0_ffn_w_up_q": inputs["L0_ffn_w_up_q"],
                "L0_ffn_w_up_scale": inputs["L0_ffn_w_up_scale"],
                "L0_ffn_w_down_q": inputs["L0_ffn_w_down_q"],
                "L0_ffn_w_down_scale": inputs["L0_ffn_w_down_scale"]})
    rD1 = _launch(phase_ffn_prep, {}, maps, cores)

    maps = rep({**consts,
                "bup_cols": np.ascontiguousarray(inputs["L0_ffn_b_up"].reshape(NHC, 128).T),
                "bdnl_ext": np.ascontiguousarray(np.concatenate(
                    [inputs["L0_ffn_w_down_lora_b"],
                     inputs["L0_ffn_b_down"].reshape(1, D)], axis=0)),
                "L0_ffn_w_up_lora_a": inputs["L0_ffn_w_up_lora_a"],
                "L0_ffn_w_up_lora_b": inputs["L0_ffn_w_up_lora_b"],
                "L0_ffn_w_down_lora_a": inputs["L0_ffn_w_down_lora_a"]})
    for b in cores:
        maps[b].update(x1=rC[b]["x1"], WupT=rD1[b]["WupT"], WdnT=rD1[b]["WdnT"])
    rD2 = _launch(phase_ffn_main, {}, maps, cores)

    maps = rep({**consts, "L1_attn_W_qkv": inputs["L1_attn_W_qkv"]})
    for b in cores:
        maps[b]["x2"] = rD2[b]["x2"]
    rE = _launch(phase_qkv1, {}, maps, cores)

    maps = rep(consts)
    for b in cores:
        maps[b].update(Q1_fm=rE[b]["Q1_fm"], K1_fm=rE[b]["K1_fm"], V1_tm=rE[b]["V1_tm"])
    rF = _launch(phase_attn, {"causal": True, "qn": "Q1_fm", "kn": "K1_fm",
                              "vn": "V1_tm", "on": "O1_fm"}, maps, cores)

    maps = rep({**consts, "L1_attn_W_o": inputs["L1_attn_W_o"]})
    for b in cores:
        maps[b].update(O1_fm=rF[b]["O1_fm"], x2=rD2[b]["x2"])
    rG = _launch(phase_proj_ln, {"layer": 1, "on": "O1_fm", "wname": "L1_attn_W_o",
                                 "bname": None, "xres_name": "x2", "xout_name": "x3",
                                 "w_transposed": True}, maps, cores)

    maps = rep({**consts, "L1_ffn_weight": inputs["L1_ffn_weight"],
                "bffn_row": np.ascontiguousarray(inputs["L1_ffn_bias"].reshape(1, D))})
    for b in cores:
        maps[b]["x3"] = rG[b]["x3"]
    rH = _launch(phase_bffn, {}, maps, cores)

    out = np.stack([rH[b]["x4"] for b in cores], axis=0)
    return out
